# version 01
# baseline (speedup 1.0000x reference)
import numpy as np
from contextlib import ExitStack

import concourse.bass as bass
import concourse.mybir as mybir
from concourse.bass_utils import run_bass_kernel_spmd

F32 = mybir.dt.float32
BF16 = mybir.dt.bfloat16
I32 = mybir.dt.int32
U32 = mybir.dt.uint32
U8 = mybir.dt.uint8
AF = mybir.ActivationFunctionType
ALU = mybir.AluOpType
AX = mybir.AxisListType

T = 4096
D = 1024
NEG = -1e30
NT = 512
NCH = T // NT

D_CKV, D_CQ = 256, 512
N_HEAD, D_HEAD, D_ROPE = 8, 64, 32
N_IDX_HEAD = 4
K_TS, LW = 96, 64
N_DYN = K_TS - LW
ATT_SCALE = (D_HEAD + D_ROPE) ** -0.5
RHALF = D_ROPE // 2
THETA = (1.0 / (10000.0 ** (2.0 * np.arange(RHALF) / D_ROPE)))

ENGINES = ("sync", "vector", "scalar", "tensor", "gpsimd")


class Sem:
    def __init__(self, h):
        self.h = h
        self.n = 0


class B:

    def __init__(self, nc, ctx, block):
        self.nc = nc
        self.ctx = ctx
        self.block = block
        self.dram = {}
        self._sems = {}
        self._bar = 0

    def sem(self, name) -> Sem:
        if name not in self._sems:
            self._sems[name] = Sem(self.ctx.enter_context(self.nc.semaphore(name)))
        return self._sems[name]

    def sb(self, name, shape, dt=F32):
        return self.ctx.enter_context(self.nc.sbuf_tensor(name, shape, dt))

    def psum(self, name, shape, dt=F32):
        return self.ctx.enter_context(self.nc.psum_tensor(name, shape, dt))

    def barrier(self):
        self._bar += 1
        s = self.sem(f"bar{self._bar}")
        for e in ENGINES:
            @getattr(self.block, e)
            def _(eng, s=s):
                eng.nop().then_inc(s.h, 1)
        s.n += 5
        tgt = s.n
        for e in ENGINES:
            @getattr(self.block, e)
            def _(eng, s=s, tgt=tgt):
                eng.wait_ge(s.h, tgt)




class LNAffine:

    def __init__(self, b: B, tag, ctx, gvec, bvec, eps):
        nc = b.nc
        self.b, self.tag, self.eps = b, tag, eps
        self.gvec, self.bvec = gvec, bvec
        sb = lambda n, s, d=F32: ctx.enter_context(nc.sbuf_tensor(tag + n, s, d))
        self.y_bf = ctx.enter_context(nc.sbuf_tensor(tag + "ybf", [128, 8 * NT], BF16))
        self.tn = ctx.enter_context(nc.sbuf_tensor(tag + "tn", [128, 8 * NT], F32))
        self.ysq = ctx.enter_context(nc.sbuf_tensor(tag + "ysq", [128, 8 * NT], BF16))
        self.ones1 = ctx.enter_context(nc.sbuf_tensor(tag + "on1", [128, 1], BF16))
        self.onesc = sb("onc", [1, 128])
        self.mu_r = sb("mu", [1, NT])
        self.var_r = sb("var", [1, NT])
        self.musq = sb("musq", [1, NT])
        self.rstd_r = sb("rstd", [1, NT])
        self.ps_sx = ctx.enter_context(nc.psum_tensor(tag + "psx", [1, NT], F32))
        self.ps_sq = ctx.enter_context(nc.psum_tensor(tag + "psq", [1, NT], F32))
        self.ps_mu = ctx.enter_context(nc.psum_tensor(tag + "pmu", [128, NT], F32))
        self.ps_rs = ctx.enter_context(nc.psum_tensor(tag + "prs", [128, NT], F32))
        self.cvy = b.sem(tag + "cvy")
        self.smm = b.sem(tag + "smm")
        self.sve = b.sem(tag + "sve")
        self.sae = b.sem(tag + "sae")
        self.bcy = b.sem(tag + "bcy")
        self.nrm = b.sem(tag + "nrm")
        self.k = 0

        @b.block.vector
        def _(v):
            v.memset(self.ones1[:], 1.0)
            v.memset(self.onesc[:], 1.0)

    def emit_chunk(self, y_ready: Sem, y_ready_n: int, y_sb, out_sb,
                   out_free: Sem = None, out_free_n: int = 0):
        b, tag = self.b, self.tag
        k = self.k
        self.k += 1
        cvy, smm, sve, sae, bcy, nrm = (self.cvy, self.smm, self.sve, self.sae,
                                        self.bcy, self.nrm)
        y_bf, ysq, mu_r, var_r, musq, rstd_r = (self.y_bf, self.ysq, self.mu_r,
                                                self.var_r, self.musq, self.rstd_r)
        ps_sx, ps_sq, ps_mu, ps_rs = self.ps_sx, self.ps_sq, self.ps_mu, self.ps_rs
        gvec, bvec, eps = self.gvec, self.bvec, self.eps

        @b.block.vector
        def _(v, k=k, yn=y_ready_n):
            v.wait_ge(y_ready.h, yn)
            if k > 0:
                v.wait_ge(smm.h, k)
            v.tensor_copy(y_bf[:], y_sb[:]).then_inc(cvy.h, 1)
        cvy.n += 1

        @b.block.scalar
        def _(s, k=k, yn=y_ready_n):
            s.wait_ge(y_ready.h, yn)
            if k > 0:
                s.wait_ge(smm.h, k)
            s.activation(out=ysq[:], in_=y_sb[:], func=AF.Square).then_inc(cvy.h, 1)
        cvy.n += 1

        @b.block.tensor
        def _(t, k=k, cvn=cvy.n, nrn=nrm.n):
            t.wait_ge(cvy.h, cvn)
            if k > 0:
                t.wait_ge(nrm.h, nrn)
            for a in range(8):
                t.matmul(ps_sx[:], self.ones1[:], y_bf[:, a * NT:(a + 1) * NT],
                         start=(a == 0), stop=(a == 7))
            for a in range(8):
                i = t.matmul(ps_sq[:], self.ones1[:], ysq[:, a * NT:(a + 1) * NT],
                             start=(a == 0), stop=(a == 7))
            i.then_inc(smm.h, 1)
        smm.n += 1

        @b.block.vector
        def _(v, k=k, svb=sve.n):
            v.wait_ge(smm.h, k + 1)
            v.tensor_scalar(out=mu_r[:], in0=ps_sx[:], scalar1=1.0 / D,
                            scalar2=None, op0=ALU.mult)
            v.tensor_scalar(out=var_r[:], in0=ps_sq[:], scalar1=1.0 / D,
                            scalar2=float(eps), op0=ALU.mult, op1=ALU.add)
            v.tensor_tensor(out=musq[:], in0=mu_r[:], in1=mu_r[:],
                            op=ALU.mult).then_inc(sve.h, 1)
            v.wait_ge(sve.h, svb + 1)
            v.tensor_tensor(out=var_r[:], in0=var_r[:], in1=musq[:],
                            op=ALU.subtract).then_inc(sve.h, 1)
        sve.n += 2

        @b.block.scalar
        def _(s, k=k, svn=sve.n, sab=sae.n):
            s.wait_ge(sve.h, svn)
            s.activation(out=rstd_r[:], in_=var_r[:], func=AF.Ln).then_inc(sae.h, 1)
            s.wait_ge(sae.h, sab + 1)
            s.activation(out=rstd_r[:], in_=rstd_r[:], func=AF.Exp, scale=-0.5) \
                .then_inc(sae.h, 2)
        sae.n += 3

        @b.block.tensor
        def _(t, k=k, san=sae.n):
            t.wait_ge(sae.h, san)
            t.matmul(ps_mu[:], self.onesc[:], mu_r[:], start=True, stop=True)
            t.matmul(ps_rs[:], self.onesc[:], rstd_r[:], start=True, stop=True) \
                .then_inc(bcy.h, 1)
        bcy.n += 1

        @b.block.vector
        def _(v, k=k, bn=bcy.n, ofn=out_free_n):
            v.wait_ge(bcy.h, bn)
            if out_free is not None:
                v.wait_ge(out_free.h, ofn)
            for m in range(8):
                sl = slice(m * NT, (m + 1) * NT)
                v.tensor_tensor(out=self.tn[:, sl], in0=y_sb[:, sl], in1=ps_mu[:],
                                op=ALU.subtract)
            for m in range(8):
                sl = slice(m * NT, (m + 1) * NT)
                v.tensor_tensor(out=out_sb[:, sl], in0=self.tn[:, sl], in1=ps_rs[:],
                                op=ALU.mult)
            for m in range(8):
                sl = slice(m * NT, (m + 1) * NT)
                i = v.tensor_scalar(out=out_sb[:, sl], in0=out_sb[:, sl],
                                    scalar1=gvec[:, m:m + 1],
                                    scalar2=bvec[:, m:m + 1],
                                    op0=ALU.mult, op1=ALU.add)
            i.then_inc(nrm.h, 1)
        nrm.n += 1
        return nrm




def emit_ln(b: B, tag, src, dst, mu_out, rstd_out):
    nc = b.nc
    with ExitStack() as ctx:
        def sb(n, s, d=F32):
            return ctx.enter_context(nc.sbuf_tensor(tag + n, s, d))
        x = sb("x", [128, 8 * NT])
        x_bf = sb("xbf", [128, 8 * NT], BF16)
        xsq = sb("xsq", [128, 8 * NT], BF16)
        tnorm = sb("tn", [128, 8 * NT])
        out_t = sb("out", [128, 8 * NT], BF16)
        ones = sb("ones", [128, 1], BF16)
        onesc = sb("onesc", [1, 128])
        mu_r = sb("mu", [1, NT])
        var_r = sb("var", [1, NT])
        musq_r = sb("musq", [1, NT])
        rstd_r = sb("rstd", [1, NT])
        ps_sx = ctx.enter_context(nc.psum_tensor(tag + "psx", [1, NT], F32))
        ps_sq = ctx.enter_context(nc.psum_tensor(tag + "psq", [1, NT], F32))
        ps_mu = ctx.enter_context(nc.psum_tensor(tag + "pmu", [128, NT], F32))
        ps_rs = ctx.enter_context(nc.psum_tensor(tag + "prs", [128, NT], F32))
        ld = b.sem(tag + "ld")
        cv = b.sem(tag + "cv")
        mm = b.sem(tag + "mm")
        sv = b.sem(tag + "sv")
        sa = b.sem(tag + "sa")
        bc = b.sem(tag + "bc")
        nr = b.sem(tag + "nr")
        st = b.sem(tag + "st")

        src_c = src.rearrange("(a p) (c n) -> c a p n", p=128, n=NT)
        dst_c = dst.rearrange("(a p) (c n) -> c a p n", p=128, n=NT)
        mu_c = mu_out.rearrange("o (c n) -> c o n", n=NT)
        rstd_c = rstd_out.rearrange("o (c n) -> c o n", n=NT)

        for c in range(NCH):
            @b.block.sync
            def _(sync, c=c, nrn=nr.n):
                if c > 0:
                    sync.wait_ge(nr.h, nrn)
                for a in range(8):
                    sync.dma_start(out=x[:, a * NT:(a + 1) * NT], in_=src_c[c, a]) \
                        .then_inc(ld.h, 16)
            ld.n += 128

            @b.block.vector
            def _(v, c=c, ldn=ld.n):
                if c == 0:
                    v.memset(ones[:], 1.0)
                    v.memset(onesc[:], 1.0)
                v.wait_ge(ld.h, ldn)
                v.tensor_copy(x_bf[:], x[:]).then_inc(cv.h, 1)
            cv.n += 1

            @b.block.scalar
            def _(s, c=c, ldn=ld.n):
                s.wait_ge(ld.h, ldn)
                s.activation(out=xsq[:], in_=x[:], func=AF.Square).then_inc(cv.h, 1)
            cv.n += 1

            @b.block.tensor
            def _(t, c=c, cvn=cv.n, nrn=nr.n):
                t.wait_ge(cv.h, cvn)
                if c > 0:
                    t.wait_ge(nr.h, nrn)
                for a in range(8):
                    t.matmul(ps_sx[:], ones[:], x_bf[:, a * NT:(a + 1) * NT],
                             start=(a == 0), stop=(a == 7))
                for a in range(8):
                    i = t.matmul(ps_sq[:], ones[:], xsq[:, a * NT:(a + 1) * NT],
                                 start=(a == 0), stop=(a == 7))
                i.then_inc(mm.h, 1)
            mm.n += 1

            @b.block.vector
            def _(v, c=c, mmn=mm.n, svb=sv.n):
                v.wait_ge(mm.h, mmn)
                v.tensor_scalar(out=mu_r[:], in0=ps_sx[:], scalar1=1.0 / D,
                                scalar2=None, op0=ALU.mult)
                v.tensor_scalar(out=var_r[:], in0=ps_sq[:], scalar1=1.0 / D,
                                scalar2=1e-5, op0=ALU.mult, op1=ALU.add)
                v.tensor_tensor(out=musq_r[:], in0=mu_r[:], in1=mu_r[:], op=ALU.mult) \
                    .then_inc(sv.h, 1)
                v.wait_ge(sv.h, svb + 1)
                v.tensor_tensor(out=var_r[:], in0=var_r[:], in1=musq_r[:],
                                op=ALU.subtract).then_inc(sv.h, 1)
            sv.n += 2

            @b.block.scalar
            def _(s, c=c, svn=sv.n, san=sa.n):
                s.wait_ge(sv.h, svn)
                s.activation(out=rstd_r[:], in_=var_r[:], func=AF.Ln) \
                    .then_inc(sa.h, 1)
                s.wait_ge(sa.h, san + 1)
                s.activation(out=rstd_r[:], in_=rstd_r[:], func=AF.Exp, scale=-0.5) \
                    .then_inc(sa.h, 2)
            sa.n += 3

            @b.block.tensor
            def _(t, c=c, san=sa.n):
                t.wait_ge(sa.h, san)
                t.matmul(ps_mu[:], onesc[:], mu_r[:], start=True, stop=True)
                i = t.matmul(ps_rs[:], onesc[:], rstd_r[:], start=True, stop=True)
                i.then_inc(bc.h, 1)
            bc.n += 1

            @b.block.vector
            def _(v, c=c, bcn=bc.n, stn=st.n):
                v.wait_ge(bc.h, bcn)
                if c > 1:
                    v.wait_ge(st.h, stn - 160)
                for a in range(8):
                    sl = slice(a * NT, (a + 1) * NT)
                    v.tensor_tensor(out=tnorm[:, sl], in0=x[:, sl], in1=ps_mu[:],
                                    op=ALU.subtract)
                for a in range(8):
                    i = v.tensor_tensor(out=out_t[:, a * NT:(a + 1) * NT],
                                        in0=tnorm[:, a * NT:(a + 1) * NT],
                                        in1=ps_rs[:], op=ALU.mult)
                i.then_inc(nr.h, 1)
            nr.n += 1

            @b.block.sync
            def _(sync, c=c, nrn=nr.n):
                sync.wait_ge(nr.h, nrn)
                for a in range(8):
                    sync.dma_start(out=dst_c[c, a], in_=out_t[:, a * NT:(a + 1) * NT]) \
                        .then_inc(st.h, 16)
                sync.dma_start(out=mu_c[c], in_=mu_r[:]).then_inc(st.h, 16)
                sync.dma_start(out=rstd_c[c], in_=rstd_r[:]).then_inc(st.h, 16)
            st.n += 160

        @b.block.sync
        def _(sync, stn=st.n):
            sync.wait_ge(st.h, stn)



BOUND = {
    "xT": ([D, T], F32),
    "xn0T": ([D, T], BF16),
    "mu0": ([1, T], F32),
    "rstd0": ([1, T], F32),
    "g0T": ([D, T], F32),
    "kcat": ([T + 16, 320], BF16),
    "ckvT": ([D_CKV, T], BF16),
    "kiT": ([D_HEAD, T], BF16),
    "gkb1T": ([RHALF, T], BF16),
    "gkb2T": ([RHALF, T], BF16),
    "gka1T": ([RHALF, 64], BF16),
    "gka2T": ([RHALF, 64], BF16),
    "wcatT": ([N_HEAD * D_CKV, D], BF16),
    "sT": ([D, T], F32),
    "x1T": ([D, T], F32),
    "mu1": ([1, T], F32),
    "rstd1": ([1, T], F32),
    "xn1T": ([D, T], BF16),
    "s1T": ([D, T], F32),
    "outT": ([D, T], F32),
}

PHASE_SCRATCH = {
    "l0att": {"idx_d": ([128, 32], U32), "dbg_sd": ([8 * 128, 512], F32),
              "dbg_sb": ([8 * 128, 80], F32), "dbg_idx": ([128, 32], U32)},
    "l0ffn": {"interG": ([4 * D, T], BF16),
              "dbg_oh": ([T, 8], F32), "dbg_log": ([T, 8], F32)},
    "l1ffn": {"l1hg": ([64 * 128, T], BF16)},
    "l1att": {"l1h": ([4 * D, T], BF16), "l1xm": ([D, T], BF16),
              "dbg_attn": ([D, T], BF16), "dbg_kv": ([128, 16 * 2 * 65], BF16),
              "dbg_kf": ([128, 32 * 256], BF16), "dbg_vh": ([128, 32 * 64], BF16),
              "dbg_kt": ([64, T], BF16)},
}

PHASES = {
    "ln0": (["xT"], ["xn0T", "mu0", "rstd0"]),
    "l0ffn": (["xT", "xn0T", "mu0", "rstd0"], ["g0T"]),
    "prep": (["xn0T"], ["kcat", "ckvT", "kiT", "gkb1T", "gkb2T", "gka1T",
                         "gka2T", "wcatT"]),
    "l0att": (["xT", "xn0T", "g0T", "kcat", "ckvT", "kiT", "gkb1T", "gkb2T",
               "gka1T", "gka2T", "wcatT"], ["x1T"]),
    "ln1": (["x1T"], ["xn1T", "mu1", "rstd1"]),
    "l1att": (["xn1T"], ["s1T"]),
    "l1ffn": (["x1T", "xn1T", "mu1", "rstd1", "s1T"], ["outT"]),
}

PHASE_WEIGHTS = {
    "ln0": [],
    "l0ffn": ["wrT", "w1", "b1r", "wo", "bor", "gamma0r", "beta0r"],
    "prep": ["w_dkv", "w_kr", "idx_wk", "w_uvTh", "w_outm", "c63", "s63", "c0", "s0"],
    "l0att": ["idx_wq", "w_dq", "w_uq", "w_qr", "w_ukTh", "identbf", "maskdyn",
              "maskband", "crot", "srot", "nfin0", "iotaj", "fill64", "idxwb",
              "ct_tm", "st_tm", "c2t_tm", "s2t_tm"],
    "ln1": [],
    "l1att": ["wqkvT", "proj", "wout1T", "b_outr", "gamma1r", "beta1r", "w1aT",
              "b1ar", "w2aT", "b2ar"],
    "l1ffn": ["eselT", "keys", "values"],
}


def emit_phase(b: B, name):
    if name == "ln0":
        emit_ln(b, "ln0", b.dram["xT"], b.dram["xn0T"], b.dram["mu0"], b.dram["rstd0"])
    elif name == "ln1":
        emit_ln(b, "ln1", b.dram["x1T"], b.dram["xn1T"], b.dram["mu1"], b.dram["rstd1"])
    elif name == "l0ffn":
        emit_l0ffn(b)
    elif name == "prep":
        emit_prep(b)
    elif name == "l0att":
        emit_l0att(b)
    elif name == "l1att":
        emit_l1att(b)
    elif name == "l1ffn":
        emit_l1ffn(b)
    else:
        raise KeyError(name)


def emit_identity(b: B, g, tile, ones_tile):
    g.affine_select(out=tile[:], in_=ones_tile[:], pattern=[[1, tile.shape[1]]],
                    compare_op=ALU.is_equal, fill=0.0, base=0, channel_multiplier=-1)


def emit_l0ffn(b: B):
    nc = b.nc
    xT, xn0T = b.dram["xT"], b.dram["xn0T"]
    mu0, rstd0 = b.dram["mu0"], b.dram["rstd0"]
    g0T = b.dram["g0T"]
    wrT, w1, b1r = b.dram["wrT"], b.dram["w1"], b.dram["b1r"]
    wo, bor = b.dram["wo"], b.dram["bor"]
    gamma0r, beta0r = b.dram["gamma0r"], b.dram["beta0r"]
    interG_d = b.dram["interG"]

    xT_c = xT.rearrange("(a p) (c n) -> c a p n", p=128, n=NT)
    xn_c = xn0T.rearrange("(a p) (c n) -> c a p n", p=128, n=NT)
    g0_c = g0T.rearrange("(m p) (c n) -> c m p n", p=128, n=NT)
    ig_d = interG_d.rearrange("(i p) (c n) -> i c p n", p=128, n=NT)
    w1_t = w1
    wo_t = wo.rearrange("(i p) (m q) -> i m p q", p=128, q=128)

    with ExitStack() as ctx:
        def sb(n, s, d=F32):
            return ctx.enter_context(nc.sbuf_tensor("fA" + n, s, d))
        xf = sb("xf", [128, 8 * NT])
        xnf = sb("xnf", [128, 8 * NT])
        wr_sb = sb("wr", [128, 8 * 8])
        murow = sb("mur", [1, NT])
        rsrow = sb("rsr", [1, NT])
        onesc = sb("onesc", [1, 128])
        ones128 = sb("ones128", [128, 128])
        identf = sb("identf", [128, 128])
        logT = sb("logT", [8, T])
        log_tm = sb("logtm", [128, 8])
        m8s = sb("m8", [128, 8])
        mval8 = sb("mval8", [128, 8])
        knock = sb("knock", [128, 8])
        oh_tm = sb("ohtm", [128, 8])
        ohT_bf = ctx.enter_context(nc.sbuf_tensor("fAohTbf", [8, T], BF16))
        sel8 = sb("sel8", [8, 8 * 128], BF16)
        masks = ctx.enter_context(nc.sbuf_tensor("fAmasks", [128, 8 * T], U8))
        ps_log = ctx.enter_context(nc.psum_tensor("fAplog", [8, NT], F32))
        ps_b = ctx.enter_context(nc.psum_tensor("fApb", [128, NT], F32))
        ps_tr = ctx.enter_context(nc.psum_tensor("fAptr", [128, 8], F32))
        ps_tr2 = ctx.enter_context(nc.psum_tensor("fAptr2", [8, 128], F32))
        ps_m = ctx.enter_context(nc.psum_tensor("fApm", [128, NT], F32))
        ld = b.sem("fAld")
        pre = b.sem("fApre")
        bcn = b.sem("fAbc")
        xnd = b.sem("fAxnd")
        lgm = b.sem("fAlgm")
        lge = b.sem("fAlge")
        trd = b.sem("fAtrd")
        ohd = b.sem("fAohd")
        tr2 = b.sem("fAtr2")
        ohe = b.sem("fAohe")
        mbc = b.sem("fAmbc")
        mev = b.sem("fAmev")
        dbg = b.sem("fAdbg")

        @b.block.gpsimd
        def _(g):
            g.memset(ones128[:], 1.0)
            emit_identity(b, g, identf, ones128)
            g.tensor_copy(onesc[:], ones128[0:1, 0:128])
            g.memset(sel8[:], 1.0)
            g.affine_select(out=sel8[:], in_=sel8[:], pattern=[[1, 8], [0, 128]],
                            compare_op=ALU.is_equal, fill=0.0, base=0,
                            channel_multiplier=-1)
            g.engine_nop().then_inc(pre.h, 1)
        pre.n += 1

        @b.block.sync
        def _(sync):
            for a in range(8):
                sync.dma_start(out=wr_sb[:, a * 8:(a + 1) * 8],
                               in_=wrT.rearrange("(a p) e -> a p e", p=128)[a]) \
                    .then_inc(ld.h, 16)
        ld.n += 128

        for c in range(NCH):
            @b.block.sync
            def _(sync, c=c, xnn=xnd.n):
                if c > 0:
                    sync.wait_ge(xnd.h, xnn)
                for a in range(8):
                    sync.dma_start(out=xf[:, a * NT:(a + 1) * NT], in_=xT_c[c, a]) \
                        .then_inc(ld.h, 16)
                sync.dma_start(out=murow[:], in_=mu0[:, c * NT:(c + 1) * NT]) \
                    .then_inc(ld.h, 16)
                sync.dma_start(out=rsrow[:], in_=rstd0[:, c * NT:(c + 1) * NT]) \
                    .then_inc(ld.h, 16)
            ld.n += 160

            @b.block.tensor
            def _(t, c=c, ldn=ld.n, xnn=xnd.n):
                t.wait_ge(ld.h, ldn)
                t.wait_ge(pre.h, 1)
                if c > 0:
                    t.wait_ge(xnd.h, xnn)
                t.matmul(ps_b[:], onesc[:], murow[:], start=True, stop=True)
                t.matmul(ps_m[:], onesc[:], rsrow[:], start=True, stop=True) \
                    .then_inc(bcn.h, 1)
            bcn.n += 1

            @b.block.vector
            def _(v, c=c, bb=bcn.n, trn=trd.n):
                v.wait_ge(bcn.h, bb)
                if c > 0:
                    v.wait_ge(trd.h, trn)
                for a in range(8):
                    sl = slice(a * NT, (a + 1) * NT)
                    v.tensor_tensor(out=xnf[:, sl], in0=xf[:, sl], in1=ps_b[:],
                                    op=ALU.subtract)
                for a in range(8):
                    sl = slice(a * NT, (a + 1) * NT)
                    i = v.tensor_tensor(out=xnf[:, sl], in0=xnf[:, sl], in1=ps_m[:],
                                        op=ALU.mult)
                i.then_inc(xnd.h, 1)
            xnd.n += 1

            for q in range(4 * c, 4 * c + 4):
                qo = (q % 4) * 128

                @b.block.tensor
                def _(t, q=q, qo=qo, ohn=ohd.n, xnn=xnd.n):
                    t.wait_ge(xnd.h, xnn)
                    if q > 0:
                        t.wait_ge(ohd.h, ohn)
                    for a in range(8):
                        i = t.matmul(ps_tr[:, 0:8],
                                     xnf[:, a * NT + qo: a * NT + qo + 128],
                                     wr_sb[:, a * 8:(a + 1) * 8],
                                     start=(a == 0), stop=(a == 7))
                    i.then_inc(trd.h, 1)
                trd.n += 1

                @b.block.vector
                def _(v, q=q, tn=trd.n, ohb=ohd.n):
                    v.wait_ge(trd.h, tn)
                    v.tensor_copy(log_tm[:], ps_tr[:, 0:8])
                    v.memset(mval8[:], NEG).then_inc(ohd.h, 1)
                    v.wait_ge(ohd.h, ohb + 1)
                    v.max(out=m8s[:], in_=log_tm[:]).then_inc(ohd.h, 1)
                    v.wait_ge(ohd.h, ohb + 2)
                    v.tensor_copy(mval8[:, 0:1], m8s[:, 0:1]).then_inc(ohd.h, 1)
                    v.wait_ge(ohd.h, ohb + 3)
                    v.match_replace(out=knock[:], in_to_replace=mval8[:],
                                    in_values=log_tm[:], imm_value=NEG) \
                        .then_inc(ohd.h, 1)
                    v.wait_ge(ohd.h, ohb + 4)
                    i = v.tensor_tensor(out=oh_tm[:], in0=log_tm[:], in1=knock[:],
                                        op=ALU.not_equal)
                    i.then_inc(ohd.h, 1)
                ohd.n += 5

                @b.block.tensor
                def _(t, q=q, ohn=ohd.n, ohe_n=ohe.n):
                    t.wait_ge(ohd.h, ohn)
                    if q > 0:
                        t.wait_ge(ohe.h, ohe_n)
                    t.transpose(ps_tr2[:], oh_tm[:], identf[:]).then_inc(tr2.h, 1)
                tr2.n += 1

                @b.block.vector
                def _(v, q=q, tn=tr2.n):
                    v.wait_ge(tr2.h, tn)
                    v.tensor_copy(ohT_bf[:, q * 128:(q + 1) * 128], ps_tr2[0:8, :]) \
                        .then_inc(ohe.h, 1)
                ohe.n += 1

                if DEBUG_TENSORS:
                    @b.block.scalar
                    def _(s, q=q, ohn=ohd.n):
                        s.wait_ge(ohd.h, ohn)
                        s.dma_start(out=b.dram["dbg_oh"][q * 128:(q + 1) * 128, :],
                                    in_=oh_tm[:]).then_inc(dbg.h, 16)
                        s.dma_start(out=b.dram["dbg_log"][q * 128:(q + 1) * 128, :],
                                    in_=log_tm[:]).then_inc(dbg.h, 16)
                    dbg.n += 32

                    @b.block.vector
                    def _(v, dbn=dbg.n):
                        v.wait_ge(dbg.h, dbn)


        for e in range(8):
            for c in range(NCH):
                r = e * NCH + c

                @b.block.tensor
                def _(t, e=e, c=c, r=r):
                    if r == 0:
                        t.wait_ge(ohe.h, T // 128)
                    if r > 0:
                        t.wait_ge(mev.h, r)
                    t.matmul(ps_m[:], sel8[:, e * 128:(e + 1) * 128],
                             ohT_bf[:, c * NT:(c + 1) * NT],
                             start=True, stop=True).then_inc(mbc.h, 1)
                mbc.n += 1

                @b.block.vector
                def _(v, e=e, c=c, mn=mbc.n):
                    v.wait_ge(mbc.h, mn)
                    v.tensor_copy(masks[:, e * T + c * NT: e * T + (c + 1) * NT],
                                  ps_m[:]).then_inc(mev.h, 1)
                mev.n += 1

        with ExitStack() as ctxB:
            def sbB(n, s, d=BF16):
                return ctxB.enter_context(nc.sbuf_tensor("fB" + n, s, d))
            xn_sb = sbB("xn", [128, 8 * T])
            w1t = [sbB(f"w1_{j}", [128, 8 * 128]) for j in range(2)]
            b1_sb = sbB("b1", [128, 8 * 32], F32)
            scr = sbB("scr", [128, NT])
            accs = [sbB(f"acc{j}", [128, T]) for j in range(2)]
            ps_i = [ctxB.enter_context(nc.psum_tensor(f"fBpi{j}", [128, NT], F32))
                    for j in range(2)]
            ldx = b.sem("fBldx")
            ldw = b.sem("fBldw")
            ldb = b.sem("fBldb")
            mmi = b.sem("fBmmi")
            gel = b.sem("fBgel")
            prc = b.sem("fBprc")
            sti = b.sem("fBsti")

            @b.block.sync
            def _(sync):
                for a in range(8):
                    for c in range(NCH):
                        sync.dma_start(
                            out=xn_sb[:, a * T + c * NT: a * T + (c + 1) * NT],
                            in_=xn_c[c, a]).then_inc(ldx.h, 16)
            ldx.n += 16 * 64

            @b.block.scalar
            def _(s):
                for e in range(8):
                    s.dma_start(out=b1_sb[:, e * 32:(e + 1) * 32], in_=b1r[e]) \
                        .then_inc(ldb.h, 16)
            ldb.n += 16 * 8

            NEI = 32 * 8

            @b.block.gpsimd
            def _(g):
                g.dma_start(out=w1t[0][:], in_=w1_t[0, 0]).then_inc(ldw.h, 16)
            ldw.n += 16

            for j in range(NEI):
                i_, e_ = j // 8, j % 8

                @b.block.gpsimd
                def _(g, j=j):
                    if j + 1 < NEI:
                        if j >= 1:
                            g.wait_ge(prc.h, 8 * j)
                        i2, e2 = (j + 1) // 8, (j + 1) % 8
                        g.dma_start(out=w1t[(j + 1) % 2][:], in_=w1_t[e2, i2]) \
                            .then_inc(ldw.h, 16)
                if j + 1 < NEI:
                    ldw.n += 16

                for n in range(NCH):
                    k = 8 * j + n

                    @b.block.tensor
                    def _(t, j=j, n=n, k=k, ldxn=ldx.n):
                        if k == 0:
                            t.wait_ge(ldx.h, ldxn)
                        t.wait_ge(ldw.h, 16 * (j + 1))
                        if k >= 2:
                            t.wait_ge(gel.h, k - 1)
                        for a in range(8):
                            i = t.matmul(
                                ps_i[k % 2][:], w1t[j % 2][:, a * 128:(a + 1) * 128],
                                xn_sb[:, a * T + n * NT: a * T + (n + 1) * NT],
                                start=(a == 0), stop=(a == 7))
                        i.then_inc(mmi.h, 1)
                    mmi.n += 1

                    @b.block.scalar
                    def _(s, j=j, n=n, k=k, i_=i_, e_=e_, ldbn=ldb.n):
                        s.wait_ge(mmi.h, k + 1)
                        if k == 0:
                            s.wait_ge(ldb.h, ldbn)
                        if k >= 1:
                            s.wait_ge(prc.h, k)
                        s.activation(out=scr[:], in_=ps_i[k % 2][:], func=AF.Gelu,
                                     bias=b1_sb[:, e_ * 32 + i_: e_ * 32 + i_ + 1]) \
                            .then_inc(gel.h, 1)
                    gel.n += 1

                    @b.block.vector
                    def _(v, j=j, n=n, k=k, i_=i_, e_=e_, stn=sti.n):
                        v.wait_ge(gel.h, k + 1)
                        if e_ == 0 and n == 0 and i_ >= 2:
                            v.wait_ge(sti.h, stn - 128)
                        v.copy_predicated(
                            out=accs[i_ % 2][:, n * NT:(n + 1) * NT],
                            mask=masks[:, e_ * T + n * NT: e_ * T + (n + 1) * NT],
                            data=scr[:]).then_inc(prc.h, 1)
                    prc.n += 1

                if e_ == 7:
                    @b.block.sync
                    def _(sync, i_=i_, prn=prc.n):
                        sync.wait_ge(prc.h, prn)
                        for c in range(NCH):
                            sync.dma_start(out=ig_d[i_, c],
                                           in_=accs[i_ % 2][:, c * NT:(c + 1) * NT]) \
                                .then_inc(sti.h, 16)
                    sti.n += 16 * 8

            @b.block.sync
            def _(sync, stn=sti.n):
                sync.wait_ge(sti.h, stn)



    b.barrier()

    with ExitStack() as ctxC:
        def sbC(n, s, d=F32):
            return ctxC.enter_context(nc.sbuf_tensor("fC" + n, s, d))
        wo_sb = ctxC.enter_context(nc.sbuf_tensor("fCwo", [128, 256 * 128], BF16))
        ig_sb = ctxC.enter_context(nc.sbuf_tensor("fCig", [128, 32 * NT], BF16))
        xn_n = ctxC.enter_context(nc.sbuf_tensor("fCxn", [128, 8 * NT], BF16))
        y_sb = sbC("y", [128, 8 * NT])
        y_bf = ctxC.enter_context(nc.sbuf_tensor("fCybf", [128, 8 * NT], BF16))
        ysq = ctxC.enter_context(nc.sbuf_tensor("fCysq", [128, 8 * NT], BF16))
        gvec = sbC("gv", [128, 8])
        bvec = sbC("bv", [128, 8])
        bovec = sbC("bo", [128, 8])
        ones1 = ctxC.enter_context(nc.sbuf_tensor("fCones", [128, 1], BF16))
        onescC = sbC("onescC", [1, 128])
        mu_r = sbC("mu", [1, NT])
        var_r = sbC("var", [1, NT])
        musq = sbC("musq", [1, NT])
        rstd_r = sbC("rstd", [1, NT])
        outg = sbC("out", [128, 8 * NT])
        ps_y = [ctxC.enter_context(nc.psum_tensor(f"fCpy{j}", [128, NT], F32))
                for j in range(2)]
        ps_sx = ctxC.enter_context(nc.psum_tensor("fCpsx", [1, NT], F32))
        ps_sq = ctxC.enter_context(nc.psum_tensor("fCpsq", [1, NT], F32))
        ps_mu = ctxC.enter_context(nc.psum_tensor("fCpmu", [128, NT], F32))
        ps_rs = ctxC.enter_context(nc.psum_tensor("fCprs", [128, NT], F32))
        ldw2 = b.sem("fCldw")
        ldi = b.sem("fCldi")
        ldn2 = b.sem("fCldn")
        mmy = b.sem("fCmmy")
        evy = b.sem("fCevy")
        cvy = b.sem("fCcvy")
        smm = b.sem("fCsmm")
        sve = b.sem("fCsve")
        sae = b.sem("fCsae")
        bcy = b.sem("fCbcy")
        nrm = b.sem("fCnrm")
        sil = b.sem("fCsil")
        stg = b.sem("fCstg")

        @b.block.vector
        def _(v):
            v.memset(ones1[:], 1.0)
            v.memset(onescC[:], 1.0)

        @b.block.gpsimd
        def _(g):
            for i in range(32):
                for m in range(8):
                    g.dma_start(
                        out=wo_sb[:, (i * 8 + m) * 128:(i * 8 + m + 1) * 128],
                        in_=wo_t[i, m]).then_inc(ldw2.h, 16)
            g.dma_start(out=bovec[:], in_=bor).then_inc(ldw2.h, 16)
            g.dma_start(out=gvec[:], in_=gamma0r).then_inc(ldw2.h, 16)
            g.dma_start(out=bvec[:], in_=beta0r).then_inc(ldw2.h, 16)
        ldw2.n += 16 * (256 + 3)

        for c in range(NCH):
            @b.block.sync
            def _(sync, c=c, nrn=nrm.n):
                if c > 0:
                    sync.wait_ge(nrm.h, nrn)
                for i in range(32):
                    sync.dma_start(out=ig_sb[:, i * NT:(i + 1) * NT],
                                   in_=ig_d[i, c]).then_inc(ldi.h, 16)
                for a in range(8):
                    sync.dma_start(out=xn_n[:, a * NT:(a + 1) * NT],
                                   in_=xn_c[c, a]).then_inc(ldn2.h, 16)
            ldi.n += 16 * 32
            ldn2.n += 16 * 8

            for m in range(8):
                km = c * 8 + m

                @b.block.tensor
                def _(t, c=c, m=m, km=km, ldin=ldi.n, ldwn=ldw2.n):
                    t.wait_ge(ldi.h, ldin)
                    if km == 0:
                        t.wait_ge(ldw2.h, ldwn)
                    if km >= 2:
                        t.wait_ge(evy.h, km - 1)
                    for i in range(32):
                        ii = t.matmul(
                            ps_y[m % 2][:],
                            wo_sb[:, (i * 8 + m) * 128:(i * 8 + m + 1) * 128],
                            ig_sb[:, i * NT:(i + 1) * NT],
                            start=(i == 0), stop=(i == 31))
                    ii.then_inc(mmy.h, 1)
                mmy.n += 1

                @b.block.vector
                def _(v, c=c, m=m, km=km, ldnn=ldn2.n):
                    v.wait_ge(mmy.h, km + 1)
                    v.wait_ge(ldn2.h, ldnn)
                    i = v.scalar_tensor_tensor(
                        out=y_sb[:, m * NT:(m + 1) * NT], in0=ps_y[m % 2][:],
                        scalar=bovec[:, m:m + 1], in1=xn_n[:, m * NT:(m + 1) * NT],
                        op0=ALU.add, op1=ALU.add)
                    i.then_inc(evy.h, 1)
                evy.n += 1

            @b.block.vector
            def _(v, c=c, evn=evy.n):
                v.wait_ge(evy.h, evn)
                if c > 0:
                    v.wait_ge(smm.h, c)
                i = v.tensor_copy(y_bf[:], y_sb[:])
                i.then_inc(cvy.h, 1)
            cvy.n += 1

            @b.block.scalar
            def _(s, c=c, evn=evy.n):
                s.wait_ge(evy.h, evn)
                if c > 0:
                    s.wait_ge(smm.h, c)
                s.activation(out=ysq[:], in_=y_sb[:], func=AF.Square) \
                    .then_inc(cvy.h, 1)
            cvy.n += 1

            @b.block.tensor
            def _(t, c=c, cvn=cvy.n, nrn=nrm.n):
                t.wait_ge(cvy.h, cvn)
                if c > 0:
                    t.wait_ge(nrm.h, nrn)
                for a in range(8):
                    t.matmul(ps_sx[:], ones1[:], y_bf[:, a * NT:(a + 1) * NT],
                             start=(a == 0), stop=(a == 7))
                for a in range(8):
                    i = t.matmul(ps_sq[:], ones1[:], ysq[:, a * NT:(a + 1) * NT],
                                 start=(a == 0), stop=(a == 7))
                i.then_inc(smm.h, 1)
            smm.n += 1

            @b.block.vector
            def _(v, c=c, svb=sve.n):
                v.wait_ge(smm.h, c + 1)
                v.tensor_scalar(out=mu_r[:], in0=ps_sx[:], scalar1=1.0 / D,
                                scalar2=None, op0=ALU.mult)
                v.tensor_scalar(out=var_r[:], in0=ps_sq[:], scalar1=1.0 / D,
                                scalar2=1e-12, op0=ALU.mult, op1=ALU.add)
                v.tensor_tensor(out=musq[:], in0=mu_r[:], in1=mu_r[:],
                                op=ALU.mult).then_inc(sve.h, 1)
                v.wait_ge(sve.h, svb + 1)
                v.tensor_tensor(out=var_r[:], in0=var_r[:], in1=musq[:],
                                op=ALU.subtract).then_inc(sve.h, 1)
            sve.n += 2

            @b.block.scalar
            def _(s, c=c, svn=sve.n, sab=sae.n):
                s.wait_ge(sve.h, svn)
                s.activation(out=rstd_r[:], in_=var_r[:], func=AF.Ln) \
                    .then_inc(sae.h, 1)
                s.wait_ge(sae.h, sab + 1)
                s.activation(out=rstd_r[:], in_=rstd_r[:], func=AF.Exp,
                             scale=-0.5).then_inc(sae.h, 2)
            sae.n += 3

            @b.block.tensor
            def _(t, c=c, san=sae.n):
                t.wait_ge(sae.h, san)
                t.matmul(ps_mu[:], onescC[:], mu_r[:], start=True, stop=True)
                t.matmul(ps_rs[:], onescC[:], rstd_r[:], start=True, stop=True) \
                    .then_inc(bcy.h, 1)
            bcy.n += 1

            @b.block.vector
            def _(v, c=c, bn=bcy.n, stn=stg.n):
                v.wait_ge(bcy.h, bn)
                if c > 0:
                    v.wait_ge(stg.h, stn)
                for m in range(8):
                    sl = slice(m * NT, (m + 1) * NT)
                    v.tensor_tensor(out=outg[:, sl], in0=y_sb[:, sl],
                                    in1=ps_mu[:], op=ALU.subtract)
                for m in range(8):
                    sl = slice(m * NT, (m + 1) * NT)
                    v.tensor_tensor(out=outg[:, sl], in0=outg[:, sl],
                                    in1=ps_rs[:], op=ALU.mult)
                for m in range(8):
                    sl = slice(m * NT, (m + 1) * NT)
                    i = v.tensor_scalar(out=outg[:, sl], in0=outg[:, sl],
                                        scalar1=gvec[:, m:m + 1],
                                        scalar2=bvec[:, m:m + 1],
                                        op0=ALU.mult, op1=ALU.add)
                i.then_inc(nrm.h, 1)
            nrm.n += 1

            @b.block.scalar
            def _(s, c=c, nrn=nrm.n):
                s.wait_ge(nrm.h, nrn)
                i = s.activation(out=outg[:], in_=outg[:], func=AF.Silu)
                i.then_inc(sil.h, 1)
            sil.n += 1

            @b.block.sync
            def _(sync, c=c, sln=sil.n):
                sync.wait_ge(sil.h, sln)
                for m in range(8):
                    sync.dma_start(out=g0_c[c, m],
                                   in_=outg[:, m * NT:(m + 1) * NT]) \
                        .then_inc(stg.h, 16)
            stg.n += 128

        @b.block.sync
        def _(sync, stn=stg.n):
            sync.wait_ge(stg.h, stn)


def emit_prep(b: B):
    nc = b.nc
    xn0T = b.dram["xn0T"]
    xn_c = xn0T.rearrange("(a p) (c n) -> c a p n", p=128, n=NT)
    ckvT_c = b.dram["ckvT"].rearrange("(m p) (c n) -> c m p n", p=128, n=NT)
    kiT_c = b.dram["kiT"].rearrange("p (c n) -> c p n", n=NT)
    kcat_r = b.dram["kcat"][0:T, :].rearrange("(tm p) f -> tm p f", p=128)
    wcat_r = b.dram["wcatT"].rearrange("(h m p) (nn q) -> h m nn p q", h=8, p=128, q=NT)

    with ExitStack() as ctx:
        def sb(n, s, d=BF16):
            return ctx.enter_context(nc.sbuf_tensor("pp" + n, s, d))
        xn_sb = sb("xn", [128, 8 * T])
        wdkv = sb("wdkv", [128, 8 * 256])
        wkr = sb("wkr", [128, 8 * 32])
        wik = sb("wik", [128, 8 * 64])
        wuvh = sb("wuvh", [64, 8 * 256])
        wout_h = sb("wouth", [64, 1024])
        c63 = sb("c63", [16, T], F32)
        s63 = sb("s63", [16, T], F32)
        c0t = sb("c0t", [16, 64], F32)
        s0t = sb("s0t", [16, 64], F32)
        kmt = sb("kmt", [32, T], F32)
        km2t = sb("km2t", [16, T], F32)
        sps = sb("sps", [32, NT], F32)
        onesb = sb("onesb", [32, 1], F32)
        gko = sb("gko", [16, T])
        gkoa = sb("gkoa", [16, 64])
        t16 = sb("t16", [16, T], F32)
        t16b = sb("t16b", [16, T], F32)
        ident32 = sb("id32", [32, 32], F32)
        ones32 = sb("on32", [32, 32], F32)
        ckv_ev = sb("ckvev", [128, NT])
        ki_ev = sb("kiev", [64, NT])
        kcat_t = sb("kct", [128, 320])
        wc_ev = sb("wcev", [128, NT])
        ps_ck = ctx.enter_context(nc.psum_tensor("ppPc", [128, NT], F32))
        ps_ki = ctx.enter_context(nc.psum_tensor("ppPk", [64, NT], F32))
        ps_kr = ctx.enter_context(nc.psum_tensor("ppPr", [32, NT], F32))
        ps_tm = [ctx.enter_context(nc.psum_tensor(f"ppPt{j}", [128, 288], F32))
                 for j in range(2)]
        ps_wc = ctx.enter_context(nc.psum_tensor("ppPw", [128, NT], F32))
        ldx = b.sem("ppldx")
        ldw = b.sem("ppldw")
        ldt = b.sem("ppldt")
        mme = b.sem("ppmme")
        eve = b.sem("ppeve")
        spd = b.sem("ppspd")
        stc = b.sem("ppstc")
        kmd = b.sem("ppkmd")
        gkd = b.sem("ppgkd")

        @b.block.sync
        def _(sync):
            for a in range(8):
                for c in range(NCH):
                    sync.dma_start(out=xn_sb[:, a * T + c * NT: a * T + (c + 1) * NT],
                                   in_=xn_c[c, a]).then_inc(ldx.h, 16)
        ldx.n += 16 * 64

        @b.block.gpsimd
        def _(g):
            for a in range(8):
                g.dma_start(out=wdkv[:, a * 256:(a + 1) * 256],
                            in_=b.dram["w_dkv"][a * 128:(a + 1) * 128, :]) \
                    .then_inc(ldw.h, 16)
                g.dma_start(out=wkr[:, a * 32:(a + 1) * 32],
                            in_=b.dram["w_kr"][a * 128:(a + 1) * 128, :]) \
                    .then_inc(ldw.h, 16)
                g.dma_start(out=wik[:, a * 64:(a + 1) * 64],
                            in_=b.dram["idx_wk"][a * 128:(a + 1) * 128, :]) \
                    .then_inc(ldw.h, 16)
            for h in range(8):
                g.dma_start(out=wuvh[:, h * 256:(h + 1) * 256],
                            in_=b.dram["w_uvTh"][h]).then_inc(ldw.h, 16)
            g.dma_start(out=c63[:], in_=b.dram["c63"]).then_inc(ldt.h, 16)
            g.dma_start(out=s63[:], in_=b.dram["s63"]).then_inc(ldt.h, 16)
            g.dma_start(out=c0t[:], in_=b.dram["c0"]).then_inc(ldt.h, 16)
            g.dma_start(out=s0t[:], in_=b.dram["s0"]).then_inc(ldt.h, 16)
        ldw.n += 16 * 32
        ldt.n += 64

        @b.block.vector
        def _(v):
            v.memset(onesb[:], 1.0)
            v.memset(ones32[:], 1.0)
            v.memset(kcat_t[:], 0.0)

        @b.block.gpsimd
        def _(g):
            g.affine_select(out=ident32[:], in_=ones32[:], pattern=[[1, 32]],
                            compare_op=ALU.is_equal, fill=0.0, base=0,
                            channel_multiplier=-1).then_inc(ldt.h, 1)
        ldt.n += 1

        for c in range(NCH):
            for mc in range(2):
                @b.block.tensor
                def _(t, c=c, mc=mc, en=eve.n):
                    if c == 0 and mc == 0:
                        t.wait_ge(ldx.h, ldx.n)
                        t.wait_ge(ldw.h, 16 * 24)
                    if c + mc > 0:
                        t.wait_ge(eve.h, en)
                    for a in range(8):
                        i = t.matmul(ps_ck[:],
                                     wdkv[:, a * 256 + mc * 128: a * 256 + (mc + 1) * 128],
                                     xn_sb[:, a * T + c * NT: a * T + (c + 1) * NT],
                                     start=(a == 0), stop=(a == 7))
                    i.then_inc(mme.h, 1)
                mme.n += 1

                @b.block.vector
                def _(v, c=c, mc=mc, mn=mme.n, sn=stc.n):
                    v.wait_ge(mme.h, mn)
                    if c + mc > 0:
                        v.wait_ge(stc.h, sn)
                    v.tensor_copy(ckv_ev[:], ps_ck[:]).then_inc(eve.h, 1)
                eve.n += 1

                @b.block.sync
                def _(sync, c=c, mc=mc, en=eve.n):
                    sync.wait_ge(eve.h, en)
                    sync.dma_start(out=ckvT_c[c, mc], in_=ckv_ev[:]).then_inc(stc.h, 16)
                stc.n += 16

            @b.block.tensor
            def _(t, c=c, en=eve.n):
                if c > 0:
                    t.wait_ge(eve.h, en - 1)
                for a in range(8):
                    i = t.matmul(ps_ki[:], wik[:, a * 64:(a + 1) * 64],
                                 xn_sb[:, a * T + c * NT: a * T + (c + 1) * NT],
                                 start=(a == 0), stop=(a == 7))
                i.then_inc(mme.h, 1)
            mme.n += 1

            @b.block.vector
            def _(v, c=c, mn=mme.n, sn=stc.n):
                v.wait_ge(mme.h, mn)
                if c > 0:
                    v.wait_ge(stc.h, sn - 16)
                v.tensor_copy(ki_ev[:], ps_ki[:]).then_inc(eve.h, 1)
            eve.n += 1

            @b.block.sync
            def _(sync, c=c, en=eve.n):
                sync.wait_ge(eve.h, en)
                sync.dma_start(out=kiT_c[c], in_=ki_ev[:]).then_inc(stc.h, 16)
            stc.n += 16

            @b.block.tensor
            def _(t, c=c, spn=spd.n):
                if c > 0:
                    t.wait_ge(spd.h, spn)
                for a in range(8):
                    i = t.matmul(ps_kr[:], wkr[:, a * 32:(a + 1) * 32],
                                 xn_sb[:, a * T + c * NT: a * T + (c + 1) * NT],
                                 start=(a == 0), stop=(a == 7))
                i.then_inc(mme.h, 1)
            mme.n += 1

            @b.block.scalar
            def _(s, c=c, mn=mme.n, spb=spd.n):
                s.wait_ge(mme.h, mn)
                s.activation(out=sps[:], in_=ps_kr[:], func=AF.Abs).then_inc(spd.h, 1)
                s.wait_ge(spd.h, spb + 1)
                s.activation(out=sps[:], in_=sps[:], func=AF.Exp, scale=-1.0) \
                    .then_inc(spd.h, 1)
                s.wait_ge(spd.h, spb + 2)
                s.activation(out=sps[:], in_=sps[:], func=AF.Ln, bias=onesb[:]) \
                    .then_inc(spd.h, 1)
            spd.n += 3

            @b.block.vector
            def _(v, c=c, spn=spd.n):
                v.wait_ge(spd.h, spn)
                i = v.scalar_tensor_tensor(out=kmt[:, c * NT:(c + 1) * NT],
                                           in0=ps_kr[:], scalar=0.0, in1=sps[:],
                                           op0=ALU.max, op1=ALU.add)
                i.then_inc(spd.h, 1)
            spd.n += 1

        @b.block.gpsimd
        def _(g, spn=spd.n):
            g.wait_ge(spd.h, spn)
            g.dma_start(out=km2t[:], in_=kmt[16:32, :]).then_inc(kmd.h, 16)
        kmd.n += 16

        @b.block.vector
        def _(v, kn=kmd.n, gb=gkd.n):
            v.wait_ge(kmd.h, kn)
            v.wait_ge(ldt.h, 65)
            v.tensor_tensor(out=t16[:], in0=kmt[0:16, :], in1=c63[:], op=ALU.mult)
            v.tensor_tensor(out=t16b[:], in0=km2t[:], in1=s63[:], op=ALU.mult)
            v.tensor_tensor(out=gko[:], in0=t16[:], in1=t16b[:], op=ALU.subtract) \
                .then_inc(gkd.h, 1)
        gkd.n += 1

        @b.block.sync
        def _(sync, gn=gkd.n):
            sync.wait_ge(gkd.h, gn)
            sync.dma_start(out=b.dram["gkb1T"], in_=gko[:]).then_inc(stc.h, 16)
        stc.n += 16

        @b.block.vector
        def _(v, sn=stc.n):
            v.wait_ge(stc.h, sn)
            v.tensor_tensor(out=t16[:], in0=kmt[0:16, :], in1=s63[:], op=ALU.mult)
            v.tensor_tensor(out=t16b[:], in0=km2t[:], in1=c63[:], op=ALU.mult)
            v.tensor_tensor(out=gko[:], in0=t16[:], in1=t16b[:], op=ALU.add) \
                .then_inc(gkd.h, 1)
        gkd.n += 1

        @b.block.sync
        def _(sync, gn=gkd.n):
            sync.wait_ge(gkd.h, gn)
            sync.dma_start(out=b.dram["gkb2T"], in_=gko[:]).then_inc(stc.h, 16)
        stc.n += 16

        @b.block.vector
        def _(v, sn=stc.n):
            v.wait_ge(stc.h, sn)
            v.tensor_tensor(out=t16[:, 0:64], in0=kmt[0:16, 0:64], in1=c0t[:],
                            op=ALU.mult)
            v.tensor_tensor(out=t16b[:, 0:64], in0=km2t[:, 0:64], in1=s0t[:],
                            op=ALU.mult)
            v.tensor_tensor(out=gkoa[:], in0=t16[:, 0:64], in1=t16b[:, 0:64],
                            op=ALU.subtract).then_inc(gkd.h, 1)
        gkd.n += 1

        @b.block.sync
        def _(sync, gn=gkd.n):
            sync.wait_ge(gkd.h, gn)
            sync.dma_start(out=b.dram["gka1T"], in_=gkoa[:]).then_inc(stc.h, 16)
        stc.n += 16

        @b.block.vector
        def _(v, sn=stc.n):
            v.wait_ge(stc.h, sn)
            v.tensor_tensor(out=t16[:, 0:64], in0=kmt[0:16, 0:64], in1=s0t[:],
                            op=ALU.mult)
            v.tensor_tensor(out=t16b[:, 0:64], in0=km2t[:, 0:64], in1=c0t[:],
                            op=ALU.mult)
            v.tensor_tensor(out=gkoa[:], in0=t16[:, 0:64], in1=t16b[:, 0:64],
                            op=ALU.add).then_inc(gkd.h, 1)
        gkd.n += 1

        @b.block.sync
        def _(sync, gn=gkd.n):
            sync.wait_ge(gkd.h, gn)
            sync.dma_start(out=b.dram["gka2T"], in_=gkoa[:]).then_inc(stc.h, 16)
        stc.n += 16

        tmev = b.sem("pptmev")
        tmst = b.sem("pptmst")
        for tm in range(32):
            @b.block.tensor
            def _(t, tm=tm, en=tmev.n, spn=spd.n, ltn=ldt.n):
                if tm == 0:
                    t.wait_ge(spd.h, spn)
                    t.wait_ge(ldt.h, ltn)
                if tm >= 2:
                    t.wait_ge(tmev.h, en - 1)
                for a in range(8):
                    t.matmul(ps_tm[tm % 2][:, 0:256],
                             xn_sb[:, a * T + tm * 128: a * T + (tm + 1) * 128],
                             wdkv[:, a * 256:(a + 1) * 256],
                             start=(a == 0), stop=(a == 7))
                i = t.transpose(ps_tm[tm % 2][:, 256:288],
                                kmt[:, tm * 128:(tm + 1) * 128], ident32[:])
                i.then_inc(mme.h, 1)
            mme.n += 1

            @b.block.vector
            def _(v, tm=tm, mn=mme.n, sn=tmst.n):
                v.wait_ge(mme.h, mn)
                if tm >= 1:
                    v.wait_ge(tmst.h, sn)
                v.tensor_copy(kcat_t[:, 0:256], ps_tm[tm % 2][:, 0:256])
                v.tensor_copy(kcat_t[:, 256:272], ps_tm[tm % 2][:, 256:272])
                i = v.tensor_copy(kcat_t[:, 288:304], ps_tm[tm % 2][:, 272:288])
                i.then_inc(tmev.h, 1)
            tmev.n += 1

            @b.block.sync
            def _(sync, tm=tm, en=tmev.n):
                sync.wait_ge(tmev.h, en)
                sync.dma_start(out=kcat_r[tm], in_=kcat_t[:]).then_inc(tmst.h, 16)
            tmst.n += 16

        wch = b.sem("ppwch")
        for h in range(8):
            @b.block.gpsimd
            def _(g, h=h, wn=mme.n):
                if h > 0:
                    g.wait_ge(mme.h, wn)
                g.dma_start(out=wout_h[:],
                            in_=b.dram["w_outm"][h * 64:(h + 1) * 64, :]) \
                    .then_inc(ldw.h, 16)
            ldw.n += 16

            for mc2 in range(2):
                for nn in range(2):
                    @b.block.tensor
                    def _(t, h=h, mc2=mc2, nn=nn, ldn=ldw.n, en=eve.n):
                        t.wait_ge(ldw.h, ldn)
                        if h + mc2 + nn > 0:
                            t.wait_ge(eve.h, en)
                        i = t.matmul(ps_wc[:],
                                     wuvh[:, h * 256 + mc2 * 128: h * 256 + (mc2 + 1) * 128],
                                     wout_h[:, nn * NT:(nn + 1) * NT],
                                     start=True, stop=True)
                        i.then_inc(mme.h, 1)
                    mme.n += 1

                    @b.block.vector
                    def _(v, h=h, mc2=mc2, nn=nn, mn=mme.n, sn=stc.n):
                        v.wait_ge(mme.h, mn)
                        if h + mc2 + nn > 0:
                            v.wait_ge(stc.h, sn)
                        v.tensor_copy(wc_ev[:], ps_wc[:]).then_inc(eve.h, 1)
                    eve.n += 1

                    @b.block.sync
                    def _(sync, h=h, mc2=mc2, nn=nn, en=eve.n):
                        sync.wait_ge(eve.h, en)
                        sync.dma_start(out=wcat_r[h, mc2, nn], in_=wc_ev[:]) \
                            .then_inc(stc.h, 16)
                    stc.n += 16

        @b.block.sync
        def _(sync, tn=tmev.n):
            sync.wait_ge(tmev.h, tn)

        @b.block.vector
        def _(v, tn=tmst.n):
            v.wait_ge(tmst.h, tn)
            v.memset(kcat_t[:, :], 0.0).then_inc(tmev.h, 1)
        tmev.n += 1

        @b.block.sync
        def _(sync, tn=tmev.n):
            sync.wait_ge(tmev.h, tn)
            sync.dma_start(out=b.dram["kcat"][T:T + 16, :], in_=kcat_t[0:16, :]) \
                .then_inc(tmst.h, 16)
        tmst.n += 16

        @b.block.sync
        def _(sync, sn=stc.n, tn=tmst.n):
            sync.wait_ge(stc.h, sn)
            sync.wait_ge(tmst.h, tn)



L0ATT_TILES = 32
L0ATT_STAGE = 9


def emit_l0att(b: B):
    NTIL = L0ATT_TILES
    STG = L0ATT_STAGE
    nc = b.nc
    xn_r = b.dram["xn0T"].rearrange("(a p) t -> a p t", p=128)
    xT_r = b.dram["xT"].rearrange("(a p) t -> a p t", p=128)
    g0_r = b.dram["g0T"].rearrange("(a p) t -> a p t", p=128)
    x1_r = b.dram["x1T"].rearrange("(a p) t -> a p t", p=128)
    ckvT_r = b.dram["ckvT"].rearrange("(m p) t -> m p t", p=128)
    kcat = b.dram["kcat"]
    idx_d = b.dram["idx_d"]
    GL = 16
    TP = T + 16

    with ExitStack() as ctx:
        def sb(n, s, d=BF16):
            return ctx.enter_context(nc.sbuf_tensor("at" + n, s, d))
        ckvb = sb("ckvb", [128, 2 * 256])
        ki_sb = sb("ki", [64, T])
        gkwP = sb("gkwP", [128, 256])
        gkaP = sb("gkaP", [128, 64])
        wiq = sb("wiq", [128, 8 * 256])
        wdq = sb("wdq", [128, 8 * 512])
        wuq = sb("wuq", [128, 4 * 512])
        wqr = sb("wqr", [128, 4 * 256])
        wukh = sb("wukh", [64, 8 * 256])
        wcat_sb = sb("wcat", [128, 16 * 1024])
        identb = sb("idb", [128, 128])
        maskd = sb("mskd", [128, 512], F32)
        maskb = sb("mskb", [128, 80], F32)
        crot_rep = sb("crotr", [128, 512])
        srot_rep = sb("srotr", [128, 512])
        crot = sb("crot", [128, 16])
        srot = sb("srot", [128, 16])
        nfin0 = sb("nfin", [128, 1], F32)
        iotaj = sb("ioj", [128, 32], F32)
        fill64 = sb("f64", [128, 32], U32)
        idxw_b = sb("idxw", [128, 4], F32)
        ones1 = sb("on1", [128, 1], F32)
        xn_t = sb("xnt", [128, 8 * 128])
        ctr = sb("ctr", [128, 128], F32)
        str_ = sb("stru", [128, 128], F32)
        c2r = sb("c2r", [128, 128], F32)
        s2r = sb("s2r", [128, 128], F32)
        ct4 = sb("ct4", [128, 4 * 16], F32)
        qi64 = sb("qi", [64, 4 * 128])
        cq_sb = sb("cq", [128, 4 * 128])
        quc64 = sb("quc", [64, 8 * 128])
        qmu = sb("qmu", [128, 256], F32)
        fsc1 = sb("fsc1", [128, 128], F32)
        fsc2 = sb("fsc2", [128, 128], F32)
        ftile = sb("ft", [128, 4 * 128])
        qabs = sb("qabs", [128, 2 * 1024])
        qfPa = sb("qfPa", [128, 1024])
        qfPb = sb("qfPb", [128, 1024])
        I0 = ctx.enter_context(nc.sbuf_tensor("atI0", [128, T], F32))
        I1 = ctx.enter_context(nc.sbuf_tensor("atI1", [128, T], F32))
        itmp = sb("itmp", [128, 512], F32)
        m8x = sb("m8x", [128, 8], F32)
        idxall = sb("idxa", [128, 32], U32)
        selm2 = sb("selm", [128, 32], U8)
        idxg = sb("idxg", [128, 32], U32)
        kg = sb("kg", [128, 32 * 320])
        mtmp1 = sb("mt1", [128, 512])
        mtmp2 = sb("mt2", [128, 512])
        r1c = sb("r1c", [128, 512])
        kgt1 = sb("kgt1", [128, T])
        kgt2 = sb("kgt2", [128, T])
        krotP = sb("krP", [128, T])
        sdyn = sb("sdyn", [128, 512], F32)
        sband = sb("sbnd", [128, 80], F32)
        mrow = sb("mrow", [128, 1], F32)
        mrow2 = sb("mrw2", [128, 1], F32)
        negm = sb("negm", [128, 1], F32)
        z1 = sb("z1", [128, 1], F32)
        z2 = sb("z2", [128, 1], F32)
        zsum = sb("zsum", [128, 1], F32)
        zinv = sb("zinv", [128, 1], F32)
        pd_sb = sb("pd", [128, 512])
        pb_sb = sb("pb", [128, 80])
        pt_sb = sb("pt", [128, 4 * 128])
        ptb = sb("ptb", [80, 128])
        band_sb = sb("band", [80, 256])
        oc_sb = sb("oc", [128, 2 * 1024])
        xg_sb = sb("xg", [128, 2 * 128], F32)
        x1o = sb("x1o", [128, 128], F32)
        ps_q = ctx.enter_context(nc.psum_tensor("atPq", [128, 512], F32))
        ps_I = ctx.enter_context(nc.psum_tensor("atPI", [128, 512], F32))
        ps_sc = ctx.enter_context(nc.psum_tensor("atPs", [128, 512], F32))
        ps_bd = ctx.enter_context(nc.psum_tensor("atPbd", [128, 80], F32))
        ps_t1 = ctx.enter_context(nc.psum_tensor("atPt1", [128, 128], BF16))
        ps_t2 = ctx.enter_context(nc.psum_tensor("atPt2", [128, 128], BF16))
        ps_av = ctx.enter_context(nc.psum_tensor("atPav", [128, 256], F32))
        ps_ep = ctx.enter_context(nc.psum_tensor("atPep", [128, 128], F32))

        ldp = b.sem("atldp")
        ldn = b.sem("atldn")
        qmm = b.sem("atqmm")
        qev = b.sem("atqev")
        sfp = b.sem("atsfp")
        iev = b.sem("atiev")
        ext = b.sem("atext")
        gpd = b.sem("atgpd")
        idd = b.sem("atidd")
        mrt = b.sem("atmrt")
        tev = b.sem("attev")
        smx = b.sem("atsmx")
        sxe = b.sem("atsxe")
        ave = b.sem("atave")
        bnd = b.sem("atbnd")
        cmb = b.sem("atcmb")
        xst = b.sem("atxst")
        dbgs = b.sem("atdbgs")

        @b.block.sync
        def _(sync):
            sync.dma_start(out=ki_sb[:], in_=b.dram["kiT"]).then_inc(ldp.h, 16)
            sync.dma_start(out=identb[:], in_=b.dram["identbf"]).then_inc(ldp.h, 16)
            for hc in range(16):
                sync.dma_start(out=wcat_sb[:, hc * 1024:(hc + 1) * 1024],
                               in_=b.dram["wcatT"][hc * 128:(hc + 1) * 128, :]) \
                    .then_inc(ldp.h, 16)
        ldp.n += 16 * 18

        @b.block.gpsimd
        def _(g):
            for a in range(8):
                g.dma_start(out=wiq[:, a * 256:(a + 1) * 256],
                            in_=b.dram["idx_wq"][a * 128:(a + 1) * 128, :]) \
                    .then_inc(ldp.h, 16)
                g.dma_start(out=wdq[:, a * 512:(a + 1) * 512],
                            in_=b.dram["w_dq"][a * 128:(a + 1) * 128, :]) \
                    .then_inc(ldp.h, 16)
            for a in range(4):
                g.dma_start(out=wuq[:, a * 512:(a + 1) * 512],
                            in_=b.dram["w_uq"][a * 128:(a + 1) * 128, :]) \
                    .then_inc(ldp.h, 16)
                g.dma_start(out=wqr[:, a * 256:(a + 1) * 256],
                            in_=b.dram["w_qr"][a * 128:(a + 1) * 128, :]) \
                    .then_inc(ldp.h, 16)
            for h in range(8):
                g.dma_start(out=wukh[:, h * 256:(h + 1) * 256],
                            in_=b.dram["w_ukTh"][h]).then_inc(ldp.h, 16)
            g.dma_start(out=maskd[:], in_=b.dram["maskdyn"]).then_inc(ldp.h, 16)
            g.dma_start(out=maskb[:], in_=b.dram["maskband"]).then_inc(ldp.h, 16)
            g.dma_start(out=crot[:], in_=b.dram["crot"]).then_inc(ldp.h, 16)
            g.dma_start(out=srot[:], in_=b.dram["srot"]).then_inc(ldp.h, 16)
            g.dma_start(out=nfin0[:], in_=b.dram["nfin0"]).then_inc(ldp.h, 16)
            g.dma_start(out=iotaj[:], in_=b.dram["iotaj"]).then_inc(ldp.h, 16)
            g.dma_start(out=fill64[:], in_=b.dram["fill64"]).then_inc(ldp.h, 16)
            g.dma_start(out=idxw_b[:], in_=b.dram["idxwb"]).then_inc(ldp.h, 16)
        ldp.n += 16 * 36

        @b.block.vector
        def _(v, lp=ldp.n):
            v.memset(ones1[:], 1.0)
            v.memset(qfPa[:], 0.0)
            v.memset(qfPb[:], 0.0)
            v.memset(krotP[:], 0.0)
            v.memset(gkaP[:], 0.0)
            v.wait_ge(ldp.h, lp)
            for cc in range(32):
                v.tensor_copy(crot_rep[:, cc * 16:(cc + 1) * 16], crot[:])
                i = v.tensor_copy(srot_rep[:, cc * 16:(cc + 1) * 16], srot[:])
            i.then_inc(ldp.h, 1)
        ldp.n += 1

        @b.block.sync
        def _(sync, lp=ldp.n):
            sync.wait_ge(ldp.h, lp)
            sync.dma_start(out=gkaP[0:16, :], in_=b.dram["gka1T"]).then_inc(ldp.h, 16)
            sync.dma_start(out=gkaP[32:48, :], in_=b.dram["gka2T"]).then_inc(ldp.h, 16)
        ldp.n += 32

        for ti in range(NTIL):
            t0 = ti * 128
            WS = min((t0 + 64 + NT - 1) // NT, NCH)
            tile0 = (ti == 0)

            qmm_prev = qmm.n
            gpd_prev = gpd.n
            idd_prev = idd.n

            @b.block.sync
            def _(sync, ti=ti, t0=t0, qp=qmm_prev):
                if ti > 0:
                    sync.wait_ge(qmm.h, qp)
                for a in range(8):
                    sync.dma_start(out=xn_t[:, a * 128:(a + 1) * 128],
                                   in_=xn_r[a][:, t0:t0 + 128]).then_inc(ldn.h, 16)
                for (j, nmk) in enumerate(("ct_tm", "st_tm", "c2t_tm", "s2t_tm")):
                    sync.dma_start(out=ct4[:, j * 16:(j + 1) * 16],
                                   in_=b.dram[nmk][t0:t0 + 128, :]).then_inc(ldn.h, 16)
            ldn.n += 16 * 12

            wbase = max(t0 - 63, 0)
            wid = min(192, T - wbase)

            @b.block.vector
            def _(v, ti=ti, qp=qmm_prev, bz=bnd.n):
                if ti > 0:
                    v.wait_ge(qmm.h, qp)
                v.memset(ckvb[:], 0.0)
                v.memset(gkwP[:], 0.0).then_inc(bnd.h, 1)
            bnd.n += 1

            @b.block.sync
            def _(sync, ti=ti, wbase=wbase, wid=wid, bz=bnd.n):
                sync.wait_ge(bnd.h, bz)
                for mc in range(2):
                    sync.dma_start(out=ckvb[:, mc * 256: mc * 256 + wid],
                                   in_=ckvT_r[mc][:, wbase:wbase + wid]) \
                        .then_inc(bnd.h, 16)
                sync.dma_start(out=gkwP[0:16, 0:wid],
                               in_=b.dram["gkb1T"][:, wbase:wbase + wid]) \
                    .then_inc(bnd.h, 16)
                sync.dma_start(out=gkwP[32:48, 0:wid],
                               in_=b.dram["gkb2T"][:, wbase:wbase + wid]) \
                    .then_inc(bnd.h, 16)
            bnd.n += 64

            for (dst, wt, nmc, kk, wstep) in (
                    ("qi", wiq, 2, 8, 256), ("cq", wdq, 4, 8, 512)):
                for mc in range(nmc):
                    @b.block.tensor
                    def _(t, ti=ti, dst=dst, wt=wt, mc=mc, kk=kk, wstep=wstep,
                          qn=qev.n, ldnn=ldn.n, lp=ldp.n):
                        t.wait_ge(ldn.h, ldnn)
                        if ti == 0 and dst == "qi" and mc == 0:
                            t.wait_ge(ldp.h, lp)
                        if qn > 0:
                            t.wait_ge(qev.h, qn)
                        for a in range(kk):
                            i = t.matmul(
                                ps_q[:, 0:128],
                                wt[:, a * wstep + mc * 128: a * wstep + (mc + 1) * 128],
                                xn_t[:, a * 128:(a + 1) * 128],
                                start=(a == 0), stop=(a == kk - 1))
                        i.then_inc(qmm.h, 1)
                    qmm.n += 1

                    @b.block.vector
                    def _(v, dst=dst, mc=mc, qn=qmm.n):
                        v.wait_ge(qmm.h, qn)
                        if dst == "qi":
                            v.tensor_copy(qi64[:, (2 * mc) * 128:(2 * mc + 1) * 128],
                                          ps_q[0:64, 0:128])
                            v.tensor_copy(qi64[:, (2 * mc + 1) * 128:(2 * mc + 2) * 128],
                                          ps_q[64:128, 0:128]).then_inc(qev.h, 1)
                        else:
                            v.tensor_copy(cq_sb[:, mc * 128:(mc + 1) * 128],
                                          ps_q[:, 0:128]).then_inc(qev.h, 1)
                    qev.n += 1

            for mc in range(4):
                @b.block.tensor
                def _(t, mc=mc, qn=qev.n):
                    t.wait_ge(qev.h, qn)
                    for a in range(4):
                        i = t.matmul(
                            ps_q[:, 0:128],
                            wuq[:, a * 512 + mc * 128: a * 512 + (mc + 1) * 128],
                            cq_sb[:, a * 128:(a + 1) * 128],
                            start=(a == 0), stop=(a == 3))
                    i.then_inc(qmm.h, 1)
                qmm.n += 1

                @b.block.vector
                def _(v, mc=mc, qn=qmm.n):
                    v.wait_ge(qmm.h, qn)
                    v.tensor_copy(quc64[:, (2 * mc) * 128:(2 * mc + 1) * 128],
                                  ps_q[0:64, 0:128])
                    v.tensor_copy(quc64[:, (2 * mc + 1) * 128:(2 * mc + 2) * 128],
                                  ps_q[64:128, 0:128]).then_inc(qev.h, 1)
                qev.n += 1

            @b.block.tensor
            def _(t, ti=ti, qn=qev.n, sn=sfp.n):
                t.wait_ge(qev.h, qn)
                if ti > 0:
                    t.wait_ge(sfp.h, sn)
                for a in range(4):
                    i = t.matmul(ps_q[:, 0:256], cq_sb[:, a * 128:(a + 1) * 128],
                                 wqr[:, a * 256:(a + 1) * 256],
                                 start=(a == 0), stop=(a == 3))
                i.then_inc(qmm.h, 1)
            qmm.n += 1

            @b.block.scalar
            def _(s, qn=qmm.n, sb_=sfp.n):
                s.wait_ge(qmm.h, qn)
                s.activation(out=qmu[:], in_=ps_q[:, 0:256], func=AF.Abs) \
                    .then_inc(sfp.h, 1)
                s.wait_ge(sfp.h, sb_ + 1)
                s.activation(out=qmu[:], in_=qmu[:], func=AF.Exp, scale=-1.0) \
                    .then_inc(sfp.h, 1)
                s.wait_ge(sfp.h, sb_ + 2)
                s.activation(out=qmu[:], in_=qmu[:], func=AF.Ln, bias=ones1[:]) \
                    .then_inc(sfp.h, 1)
            sfp.n += 3

            @b.block.vector
            def _(v, sn=sfp.n, ldnn=ldn.n):
                v.wait_ge(sfp.h, sn)
                i = v.scalar_tensor_tensor(out=qmu[:], in0=ps_q[:, 0:256],
                                           scalar=0.0, in1=qmu[:],
                                           op0=ALU.max, op1=ALU.add)
                i.then_inc(sfp.h, 1)
                v.wait_ge(ldn.h, ldnn)
                for h in range(8):
                    v.tensor_copy(ctr[:, h * 16:(h + 1) * 16], ct4[:, 0:16])
                    v.tensor_copy(str_[:, h * 16:(h + 1) * 16], ct4[:, 16:32])
                    v.tensor_copy(c2r[:, h * 16:(h + 1) * 16], ct4[:, 32:48])
                    i = v.tensor_copy(s2r[:, h * 16:(h + 1) * 16], ct4[:, 48:64])
                i.then_inc(sfp.h, 1)
                for (fi, c_, s_) in ((0, ctr, str_), (2, c2r, s2r)):
                    v.tensor_tensor(out=fsc1[:], in0=qmu[:, 0:128], in1=c_[:],
                                    op=ALU.mult)
                    v.tensor_tensor(out=fsc2[:], in0=qmu[:, 128:256], in1=s_[:],
                                    op=ALU.mult)
                    v.tensor_tensor(out=ftile[:, fi * 128:(fi + 1) * 128],
                                    in0=fsc1[:], in1=fsc2[:], op=ALU.subtract)
                    v.tensor_tensor(out=fsc1[:], in0=qmu[:, 0:128], in1=s_[:],
                                    op=ALU.mult)
                    v.tensor_tensor(out=fsc2[:], in0=qmu[:, 128:256], in1=c_[:],
                                    op=ALU.mult)
                    i = v.tensor_tensor(out=ftile[:, (fi + 1) * 128:(fi + 2) * 128],
                                        in0=fsc1[:], in1=fsc2[:], op=ALU.add)
                i.then_inc(sfp.h, 1)
            sfp.n += 3

            for h in range(8):
                for mc in range(2):
                    @b.block.tensor
                    def _(t, h=h, mc=mc, qn=qev.n, sn=sfp.n):
                        t.wait_ge(qev.h, qn)
                        if h == 0 and mc == 0:
                            t.wait_ge(sfp.h, sn)
                        i = t.matmul(
                            ps_q[:, 0:128],
                            wukh[:, h * 256 + mc * 128: h * 256 + (mc + 1) * 128],
                            quc64[:, h * 128:(h + 1) * 128],
                            start=True, stop=True)
                        i.then_inc(qmm.h, 1)
                    qmm.n += 1

                    @b.block.vector
                    def _(v, h=h, mc=mc, qn=qmm.n):
                        v.wait_ge(qmm.h, qn)
                        dst = qabs[:, mc * 1024:(mc + 1) * 1024] \
                            .rearrange("p (m x) -> p m x", x=128)[:, :, h * GL:(h + 1) * GL]
                        v.tensor_scalar(out=dst, in0=ps_q[:, 0:128],
                                        scalar1=ATT_SCALE, scalar2=None,
                                        op0=ALU.mult).then_inc(qev.h, 1)
                    qev.n += 1

            qf_order = ("1a", "2a", "1b", "2b")
            for fi in range(4):
                for h in range(8):
                    kk = fi * 8 + h

                    @b.block.tensor
                    def _(t, fi=fi, h=h, kk=kk, sn=sfp.n, tn=tev.n):
                        t.wait_ge(sfp.h, sn)
                        if tn >= 2:
                            t.wait_ge(tev.h, tn - 1)
                        pst = ps_t1 if kk % 2 == 0 else ps_t2
                        i = t.transpose(pst[0:16, :],
                                        ftile[:, fi * 128 + h * 16: fi * 128 + (h + 1) * 16],
                                        identb[:])
                        i.then_inc(qmm.h, 1)
                    qmm.n += 1

                    @b.block.vector
                    def _(v, fi=fi, h=h, kk=kk, qn=qmm.n):
                        v.wait_ge(qmm.h, qn)
                        pst = ps_t1 if kk % 2 == 0 else ps_t2
                        tile_ = qfPa if fi < 2 else qfPb
                        rbase = 0 if fi % 2 == 0 else 32
                        dst = tile_[rbase:rbase + 16, :] \
                            .rearrange("p (m x) -> p m x", x=128)[:, :, h * GL:(h + 1) * GL]
                        v.tensor_scalar(out=dst, in0=pst[0:16, :],
                                        scalar1=ATT_SCALE, scalar2=None,
                                        op0=ALU.mult).then_inc(tev.h, 1)
                    tev.n += 1

            if STG < 2:
                continue
            for wc in range(WS):
                for h4 in range(4):
                    @b.block.tensor
                    def _(t, wc=wc, h4=h4, ien=iev.n, gp=gpd_prev):
                        if wc == 0 and h4 == 0:
                            t.wait_ge(gpd.h, gp)
                        if ien > 0:
                            t.wait_ge(iev.h, ien)
                        i = t.matmul(ps_I[:],
                                     qi64[:, h4 * 128:(h4 + 1) * 128],
                                     ki_sb[:, wc * NT:(wc + 1) * NT],
                                     start=True, stop=True)
                        i.then_inc(qmm.h, 1)
                    qmm.n += 1

                    @b.block.vector
                    def _(v, wc=wc, h4=h4, qn=qmm.n, ib=iev.n):
                        v.wait_ge(qmm.h, qn)
                        sl = slice(wc * NT, (wc + 1) * NT)
                        if h4 == 0:
                            v.tensor_scalar(out=I0[:, sl], in0=ps_I[:],
                                            scalar1=0.0, scalar2=idxw_b[:, 0:1],
                                            op0=ALU.max, op1=ALU.mult) \
                                .then_inc(iev.h, 1)
                        else:
                            v.tensor_scalar(out=itmp[:], in0=ps_I[:],
                                            scalar1=0.0,
                                            scalar2=idxw_b[:, h4:h4 + 1],
                                            op0=ALU.max, op1=ALU.mult) \
                                .then_inc(iev.h, 1)
                            v.wait_ge(iev.h, ib + 1)
                            v.tensor_tensor(out=I0[:, sl], in0=I0[:, sl],
                                            in1=itmp[:], op=ALU.add) \
                                .then_inc(iev.h, 1)
                    iev.n += 1 if h4 == 0 else 2

            @b.block.gpsimd
            def _(g, ti=ti, t0=t0, WS=WS, ien=iev.n, en=ext.n):
                g.wait_ge(iev.h, ien)
                if ti > 0:
                    g.wait_ge(ext.h, en)
                g.affine_select(out=I0[:, 0:WS * NT], in_=I0[:, 0:WS * NT],
                                pattern=[[-1, WS * NT]], compare_op=ALU.is_ge,
                                fill=NEG, base=t0 - 64, channel_multiplier=1) \
                    .then_inc(gpd.h, 1)
            gpd.n += 1

            if STG < 3:
                continue
            W_ = WS * NT
            for r in range(4):
                src = I0 if r % 2 == 0 else I1
                dst = I1 if r % 2 == 0 else I0

                @b.block.vector
                def _(v, r=r, src=src, dst=dst, W_=W_, gn=gpd.n, eb=ext.n,
                      idp=idd_prev):
                    if r == 0:
                        v.wait_ge(gpd.h, gn)
                        v.wait_ge(idd.h, idp)
                    v.max(out=m8x[:], in_=src[:, 0:W_]).then_inc(ext.h, 1)
                    v.wait_ge(ext.h, eb + 1)
                    v.max_index(out=idxall[:, r * 8:(r + 1) * 8], in_max=m8x[:],
                                in_values=src[:, 0:W_]).then_inc(ext.h, 1)
                    v.wait_ge(ext.h, eb + 2)
                    if r < 3:
                        v.match_replace(out=dst[:, 0:W_], in_to_replace=m8x[:],
                                        in_values=src[:, 0:W_], imm_value=NEG) \
                            .then_inc(ext.h, 1)
                        v.wait_ge(ext.h, eb + 3)
                ext.n += 3 if r < 3 else 2

            if tile0:
                @b.block.vector
                def _(v, eb=ext.n):
                    v.tensor_scalar(out=selm2[:], in0=iotaj[:], scalar1=nfin0[:],
                                    scalar2=None, op0=ALU.is_ge).then_inc(ext.h, 1)
                    v.wait_ge(ext.h, eb + 1)
                    v.copy_predicated(out=idxall[:], mask=selm2[:], data=fill64[:]) \
                        .then_inc(ext.h, 1)
                ext.n += 2

            if STG < 4:
                continue
            if DEBUG_TENSORS and ti == 0:
                @b.block.scalar
                def _(s, en=ext.n):
                    s.wait_ge(ext.h, en)
                    s.dma_start(out=b.dram["dbg_idx"], in_=idxall[:]) \
                        .then_inc(dbgs.h, 16)
                dbgs.n += 16
            @b.block.sync
            def _(sync, ti=ti, en=ext.n, ib=idd.n):
                sync.wait_ge(ext.h, en)
                sync.dma_start(out=idx_d, in_=idxall[:]).then_inc(idd.h, 16)
                sync.wait_ge(idd.h, ib + 16)
                for cc in range(32):
                    sync.dma_start(
                        out=idxg[:, cc:cc + 1],
                        in_=idx_d.rearrange("l (j o) -> (l j) o", o=1)[cc * 128:(cc + 1) * 128]) \
                        .then_inc(idd.h, 16)
            idd.n += 16 * 33

            @b.block.gpsimd
            def _(g, ti=ti, idn=idd.n, qp=qmm_prev):
                g.wait_ge(idd.h, idn)
                if ti > 0:
                    g.wait_ge(qmm.h, qp)
                for cc in range(32):
                    g.indirect_dma_start(
                        out=kg[:, cc * 320:(cc + 1) * 320], out_offset=None,
                        in_=kcat,
                        in_offset=bass.IndirectOffsetOnAxis(ap=idxg[:, cc:cc + 1],
                                                            axis=0)) \
                        .then_inc(gpd.h, 16)
            gpd.n += 16 * 32

            if STG < 5:
                continue
            kg3 = kg[:].rearrange("p (cc f) -> p cc f", f=320)

            @b.block.vector
            def _(v, gn=gpd.n, mb=mrt.n):
                v.wait_ge(gpd.h, gn)
                m1v = mtmp1[:].rearrange("p (c i) -> p c i", i=16)
                m2v = mtmp2[:].rearrange("p (c i) -> p c i", i=16)
                r1v = r1c[:].rearrange("p (c i) -> p c i", i=16)
                crv = crot_rep[:].rearrange("p (c i) -> p c i", i=16)
                srv = srot_rep[:].rearrange("p (c i) -> p c i", i=16)
                v.tensor_tensor(out=m1v, in0=kg3[:, :, 256:272], in1=crv, op=ALU.mult)
                v.tensor_tensor(out=m2v, in0=kg3[:, :, 288:304], in1=srv, op=ALU.mult)
                v.tensor_tensor(out=r1v, in0=m1v, in1=m2v, op=ALU.subtract)
                v.tensor_tensor(out=m1v, in0=kg3[:, :, 256:272], in1=srv, op=ALU.mult)
                v.tensor_tensor(out=m2v, in0=kg3[:, :, 288:304], in1=crv,
                                op=ALU.mult).then_inc(mrt.h, 1)
                v.wait_ge(mrt.h, mb + 1)
                v.tensor_tensor(out=kg3[:, :, 288:304], in0=m1v, in1=m2v, op=ALU.add)
                v.tensor_copy(kg3[:, :, 256:272], r1v).then_inc(mrt.h, 1)
            mrt.n += 2

            if STG < 6:
                continue
            for cc in range(32):
                @b.block.tensor
                def _(t, cc=cc, mn=mrt.n, tn=tev.n):
                    t.wait_ge(mrt.h, mn)
                    if tn >= 4:
                        t.wait_ge(tev.h, tn)
                    t.transpose(ps_t1[:], kg[:, cc * 320: cc * 320 + 128],
                                identb[:]).then_inc(qmm.h, 1)
                    t.transpose(ps_t2[:], kg[:, cc * 320 + 128: cc * 320 + 256],
                                identb[:]).then_inc(qmm.h, 1)
                qmm.n += 2

                @b.block.vector
                def _(v, cc=cc, qn=qmm.n):
                    v.wait_ge(qmm.h, qn)
                    v.tensor_copy(kgt1[:, cc * 128:(cc + 1) * 128], ps_t1[:]) \
                        .then_inc(tev.h, 1)
                    v.tensor_copy(kgt2[:, cc * 128:(cc + 1) * 128], ps_t2[:]) \
                        .then_inc(tev.h, 1)
                tev.n += 2

                @b.block.tensor
                def _(t, cc=cc, tn=tev.n):
                    t.wait_ge(tev.h, tn)
                    t.transpose(ps_t1[0:64, :], kg[:, cc * 320 + 256:(cc + 1) * 320],
                                identb[:]).then_inc(qmm.h, 1)
                qmm.n += 1

                @b.block.vector
                def _(v, cc=cc, qn=qmm.n):
                    v.wait_ge(qmm.h, qn)
                    v.tensor_copy(krotP[0:16, cc * 128:(cc + 1) * 128],
                                  ps_t1[0:16, :]).then_inc(tev.h, 1)
                    v.tensor_copy(krotP[32:48, cc * 128:(cc + 1) * 128],
                                  ps_t1[32:48, :]).then_inc(tev.h, 1)
                tev.n += 2

            if STG < 6.5:
                continue
            for m in range(8):
                formA = tile0 and m < 4
                wlo = 0 if formA else t0 + GL * m - 63
                wwid = 64 if formA else 80

                @b.block.sync
                def _(sync, m=m, wlo=wlo, wwid=wwid, qp=qmm.n, bn=bnd.n):
                    if bn > 0:
                        sync.wait_ge(qmm.h, qp)
                    sync.dma_start(out=band_sb[0:wwid, :],
                                   in_=kcat[wlo:wlo + wwid, 0:256]) \
                        .then_inc(bnd.h, 16)
                bnd.n += 16

                wrel = wlo - (max(t0 - 63, 0))
                if STG < 6.9:
                    continue

                @b.block.tensor
                def _(t, m=m, ti=ti, formA=formA, wrel=wrel, wwid=wwid,
                      tn=tev.n, sn=smx.n, bz=bnd.n):
                    t.wait_ge(tev.h, tn)
                    t.wait_ge(bnd.h, bz)
                    if sn > 0:
                        t.wait_ge(smx.h, sn)
                    for mc in range(2):
                        t.matmul(ps_sc[:],
                                 qabs[:, mc * 1024 + m * 128: mc * 1024 + (m + 1) * 128],
                                 (kgt1 if mc == 0 else kgt2)[:, m * 512:(m + 1) * 512],
                                 start=(mc == 0), stop=False)
                    i = t.matmul(ps_sc[:], qfPa[:, m * 128:(m + 1) * 128],
                                 krotP[:, m * 512:(m + 1) * 512], start=False,
                                 stop=True)
                    i.then_inc(qmm.h, 1)
                    for mc in range(2):
                        t.matmul(ps_bd[:, 0:wwid],
                                 qabs[:, mc * 1024 + m * 128: mc * 1024 + (m + 1) * 128],
                                 ckvb[:, mc * 256 + wrel: mc * 256 + wrel + wwid],
                                 start=(mc == 0), stop=False)
                    if formA:
                        i = t.matmul(ps_bd[:, 0:wwid], qfPa[:, m * 128:(m + 1) * 128],
                                     gkaP[:, :], start=False, stop=True)
                    else:
                        i = t.matmul(ps_bd[:, 0:wwid], qfPb[:, m * 128:(m + 1) * 128],
                                     gkwP[:, wrel:wrel + wwid], start=False, stop=True)
                    i.then_inc(qmm.h, 1)
                qmm.n += 2

                if STG < 7.3:
                    continue

                @b.block.vector
                def _(v, m=m, formA=formA, wwid=wwid, qn=qmm.n, sb_=smx.n):
                    v.wait_ge(qmm.h, qn)
                    v.tensor_tensor(out=sdyn[:], in0=ps_sc[:], in1=maskd[:],
                                    op=ALU.add)
                    if formA:
                        v.tensor_copy(sband[:, 0:wwid], ps_bd[:, 0:wwid])
                    else:
                        v.tensor_tensor(out=sband[:, 0:wwid], in0=ps_bd[:, 0:wwid],
                                        in1=maskb[:], op=ALU.add)
                    v.tensor_reduce(out=mrow[:], in_=sdyn[:], axis=AX.X, op=ALU.max)
                    v.tensor_reduce(out=mrow2[:], in_=sband[:, 0:wwid], axis=AX.X,
                                    op=ALU.max).then_inc(smx.h, 1)
                    v.wait_ge(smx.h, sb_ + 1)
                    v.tensor_tensor(out=mrow[:], in0=mrow[:], in1=mrow2[:],
                                    op=ALU.max)
                    v.tensor_scalar(out=negm[:], in0=mrow[:], scalar1=-1.0,
                                    scalar2=None, op0=ALU.mult).then_inc(smx.h, 1)
                smx.n += 2

                if STG < 7.6:
                    continue

                @b.block.scalar
                def _(s, m=m, wwid=wwid, sn=smx.n, xb=sxe.n, tn=tev.n):
                    s.wait_ge(smx.h, sn)
                    if xb > 0:
                        s.wait_ge(tev.h, tn)
                    s.activation(out=pd_sb[:], in_=sdyn[:], func=AF.Exp,
                                 bias=negm[:], accum_out=z1[:]).then_inc(sxe.h, 1)
                    s.wait_ge(sxe.h, xb + 1)
                    s.activation(out=pb_sb[:, 0:wwid], in_=sband[:, 0:wwid],
                                 func=AF.Exp, bias=negm[:], accum_out=z2[:]) \
                        .then_inc(sxe.h, 1)
                sxe.n += 2

                if STG < 7.8:
                    continue

                @b.block.vector
                def _(v, m=m, wwid=wwid, xn_=sxe.n, sb_=smx.n):
                    v.wait_ge(sxe.h, xn_)
                    v.tensor_tensor(out=zsum[:], in0=z1[:], in1=z2[:], op=ALU.add) \
                        .then_inc(smx.h, 1)
                    v.wait_ge(smx.h, sb_ + 1)
                    v.reciprocal(out=zinv[:], in_=zsum[:]).then_inc(smx.h, 1)
                    v.wait_ge(smx.h, sb_ + 2)
                    v.tensor_scalar(out=pd_sb[:], in0=pd_sb[:], scalar1=zinv[:],
                                    scalar2=None, op0=ALU.mult)
                    v.tensor_scalar(out=pb_sb[:, 0:wwid], in0=pb_sb[:, 0:wwid],
                                    scalar1=zinv[:], scalar2=None, op0=ALU.mult) \
                        .then_inc(smx.h, 1)
                smx.n += 3

                if DEBUG_TENSORS and ti == 0:
                    @b.block.scalar
                    def _(s, m=m, sn=smx.n):
                        s.wait_ge(smx.h, sn)
                        s.dma_start(out=b.dram["dbg_sd"][m * 128:(m + 1) * 128, :],
                                    in_=sdyn[:]).then_inc(dbgs.h, 16)
                        s.dma_start(out=b.dram["dbg_sb"][m * 128:(m + 1) * 128, :],
                                    in_=sband[:]).then_inc(dbgs.h, 16)
                    dbgs.n += 32

                    @b.block.vector
                    def _(v, dn=dbgs.n):
                        v.wait_ge(dbgs.h, dn)

                if STG < 8:
                    continue
                for pc in range(4):
                    @b.block.tensor
                    def _(t, m=m, pc=pc, sn=smx.n, tn=tev.n):
                        t.wait_ge(smx.h, sn)
                        if tn >= 2:
                            t.wait_ge(tev.h, tn - 1)
                        pst = ps_t1 if pc % 2 == 0 else ps_t2
                        t.transpose(pst[:], pd_sb[:, pc * 128:(pc + 1) * 128],
                                    identb[:]).then_inc(qmm.h, 1)
                    qmm.n += 1

                    @b.block.vector
                    def _(v, m=m, pc=pc, qn=qmm.n):
                        v.wait_ge(qmm.h, qn)
                        pst = ps_t1 if pc % 2 == 0 else ps_t2
                        v.tensor_copy(pt_sb[:, pc * 128:(pc + 1) * 128], pst[:]) \
                            .then_inc(tev.h, 1)
                    tev.n += 1

                @b.block.tensor
                def _(t, m=m, wwid=wwid, sn=smx.n, tn=tev.n):
                    t.wait_ge(smx.h, sn)
                    t.wait_ge(tev.h, tn - 1)
                    t.transpose(ps_t1[0:wwid, :], pb_sb[:, 0:wwid], identb[:]) \
                        .then_inc(qmm.h, 1)
                qmm.n += 1

                @b.block.vector
                def _(v, m=m, wwid=wwid, qn=qmm.n):
                    v.wait_ge(qmm.h, qn)
                    v.tensor_copy(ptb[0:wwid, :], ps_t1[0:wwid, :]).then_inc(tev.h, 1)
                tev.n += 1

                @b.block.tensor
                def _(t, m=m, wwid=wwid, tn=tev.n, an=ave.n, bn=bnd.n):
                    t.wait_ge(tev.h, tn)
                    t.wait_ge(bnd.h, bn)
                    if an > 0:
                        t.wait_ge(ave.h, an)
                    first = True
                    for mc in range(2):
                        for cc4 in range(4):
                            t.matmul(ps_av[:, mc * 128:(mc + 1) * 128],
                                     kg[:, (4 * m + cc4) * 320 + mc * 128:
                                        (4 * m + cc4) * 320 + (mc + 1) * 128],
                                     pt_sb[:, cc4 * 128:(cc4 + 1) * 128],
                                     start=(cc4 == 0), stop=False)
                        i = t.matmul(ps_av[:, mc * 128:(mc + 1) * 128],
                                     band_sb[0:wwid, mc * 128:(mc + 1) * 128],
                                     ptb[0:wwid, :], start=False, stop=True)
                    i.then_inc(qmm.h, 1)
                qmm.n += 1

                @b.block.vector
                def _(v, m=m, qn=qmm.n):
                    v.wait_ge(qmm.h, qn)
                    v.tensor_copy(oc_sb[:, 0 * 1024 + m * 128: 0 * 1024 + (m + 1) * 128],
                                  ps_av[:, 0:128])
                    v.tensor_copy(oc_sb[:, 1 * 1024 + m * 128: 1 * 1024 + (m + 1) * 128],
                                  ps_av[:, 128:256]).then_inc(ave.h, 1)
                ave.n += 1

            if STG < 9:
                continue
            for md in range(8):
                @b.block.sync
                def _(sync, md=md, ti=ti, t0=t0, cn=cmb.n):
                    if cmb.n > 0:
                        sync.wait_ge(cmb.h, cn)
                    sync.dma_start(out=xg_sb[:, 0:128],
                                   in_=xT_r[md][:, t0:t0 + 128]).then_inc(xst.h, 16)
                    sync.dma_start(out=xg_sb[:, 128:256],
                                   in_=g0_r[md][:, t0:t0 + 128]).then_inc(xst.h, 16)
                xst.n += 32

                @b.block.tensor
                def _(t, md=md, an=ave.n, cn=cmb.n):
                    t.wait_ge(ave.h, an)
                    if cn > 0:
                        t.wait_ge(cmb.h, cn)
                    first = True
                    for h in range(8):
                        for mc in range(2):
                            rhs = oc_sb[:, mc * 1024:(mc + 1) * 1024] \
                                .rearrange("p (m x) -> p m x", x=128)[:, :, h * GL:(h + 1) * GL]
                            i = t.matmul(
                                ps_ep[:],
                                wcat_sb[:, (h * 2 + mc) * 1024 + md * 128:
                                        (h * 2 + mc) * 1024 + (md + 1) * 128],
                                rhs, start=first, stop=(h == 7 and mc == 1))
                            first = False
                    i.then_inc(qmm.h, 1)
                qmm.n += 1

                @b.block.vector
                def _(v, md=md, qn=qmm.n, xn_=xst.n, cb=cmb.n):
                    v.wait_ge(qmm.h, qn)
                    v.wait_ge(xst.h, xn_)
                    v.tensor_tensor(out=x1o[:], in0=ps_ep[:], in1=xg_sb[:, 128:256],
                                    op=ALU.mult).then_inc(cmb.h, 1)
                    v.wait_ge(cmb.h, cb + 1)
                    v.tensor_tensor(out=x1o[:], in0=x1o[:], in1=xg_sb[:, 0:128],
                                    op=ALU.add).then_inc(cmb.h, 1)
                cmb.n += 2

                @b.block.sync
                def _(sync, md=md, t0=t0, cn=cmb.n):
                    sync.wait_ge(cmb.h, cn)
                    sync.dma_start(out=x1_r[md][:, t0:t0 + 128], in_=x1o[:]) \
                        .then_inc(xst.h, 16)
                xst.n += 16

        @b.block.sync
        def _(sync, a=idd.n, b2=bnd.n, c2=ldn.n, d2=xst.n, e2=ldp.n):
            for s_, v_ in ((idd, a), (bnd, b2), (ldn, c2), (xst, d2), (ldp, e2)):
                if v_ > 0:
                    sync.wait_ge(s_.h, v_)

        @b.block.gpsimd
        def _(g, a=gpd.n):
            if a > 0:
                g.wait_ge(gpd.h, a)

        @b.block.tensor
        def _(t, a=qmm.n):
            if a > 0:
                t.wait_ge(qmm.h, a)

        @b.block.scalar
        def _(s, a=sxe.n, b2=sfp.n):
            for s_, v_ in ((sxe, a), (sfp, b2)):
                if v_ > 0:
                    s.wait_ge(s_.h, v_)

        @b.block.vector
        def _(v, vals=(ext.n, tev.n, smx.n, ave.n, iev.n, mrt.n, cmb.n, qev.n)):
            for s_, v_ in zip((ext, tev, smx, ave, iev, mrt, cmb, qev), vals):
                if v_ > 0:
                    v.wait_ge(s_.h, v_)



def emit_l1att(b: B):
    nc = b.nc
    xn1T = b.dram["xn1T"]
    s1T = b.dram["s1T"]
    wqkvT = b.dram["wqkvT"]
    proj = b.dram["proj"]
    wout1T = b.dram["wout1T"]
    l1h, l1xm = b.dram["l1h"], b.dram["l1xm"]
    H, F, HD = 16, 256, 64

    xn_c = xn1T.rearrange("(a p) (c n) -> c a p n", p=128, n=NT)
    s1_c = s1T.rearrange("(m p) (c n) -> c m p n", p=128, n=NT)
    xm_c = l1xm.rearrange("(a p) (c n) -> c a p n", p=128, n=NT)
    h_d = l1h.rearrange("(i p) (c n) -> i c p n", p=128, n=NT)

    proj_sb = b.sb("aProj", [64, H * F], BF16)
    kvall = b.sb("aKval", [128, H * 2 * 65], BF16)
    onesC = b.sb("aOnes", [128, 1], BF16)
    onesc64 = b.sb("aOn64", [1, 64])
    gvec = b.sb("aGv", [128, 8])
    bvec = b.sb("aBv", [128, 8])
    bovec = b.sb("aBo", [128, 8])
    ldw = b.sem("aldw")
    pre = b.sem("apre")

    @b.block.vector
    def _(v):
        v.memset(onesC[:], 1.0)
        v.memset(onesc64[:], 1.0)

    @b.block.scalar
    def _(s):
        s.dma_start(out=gvec[:], in_=b.dram["gamma1r"]).then_inc(ldw.h, 16)
        s.dma_start(out=bvec[:], in_=b.dram["beta1r"]).then_inc(ldw.h, 16)
        s.dma_start(out=bovec[:], in_=b.dram["b_outr"]).then_inc(ldw.h, 16)
    ldw.n += 48

    @b.block.gpsimd
    def _(g):
        for h in range(H):
            g.dma_start(out=proj_sb[:, h * F:(h + 1) * F], in_=proj[h]) \
                .then_inc(pre.h, 16)
    pre.n += 16 * H

    with ExitStack() as ctx1:
        wkv_bf = ctx1.enter_context(nc.sbuf_tensor("aWkv", [128, 8 * 2048], BF16))
        cvs = b.sem("acvs")
        with ExitStack() as ctxs:
            stg = ctxs.enter_context(nc.sbuf_tensor("aStg", [128, 4 * 2048], F32))
            lds = b.sem("alds")
            for half in range(2):
                @b.block.sync
                def _(sync, half=half, cn=cvs.n):
                    if half > 0:
                        sync.wait_ge(cvs.h, cn)
                    for a in range(4):
                        aa = half * 4 + a
                        sync.dma_start(out=stg[:, a * 2048:(a + 1) * 2048],
                                       in_=wqkvT[aa * 128:(aa + 1) * 128, 1024:3072]) \
                            .then_inc(lds.h, 16)
                lds.n += 64

                @b.block.vector
                def _(v, half=half, ln=lds.n):
                    v.wait_ge(lds.h, ln)
                    v.tensor_copy(wkv_bf[:, half * 4 * 2048:(half + 1) * 4 * 2048],
                                  stg[:]).then_inc(cvs.h, 1)
                cvs.n += 1
        xn_sb = ctx1.enter_context(nc.sbuf_tensor("aXn", [128, 8 * T], BF16))
        kT_h = ctx1.enter_context(nc.sbuf_tensor("aKTh", [64, T], BF16))
        v_h = ctx1.enter_context(nc.sbuf_tensor("aVh", [128, 32 * 64], BF16))
        kf_h = ctx1.enter_context(nc.sbuf_tensor("aKfh", [128, 32 * F], BF16))
        ps_k = ctx1.enter_context(nc.psum_tensor("aPk", [64, NT], F32))
        ps_v = [ctx1.enter_context(nc.psum_tensor(f"aPv{j}", [128, 64], F32))
                for j in range(2)]
        ps_kf = [ctx1.enter_context(nc.psum_tensor(f"aPkf{j}", [128, F], F32))
                 for j in range(2)]
        ps_kv = ctx1.enter_context(nc.psum_tensor("aPkv", [128, 2 * 65], F32))
        ldx = b.sem("aldx")
        kmm = b.sem("akmm")
        kev = b.sem("akev")
        vmm = b.sem("avmm")
        vev = b.sem("avev")
        fmm = b.sem("afmm")
        fel = b.sem("afel")
        kvm = b.sem("akvm")
        kve = b.sem("akve")

        @b.block.sync
        def _(sync):
            sync.wait_ge(cvs.h, 2)
            for a in range(8):
                for c in range(NCH):
                    sync.dma_start(out=xn_sb[:, a * T + c * NT: a * T + (c + 1) * NT],
                                   in_=xn_c[c, a]).then_inc(ldx.h, 16)
        ldx.n += 16 * 64

        for h in range(H):
            kcol = h * 64
            vcol = 1024 + h * 64
            for n in range(NCH):
                @b.block.tensor
                def _(t, h=h, n=n, kevn=kev.n):
                    if h == 0 and n == 0:
                        t.wait_ge(ldx.h, ldx.n)
                        t.wait_ge(cvs.h, 2)
                        t.wait_ge(pre.h, 16 * H)
                    if h + n > 0:
                        t.wait_ge(kev.h, kevn)
                    for a in range(8):
                        i = t.matmul(ps_k[:],
                                     wkv_bf[:, a * 2048 + kcol: a * 2048 + kcol + 64],
                                     xn_sb[:, a * T + n * NT: a * T + (n + 1) * NT],
                                     start=(a == 0), stop=(a == 7))
                    i.then_inc(kmm.h, 1)
                kmm.n += 1

                @b.block.vector
                def _(v, h=h, n=n, kn=kmm.n):
                    v.wait_ge(kmm.h, kn)
                    v.tensor_copy(kT_h[:, n * NT:(n + 1) * NT], ps_k[:]) \
                        .then_inc(kev.h, 1)
                kev.n += 1

            for tm in range(32):
                @b.block.tensor
                def _(t, h=h, tm=tm, vevn=vev.n):
                    if vevn >= 2:
                        t.wait_ge(vev.h, vevn - 1)
                    for a in range(8):
                        i = t.matmul(ps_v[tm % 2][:],
                                     xn_sb[:, a * T + tm * 128: a * T + (tm + 1) * 128],
                                     wkv_bf[:, a * 2048 + vcol: a * 2048 + vcol + 64],
                                     start=(a == 0), stop=(a == 7))
                    i.then_inc(vmm.h, 1)
                vmm.n += 1

                @b.block.vector
                def _(v, h=h, tm=tm, vn=vmm.n):
                    v.wait_ge(vmm.h, vn)
                    v.tensor_copy(v_h[:, tm * 64:(tm + 1) * 64], ps_v[tm % 2][:]) \
                        .then_inc(vev.h, 1)
                vev.n += 1

            for tm in range(32):
                @b.block.tensor
                def _(t, h=h, tm=tm, kevn=kev.n, feln=fel.n):
                    t.wait_ge(kev.h, kevn)
                    if feln >= 6:
                        t.wait_ge(fel.h, feln - 3)
                    i = t.matmul(ps_kf[tm % 2][:],
                                 kT_h[:, tm * 128:(tm + 1) * 128],
                                 proj_sb[:, h * F:(h + 1) * F],
                                 start=True, stop=True)
                    i.then_inc(fmm.h, 1)
                fmm.n += 1

                @b.block.vector
                def _(v, h=h, tm=tm, fn=fmm.n, fb=fel.n):
                    v.wait_ge(fmm.h, fn)
                    v.tensor_scalar(out=kf_h[:, tm * F:(tm + 1) * F],
                                    in0=ps_kf[tm % 2][:], scalar1=0.0, scalar2=None,
                                    op0=ALU.min).then_inc(fel.h, 1)
                fel.n += 1

                @b.block.scalar
                def _(s, h=h, tm=tm, fb=fel.n):
                    s.wait_ge(fel.h, fb)
                    s.activation(out=kf_h[:, tm * F:(tm + 1) * F],
                                 in_=kf_h[:, tm * F:(tm + 1) * F], func=AF.Exp) \
                        .then_inc(fel.h, 1)
                fel.n += 1

                @b.block.vector
                def _(v, h=h, tm=tm, fb=fel.n, fn=fmm.n):
                    v.wait_ge(fel.h, fb)
                    i = v.scalar_tensor_tensor(
                        out=kf_h[:, tm * F:(tm + 1) * F], in0=ps_kf[tm % 2][:],
                        scalar=0.0, in1=kf_h[:, tm * F:(tm + 1) * F],
                        op0=ALU.max, op1=ALU.add)
                    i.then_inc(fel.h, 1)
                fel.n += 1

            for fm in range(2):
                @b.block.tensor
                def _(t, h=h, fm=fm, feln=fel.n, kven=kve.n, vevn=vev.n):
                    t.wait_ge(fel.h, feln)
                    t.wait_ge(vev.h, vevn)
                    if h + fm > 0:
                        t.wait_ge(kve.h, kven)
                    for tm in range(32):
                        t.matmul(ps_kv[:, fm * 65: fm * 65 + 64],
                                 kf_h[:, tm * F + fm * 128: tm * F + fm * 128 + 128],
                                 v_h[:, tm * 64:(tm + 1) * 64],
                                 start=(tm == 0), stop=(tm == 31))
                    for tm in range(32):
                        i = t.matmul(ps_kv[:, fm * 65 + 64: fm * 65 + 65],
                                     kf_h[:, tm * F + fm * 128: tm * F + fm * 128 + 128],
                                     onesC[:],
                                     start=(tm == 0), stop=(tm == 31))
                    i.then_inc(kvm.h, 1)
                kvm.n += 1

                @b.block.vector
                def _(v, h=h, fm=fm, kn=kvm.n):
                    v.wait_ge(kvm.h, kn)
                    v.tensor_copy(kvall[:, (h * 2 + fm) * 65:(h * 2 + fm + 1) * 65],
                                  ps_kv[:, fm * 65:(fm + 1) * 65]).then_inc(kve.h, 1)
                kve.n += 1


        if "dbg_kf" in DEBUG_TENSORS:
            dbk = b.sem("adbk")

            @b.block.scalar
            def _(s, kn=kve.n):
                s.wait_ge(kve.h, 32)
                s.dma_start(out=b.dram["dbg_kf"], in_=kf_h[:]).then_inc(dbk.h, 16)
                s.dma_start(out=b.dram["dbg_vh"], in_=v_h[:]).then_inc(dbk.h, 16)
                s.dma_start(out=b.dram["dbg_kt"], in_=kT_h[:]).then_inc(dbk.h, 16)
                s.wait_ge(dbk.h, 48)

    b.barrier()

    ctx23 = ExitStack()
    attn_all = ctx23.enter_context(nc.sbuf_tensor("aAttn", [128, 8 * T], BF16))
    wq_bf = ctx23.enter_context(nc.sbuf_tensor("aWq", [128, 8 * 1024], BF16))
    cvq = b.sem("acvq")
    with ExitStack() as ctxs2:
        stgq = ctxs2.enter_context(nc.sbuf_tensor("aStgq", [128, 4 * 1024], F32))
        ldsq = b.sem("aldsq")
        for half in range(2):
            @b.block.sync
            def _(sync, half=half, cn=cvq.n):
                if half > 0:
                    sync.wait_ge(cvq.h, cn)
                for a in range(4):
                    aa = half * 4 + a
                    sync.dma_start(out=stgq[:, a * 1024:(a + 1) * 1024],
                                   in_=wqkvT[aa * 128:(aa + 1) * 128, 0:1024]) \
                        .then_inc(ldsq.h, 16)
            ldsq.n += 64

            @b.block.vector
            def _(v, half=half, ln=ldsq.n):
                v.wait_ge(ldsq.h, ln)
                v.tensor_copy(wq_bf[:, half * 4 * 1024:(half + 1) * 4 * 1024],
                              stgq[:]).then_inc(cvq.h, 1)
            cvq.n += 1
    with ExitStack() as ctx2:
        xn_sb2 = ctx2.enter_context(nc.sbuf_tensor("aXn2", [128, 8 * T], BF16))
        qT_h = ctx2.enter_context(nc.sbuf_tensor("aQTh", [64, T], BF16))
        qf_h = ctx2.enter_context(nc.sbuf_tensor("aQfh", [128, 2 * T], BF16))
        z_sb = ctx2.enter_context(nc.sbuf_tensor("aZsb", [64, NT], F32))
        zrow = ctx2.enter_context(nc.sbuf_tensor("aZr", [1, NT], F32))
        ps_q = ctx2.enter_context(nc.psum_tensor("aPq", [64, NT], F32))
        ps_qf = [ctx2.enter_context(nc.psum_tensor(f"aPqf{j}", [128, NT], F32))
                 for j in range(2)]
        ps_at = ctx2.enter_context(nc.psum_tensor("aPat", [65, NT], F32))
        ps_z = ctx2.enter_context(nc.psum_tensor("aPz", [64, NT], F32))
        ldx2 = b.sem("aldx2")
        qmm = b.sem("aqmm")
        qev = b.sem("aqev")
        qfm = b.sem("aqfm")
        qfe = b.sem("aqfe")
        amm = b.sem("aamm")
        zst = b.sem("azst")
        zbc = b.sem("azbc")
        aev = b.sem("aaev")

        @b.block.sync
        def _(sync):
            sync.wait_ge(cvq.h, 2)
            for a in range(8):
                for c in range(NCH):
                    sync.dma_start(out=xn_sb2[:, a * T + c * NT: a * T + (c + 1) * NT],
                                   in_=xn_c[c, a]).then_inc(ldx2.h, 16)
        ldx2.n += 16 * 64

        for h in range(H):
            qcol = h * 64
            for n in range(NCH):
                @b.block.tensor
                def _(t, h=h, n=n, qevn=qev.n):
                    if h == 0 and n == 0:
                        t.wait_ge(ldx2.h, ldx2.n)
                        t.wait_ge(cvq.h, 2)
                    if h + n > 0:
                        t.wait_ge(qev.h, qevn)
                    for a in range(8):
                        i = t.matmul(ps_q[:],
                                     wq_bf[:, a * 1024 + qcol: a * 1024 + qcol + 64],
                                     xn_sb2[:, a * T + n * NT: a * T + (n + 1) * NT],
                                     start=(a == 0), stop=(a == 7))
                    i.then_inc(qmm.h, 1)
                qmm.n += 1

                @b.block.vector
                def _(v, h=h, n=n, qn=qmm.n):
                    v.wait_ge(qmm.h, qn)
                    v.tensor_copy(qT_h[:, n * NT:(n + 1) * NT], ps_q[:]) \
                        .then_inc(qev.h, 1)
                qev.n += 1

            for n in range(NCH):
                for fm in range(2):
                    j = n * 2 + fm

                    @b.block.tensor
                    def _(t, h=h, n=n, fm=fm, j=j, qevn=qev.n, qfen=qfe.n):
                        t.wait_ge(qev.h, qevn)
                        if qfen >= 6:
                            t.wait_ge(qfe.h, qfen - 3)
                        i = t.matmul(ps_qf[j % 2][:],
                                     proj_sb[:, h * F + fm * 128: h * F + fm * 128 + 128],
                                     qT_h[:, n * NT:(n + 1) * NT],
                                     start=True, stop=True)
                        i.then_inc(qfm.h, 1)
                    qfm.n += 1

                    @b.block.vector
                    def _(v, h=h, n=n, fm=fm, j=j, qn=qfm.n, qb=qfe.n):
                        v.wait_ge(qfm.h, qn)
                        sl = slice(fm * T + n * NT, fm * T + (n + 1) * NT)
                        v.tensor_scalar(out=qf_h[:, sl], in0=ps_qf[j % 2][:],
                                        scalar1=0.0, scalar2=None, op0=ALU.min) \
                            .then_inc(qfe.h, 1)
                    qfe.n += 1

                    @b.block.scalar
                    def _(s, h=h, n=n, fm=fm, qb=qfe.n):
                        s.wait_ge(qfe.h, qb)
                        sl = slice(fm * T + n * NT, fm * T + (n + 1) * NT)
                        s.activation(out=qf_h[:, sl], in_=qf_h[:, sl], func=AF.Exp) \
                            .then_inc(qfe.h, 1)
                    qfe.n += 1

                    @b.block.vector
                    def _(v, h=h, n=n, fm=fm, j=j, qb=qfe.n):
                        v.wait_ge(qfe.h, qb)
                        sl = slice(fm * T + n * NT, fm * T + (n + 1) * NT)
                        i = v.scalar_tensor_tensor(out=qf_h[:, sl], in0=ps_qf[j % 2][:],
                                                   scalar=0.0, in1=qf_h[:, sl],
                                                   op0=ALU.max, op1=ALU.add)
                        i.then_inc(qfe.h, 1)
                    qfe.n += 1

            for n in range(NCH):
                @b.block.tensor
                def _(t, h=h, n=n, qfen=qfe.n, aevn=aev.n):
                    t.wait_ge(qfe.h, qfen)
                    if h + n > 0:
                        t.wait_ge(aev.h, aevn)
                    for fm in range(2):
                        i = t.matmul(ps_at[:],
                                     kvall[:, (h * 2 + fm) * 65:(h * 2 + fm + 1) * 65],
                                     qf_h[:, fm * T + n * NT: fm * T + (n + 1) * NT],
                                     start=(fm == 0), stop=(fm == 1))
                    i.then_inc(amm.h, 1)
                amm.n += 1

                @b.block.vector
                def _(v, h=h, n=n, an=amm.n, zb=zst.n):
                    v.wait_ge(amm.h, an)
                    v.tensor_scalar(out=zrow[:], in0=ps_at[64:65, :], scalar1=1e-8,
                                    scalar2=None, op0=ALU.add).then_inc(zst.h, 1)
                    v.wait_ge(zst.h, zb + 1)
                    v.reciprocal(out=zrow[:], in_=zrow[:]).then_inc(zst.h, 1)
                zst.n += 2

                @b.block.tensor
                def _(t, h=h, n=n, zn=zst.n, aevn=aev.n):
                    t.wait_ge(zst.h, zn)
                    t.matmul(ps_z[:], onesc64[:], zrow[:], start=True, stop=True) \
                        .then_inc(zbc.h, 1)
                zbc.n += 1

                @b.block.vector
                def _(v, h=h, n=n, zn=zbc.n, ab=aev.n):
                    v.wait_ge(zbc.h, zn)
                    v.tensor_copy(z_sb[:], ps_z[:]).then_inc(aev.h, 1)
                    v.wait_ge(aev.h, ab + 1)
                    base = (h % 2) * 64
                    i = v.tensor_tensor(
                        out=attn_all[base:base + 64,
                                     (h // 2) * T + n * NT:(h // 2) * T + (n + 1) * NT],
                        in0=ps_at[0:64, :], in1=z_sb[:], op=ALU.mult)
                    i.then_inc(aev.h, 1)
                aev.n += 2

    b.barrier()

    with ExitStack() as ctx3:
        wo_bf = ctx3.enter_context(nc.sbuf_tensor("aWo", [128, 8 * 1024], BF16))
        stg2 = ctx3.enter_context(nc.sbuf_tensor("aStg2", [128, 2 * 1024], F32))
        xn_n3 = ctx3.enter_context(nc.sbuf_tensor("aXn3", [128, 8 * NT], BF16))
        y_sb = ctx3.enter_context(nc.sbuf_tensor("aY", [128, 8 * NT], F32))
        xm_sb = ctx3.enter_context(nc.sbuf_tensor("aXm", [128, 8 * NT], BF16))
        ps_y = [ctx3.enter_context(nc.psum_tensor(f"aPy{j}", [128, NT], F32))
                for j in range(2)]
        lno = LNAffine(b, "aL1", ctx3, gvec, bvec, 1e-5)
        ldo = b.sem("aldo")
        cvo = b.sem("acvo")
        ldn3 = b.sem("aldn3")
        ymm = b.sem("aymm")
        yev = b.sem("ayev")
        xst = b.sem("axst")

        for part in range(4):
            @b.block.sync
            def _(sync, part=part, cn=cvo.n):
                if part > 0:
                    sync.wait_ge(cvo.h, cn)
                for a in range(2):
                    aa = part * 2 + a
                    sync.dma_start(out=stg2[:, a * 1024:(a + 1) * 1024],
                                   in_=wout1T[aa * 128:(aa + 1) * 128, :]) \
                        .then_inc(ldo.h, 16)
            ldo.n += 32

            @b.block.vector
            def _(v, part=part, ln=ldo.n):
                v.wait_ge(ldo.h, ln)
                v.tensor_copy(wo_bf[:, part * 2048:(part + 1) * 2048], stg2[:]) \
                    .then_inc(cvo.h, 1)
            cvo.n += 1

        for c in range(NCH):
            @b.block.sync
            def _(sync, c=c, yevn=yev.n):
                if c > 0:
                    sync.wait_ge(yev.h, yevn)
                for a in range(8):
                    sync.dma_start(out=xn_n3[:, a * NT:(a + 1) * NT],
                                   in_=xn_c[c, a]).then_inc(ldn3.h, 16)
            ldn3.n += 128

            for m in range(8):
                km = c * 8 + m

                @b.block.tensor
                def _(t, c=c, m=m, km=km, yevn=yev.n):
                    if km == 0:
                        t.wait_ge(cvo.h, 4)
                    if km >= 2:
                        t.wait_ge(yev.h, km - 1)
                    for a in range(8):
                        i = t.matmul(ps_y[m % 2][:],
                                     wo_bf[:, a * 1024 + m * 128: a * 1024 + (m + 1) * 128],
                                     attn_all[:, a * T + c * NT: a * T + (c + 1) * NT],
                                     start=(a == 0), stop=(a == 7))
                    i.then_inc(ymm.h, 1)
                ymm.n += 1

                @b.block.vector
                def _(v, c=c, m=m, km=km, ldnn=ldn3.n):
                    v.wait_ge(ymm.h, km + 1)
                    v.wait_ge(ldn3.h, ldnn)
                    i = v.scalar_tensor_tensor(
                        out=y_sb[:, m * NT:(m + 1) * NT], in0=ps_y[m % 2][:],
                        scalar=bovec[:, m:m + 1], in1=xn_n3[:, m * NT:(m + 1) * NT],
                        op0=ALU.add, op1=ALU.add)
                    i.then_inc(yev.h, 1)
                yev.n += 1

            nrm = lno.emit_chunk(yev, yev.n, y_sb, xm_sb, xst, xst.n)

            @b.block.sync
            def _(sync, c=c, nn=nrm.n):
                sync.wait_ge(nrm.h, nn)
                for a in range(8):
                    sync.dma_start(out=xm_c[c, a], in_=xm_sb[:, a * NT:(a + 1) * NT]) \
                        .then_inc(xst.h, 16)
            xst.n += 128

        @b.block.sync
        def _(sync, xn=xst.n):
            sync.wait_ge(xst.h, xn)

    if "dbg_attn" in DEBUG_TENSORS:
        dba = b.sem("adba")

        @b.block.scalar
        def _(s):
            for a in range(8):
                for c2 in range(NCH):
                    s.dma_start(
                        out=b.dram["dbg_attn"].rearrange(
                            "(a p) t -> a p t", p=128)[a][:, c2 * NT:(c2 + 1) * NT],
                        in_=attn_all[:, a * T + c2 * NT: a * T + (c2 + 1) * NT]) \
                        .then_inc(dba.h, 16)
            s.dma_start(out=b.dram["dbg_kv"], in_=kvall[:]).then_inc(dba.h, 16)
            s.wait_ge(dba.h, 16 * 65)
        dba.n += 16 * 65

    ctx23.close()
    b.barrier()

    with ExitStack() as ctx4:
        w1_bf = ctx4.enter_context(nc.sbuf_tensor("aW1", [128, 32 * 1024], BF16))
        stg4 = ctx4.enter_context(nc.sbuf_tensor("aStg4", [128, 4096], F32))
        xm_n = [ctx4.enter_context(nc.sbuf_tensor(f"aXmn{j}", [128, 8 * NT], BF16))
                for j in range(2)]
        h_sb = ctx4.enter_context(nc.sbuf_tensor("aHsb", [128, NT], BF16))
        b1vec = ctx4.enter_context(nc.sbuf_tensor("aB1v", [128, 32], F32))
        ps_h = [ctx4.enter_context(nc.psum_tensor(f"aPh{j}", [128, NT], F32))
                for j in range(2)]
        ldw4 = b.sem("aldw4")
        cvw4 = b.sem("acvw4")
        ldm = b.sem("aldm")
        hmm_ = b.sem("ahmm")
        hge = b.sem("ahge")
        hst = b.sem("ahst")

        @b.block.scalar
        def _(s):
            s.dma_start(out=b1vec[:], in_=b.dram["b1ar"]).then_inc(ldw4.h, 16)
        ldw4.n += 16

        for part in range(8):
            @b.block.sync
            def _(sync, part=part, cn=cvw4.n):
                if part > 0:
                    sync.wait_ge(cvw4.h, cn)
                sync.dma_start(out=stg4[:],
                               in_=b.dram["w1aT"][part * 128:(part + 1) * 128, :]) \
                    .then_inc(ldw4.h, 16)
            ldw4.n += 16

            @b.block.vector
            def _(v, part=part, ln=ldw4.n):
                v.wait_ge(ldw4.h, ln)
                for i in range(32):
                    ii = v.tensor_copy(
                        w1_bf[:, i * 1024 + part * 128: i * 1024 + (part + 1) * 128],
                        stg4[:, i * 128:(i + 1) * 128])
                ii.then_inc(cvw4.h, 1)
            cvw4.n += 1

        for n in range(NCH):
            @b.block.sync
            def _(sync, n=n, hn=hmm_.n):
                if n >= 2:
                    sync.wait_ge(hmm_.h, hn - 32)
                for a in range(8):
                    sync.dma_start(out=xm_n[n % 2][:, a * NT:(a + 1) * NT],
                                   in_=xm_c[n, a]).then_inc(ldm.h, 16)
            ldm.n += 128

            for i in range(32):
                ki = n * 32 + i

                @b.block.tensor
                def _(t, n=n, i=i, ki=ki, ldmn=ldm.n, hgen=hge.n):
                    if i == 0:
                        t.wait_ge(ldm.h, ldmn)
                    if ki == 0:
                        t.wait_ge(cvw4.h, 8)
                    if ki >= 2:
                        t.wait_ge(hge.h, ki - 1)
                    for a in range(8):
                        ii = t.matmul(ps_h[ki % 2][:],
                                      w1_bf[:, i * 1024 + a * 128: i * 1024 + (a + 1) * 128],
                                      xm_n[n % 2][:, a * NT:(a + 1) * NT],
                                      start=(a == 0), stop=(a == 7))
                    ii.then_inc(hmm_.h, 1)
                hmm_.n += 1

                @b.block.scalar
                def _(s, n=n, i=i, ki=ki, hsn=hst.n):
                    s.wait_ge(hmm_.h, ki + 1)
                    if ki >= 1:
                        s.wait_ge(hst.h, hsn)
                    s.activation(out=h_sb[:], in_=ps_h[ki % 2][:], func=AF.Gelu,
                                 bias=b1vec[:, i:i + 1]).then_inc(hge.h, 1)
                hge.n += 1

                @b.block.sync
                def _(sync, n=n, i=i, ki=ki, hgn=hge.n):
                    sync.wait_ge(hge.h, hgn)
                    sync.dma_start(out=h_d[i, n], in_=h_sb[:]).then_inc(hst.h, 16)
                hst.n += 16

        @b.block.sync
        def _(sync, hn=hst.n):
            sync.wait_ge(hst.h, hn)

    b.barrier()

    with ExitStack() as ctx5:
        w2_bf = ctx5.enter_context(nc.sbuf_tensor("aW2", [128, 256 * 128], BF16))
        stg5 = ctx5.enter_context(nc.sbuf_tensor("aStg5", [128, 4 * 1024], F32))
        hload = ctx5.enter_context(nc.sbuf_tensor("aHld", [128, 32 * NT], BF16))
        xm_n5 = ctx5.enter_context(nc.sbuf_tensor("aXm5", [128, 8 * NT], BF16))
        y5 = ctx5.enter_context(nc.sbuf_tensor("aY5", [128, 8 * NT], F32))
        s1_sb = ctx5.enter_context(nc.sbuf_tensor("aS1", [128, 8 * NT], F32))
        b2vec = ctx5.enter_context(nc.sbuf_tensor("aB2v", [128, 8], F32))
        ps_y5 = [ctx5.enter_context(nc.psum_tensor(f"aPy5{j}", [128, NT], F32))
                 for j in range(2)]
        lno5 = LNAffine(b, "aL15", ctx5, gvec, bvec, 1e-5)
        ldw5 = b.sem("aldw5")
        cvw5 = b.sem("acvw5")
        ldh5 = b.sem("aldh5")
        ldm5 = b.sem("aldm5")
        ymm5 = b.sem("aymm5")
        yev5 = b.sem("ayev5")
        sst = b.sem("asst")

        @b.block.scalar
        def _(s):
            s.dma_start(out=b2vec[:], in_=b.dram["b2ar"]).then_inc(ldw5.h, 16)
        ldw5.n += 16

        for part in range(8):
            @b.block.sync
            def _(sync, part=part, cn=cvw5.n):
                if part > 0:
                    sync.wait_ge(cvw5.h, cn)
                for a in range(4):
                    ii = part * 4 + a
                    sync.dma_start(out=stg5[:, a * 1024:(a + 1) * 1024],
                                   in_=b.dram["w2aT"][ii * 128:(ii + 1) * 128, :]) \
                        .then_inc(ldw5.h, 16)
            ldw5.n += 64

            @b.block.vector
            def _(v, part=part, ln=ldw5.n):
                v.wait_ge(ldw5.h, ln)
                for a in range(4):
                    ii = part * 4 + a
                    iv = v.tensor_copy(w2_bf[:, ii * 1024:(ii + 1) * 1024],
                                       stg5[:, a * 1024:(a + 1) * 1024])
                iv.then_inc(cvw5.h, 1)
            cvw5.n += 1

        for c in range(NCH):
            @b.block.sync
            def _(sync, c=c, yevn=yev5.n):
                if c > 0:
                    sync.wait_ge(yev5.h, yevn)
                for i in range(32):
                    sync.dma_start(out=hload[:, i * NT:(i + 1) * NT],
                                   in_=h_d[i, c]).then_inc(ldh5.h, 16)
                for a in range(8):
                    sync.dma_start(out=xm_n5[:, a * NT:(a + 1) * NT],
                                   in_=xm_c[c, a]).then_inc(ldm5.h, 16)
            ldh5.n += 16 * 32
            ldm5.n += 128

            for m in range(8):
                km = c * 8 + m

                @b.block.tensor
                def _(t, c=c, m=m, km=km, ldhn=ldh5.n, yevn=yev5.n):
                    t.wait_ge(ldh5.h, ldhn)
                    if km == 0:
                        t.wait_ge(cvw5.h, 8)
                    if km >= 2:
                        t.wait_ge(yev5.h, km - 1)
                    for i in range(32):
                        ii = t.matmul(ps_y5[m % 2][:],
                                      w2_bf[:, (i * 8 + m) * 128:(i * 8 + m + 1) * 128],
                                      hload[:, i * NT:(i + 1) * NT],
                                      start=(i == 0), stop=(i == 31))
                    ii.then_inc(ymm5.h, 1)
                ymm5.n += 1

                @b.block.vector
                def _(v, c=c, m=m, km=km, ldmn=ldm5.n):
                    v.wait_ge(ymm5.h, km + 1)
                    v.wait_ge(ldm5.h, ldmn)
                    i = v.scalar_tensor_tensor(
                        out=y5[:, m * NT:(m + 1) * NT], in0=ps_y5[m % 2][:],
                        scalar=b2vec[:, m:m + 1], in1=xm_n5[:, m * NT:(m + 1) * NT],
                        op0=ALU.add, op1=ALU.add)
                    i.then_inc(yev5.h, 1)
                yev5.n += 1

            nrm5 = lno5.emit_chunk(yev5, yev5.n, y5, s1_sb, sst, sst.n)

            @b.block.sync
            def _(sync, c=c, nn=nrm5.n):
                sync.wait_ge(nrm5.h, nn)
                for m in range(8):
                    sync.dma_start(out=s1_c[c, m], in_=s1_sb[:, m * NT:(m + 1) * NT]) \
                        .then_inc(sst.h, 16)
            sst.n += 128

        @b.block.sync
        def _(sync, sn=sst.n):
            sync.wait_ge(sst.h, sn)



def emit_l1ffn(b: B):
    nc = b.nc
    x1T, xn1T = b.dram["x1T"], b.dram["xn1T"]
    mu1, rstd1 = b.dram["mu1"], b.dram["rstd1"]
    s1T, outT = b.dram["s1T"], b.dram["outT"]
    eselT, keysH, values = b.dram["eselT"], b.dram["keys"], b.dram["values"]
    hgD = b.dram["l1hg"]

    x1_c = x1T.rearrange("(a p) (c n) -> c a p n", p=128, n=NT)
    xn_c = xn1T.rearrange("(a p) (c n) -> c a p n", p=128, n=NT)
    s1_c = s1T.rearrange("(a p) (c n) -> c a p n", p=128, n=NT)
    out_c = outT.rearrange("(a p) (c n) -> c a p n", p=128, n=NT)
    hg_c = hgD.rearrange("(k p) (c n) -> k c p n", p=128, n=NT)
    NE = 64

    gTbf = b.sb("nGt", [64, T], BF16)
    sel64_holder = ExitStack()
    sel64 = sel64_holder.enter_context(nc.sbuf_tensor("nSel", [64, NE * 128], BF16))
    with ExitStack() as ctxR:
        identf = ctxR.enter_context(nc.sbuf_tensor("nIdf", [128, 128], F32))
        def sbr(n, s, d=F32):
            return ctxR.enter_context(nc.sbuf_tensor("nR" + n, s, d))
        xf = sbr("xf", [128, 8 * NT])
        xnf = sbr("xnf", [128, 8 * NT])
        esel_sb = sbr("esel", [128, 8 * 64])
        murow = sbr("mur", [1, NT])
        rsrow = sbr("rsr", [1, NT])
        onesc = sbr("onc", [1, 128])
        log_sb = sbr("log", [128, 64])
        sel_sb = sbr("sel", [128, 64])
        m8s = sbr("m8", [128, 8])
        mval8 = sbr("mv8", [128, 8])
        knock = sbr("kn", [128, 64])
        gate_tm = sbr("gtm", [128, 64])
        ps_lg = ctxR.enter_context(nc.psum_tensor("nRplg", [128, 64], F32))
        ps_b = ctxR.enter_context(nc.psum_tensor("nRpb", [128, NT], F32))
        ps_m = ctxR.enter_context(nc.psum_tensor("nRpm", [128, NT], F32))
        ps_tr = ctxR.enter_context(nc.psum_tensor("nRptr", [64, 128], F32))
        ld = b.sem("nRld")
        pre = b.sem("nRpre")
        bcn = b.sem("nRbc")
        xnd = b.sem("nRxnd")
        lgm = b.sem("nRlgm")
        ohd = b.sem("nRohd")
        tr2 = b.sem("nRtr2")
        ghe = b.sem("nRghe")
        sgd = b.sem("nRsgd")

        @b.block.gpsimd
        def _(g):
            g.memset(sel64[:], 1.0)
            g.affine_select(out=sel64[:], in_=sel64[:], pattern=[[1, NE], [0, 128]],
                            compare_op=ALU.is_equal, fill=0.0, base=0,
                            channel_multiplier=-1)
            g.memset(identf[:], 1.0)
            emit_identity(b, g, identf, identf)
            g.engine_nop().then_inc(pre.h, 1)
        pre.n += 1

        @b.block.sync
        def _(sync):
            for a in range(8):
                sync.dma_start(out=esel_sb[:, a * 64:(a + 1) * 64],
                               in_=eselT[a * 128:(a + 1) * 128, :]).then_inc(ld.h, 16)
        ld.n += 128

        for c in range(NCH):
            @b.block.sync
            def _(sync, c=c, xnn=xnd.n):
                if c > 0:
                    sync.wait_ge(xnd.h, xnn)
                for a in range(8):
                    sync.dma_start(out=xf[:, a * NT:(a + 1) * NT], in_=x1_c[c, a]) \
                        .then_inc(ld.h, 16)
                sync.dma_start(out=murow[:], in_=mu1[:, c * NT:(c + 1) * NT]) \
                    .then_inc(ld.h, 16)
                sync.dma_start(out=rsrow[:], in_=rstd1[:, c * NT:(c + 1) * NT]) \
                    .then_inc(ld.h, 16)
            ld.n += 160

            @b.block.vector
            def _(v, c=c):
                if c == 0:
                    v.memset(onesc[:], 1.0)

            @b.block.tensor
            def _(t, c=c, ldn=ld.n, xnn=xnd.n):
                t.wait_ge(ld.h, ldn)
                t.wait_ge(pre.h, 1)
                if c > 0:
                    t.wait_ge(xnd.h, xnn)
                t.matmul(ps_b[:], onesc[:], murow[:], start=True, stop=True)
                t.matmul(ps_m[:], onesc[:], rsrow[:], start=True, stop=True) \
                    .then_inc(bcn.h, 1)
            bcn.n += 1

            @b.block.vector
            def _(v, c=c, bb=bcn.n, lgn=lgm.n):
                v.wait_ge(bcn.h, bb)
                if c > 0:
                    v.wait_ge(lgm.h, lgn)
                for a in range(8):
                    sl = slice(a * NT, (a + 1) * NT)
                    v.tensor_tensor(out=xnf[:, sl], in0=xf[:, sl], in1=ps_b[:],
                                    op=ALU.subtract)
                for a in range(8):
                    sl = slice(a * NT, (a + 1) * NT)
                    i = v.tensor_tensor(out=xnf[:, sl], in0=xnf[:, sl], in1=ps_m[:],
                                        op=ALU.mult)
                i.then_inc(xnd.h, 1)
            xnd.n += 1

            for q in range(4 * c, 4 * c + 4):
                qo = (q % 4) * 128

                @b.block.tensor
                def _(t, q=q, qo=qo, ohn=ohd.n, xnn=xnd.n, sgn=sgd.n):
                    t.wait_ge(xnd.h, xnn)
                    if q > 0:
                        t.wait_ge(ohd.h, ohn)
                        t.wait_ge(sgd.h, sgn)
                    for a in range(8):
                        i = t.matmul(ps_lg[:],
                                     xnf[:, a * NT + qo: a * NT + qo + 128],
                                     esel_sb[:, a * 64:(a + 1) * 64],
                                     start=(a == 0), stop=(a == 7))
                    i.then_inc(lgm.h, 1)
                lgm.n += 1

                @b.block.scalar
                def _(s, q=q, ln=lgm.n):
                    s.wait_ge(lgm.h, ln)
                    s.activation(out=sel_sb[:], in_=ps_lg[:], func=AF.Sigmoid) \
                        .then_inc(sgd.h, 1)
                sgd.n += 1

                @b.block.vector
                def _(v, q=q, ln=lgm.n, ohb=ohd.n):
                    v.wait_ge(lgm.h, ln)
                    v.tensor_copy(log_sb[:], ps_lg[:])
                    v.memset(mval8[:], NEG).then_inc(ohd.h, 1)
                    v.wait_ge(ohd.h, ohb + 1)
                    v.max(out=m8s[:], in_=log_sb[:]).then_inc(ohd.h, 1)
                    v.wait_ge(ohd.h, ohb + 2)
                    v.tensor_copy(mval8[:, 0:4], m8s[:, 0:4]).then_inc(ohd.h, 1)
                    v.wait_ge(ohd.h, ohb + 3)
                    v.match_replace(out=knock[:], in_to_replace=mval8[:],
                                    in_values=log_sb[:], imm_value=NEG) \
                        .then_inc(ohd.h, 1)
                    v.wait_ge(ohd.h, ohb + 4)
                    v.tensor_tensor(out=gate_tm[:], in0=log_sb[:], in1=knock[:],
                                    op=ALU.not_equal).then_inc(ohd.h, 1)
                    v.wait_ge(ohd.h, ohb + 5)
                    v.wait_ge(sgd.h, q + 1)
                    i = v.tensor_tensor(out=gate_tm[:], in0=gate_tm[:], in1=sel_sb[:],
                                        op=ALU.mult)
                    i.then_inc(ohd.h, 1)
                ohd.n += 6

                @b.block.tensor
                def _(t, q=q, ohn=ohd.n, ghn=ghe.n):
                    t.wait_ge(ohd.h, ohn)
                    if q > 0:
                        t.wait_ge(ghe.h, ghn)
                    t.transpose(ps_tr[:], gate_tm[:], identf[:]).then_inc(tr2.h, 1)
                tr2.n += 1

                @b.block.vector
                def _(v, q=q, tn=tr2.n):
                    v.wait_ge(tr2.h, tn)
                    v.tensor_copy(gTbf[:, q * 128:(q + 1) * 128], ps_tr[0:64, :]) \
                        .then_inc(ghe.h, 1)
                ghe.n += 1

    b.barrier()

    with ExitStack() as ctx1:
        xn_sb = ctx1.enter_context(nc.sbuf_tensor("nXn", [128, 8 * T], BF16))
        kstg = [ctx1.enter_context(nc.sbuf_tensor(f"nKs{j}", [128, 1024], F32))
                for j in range(2)]
        kbf = [ctx1.enter_context(nc.sbuf_tensor(f"nKb{j}", [128, 1024], BF16))
               for j in range(2)]
        g_sb = ctx1.enter_context(nc.sbuf_tensor("nGsb", [128, NT], BF16))
        hg_sb = ctx1.enter_context(nc.sbuf_tensor("nHg", [128, NT], BF16))
        ps_h = [ctx1.enter_context(nc.psum_tensor(f"nPh{j}", [128, NT], F32))
                for j in range(2)]
        ps_g = ctx1.enter_context(nc.psum_tensor("nPg", [128, NT], F32))
        ldx = b.sem("nldx")
        ldk = b.sem("nldk")
        cvk = b.sem("ncvk")
        hmm = b.sem("nhmm")
        gbc = b.sem("ngbc")
        gcv = b.sem("ngcv")
        hev = b.sem("nhev")
        hst = b.sem("nhst")

        @b.block.sync
        def _(sync):
            for a in range(8):
                for c in range(NCH):
                    sync.dma_start(out=xn_sb[:, a * T + c * NT: a * T + (c + 1) * NT],
                                   in_=xn_c[c, a]).then_inc(ldx.h, 16)
            sync.dma_start(out=kstg[0][:], in_=keysH[0]).then_inc(ldk.h, 16)
        ldx.n += 16 * 64
        ldk.n += 16

        for k in range(NE):
            @b.block.vector
            def _(v, k=k, lkn=ldk.n, hmn=hmm.n):
                v.wait_ge(ldk.h, lkn)
                if k >= 2:
                    v.wait_ge(hmm.h, hmn - 8)
                v.tensor_copy(kbf[k % 2][:], kstg[k % 2][:]).then_inc(cvk.h, 1)
            cvk.n += 1

            @b.block.sync
            def _(sync, k=k, cvn=cvk.n):
                if k + 1 < NE:
                    if k >= 1:
                        sync.wait_ge(cvk.h, cvn - 1)
                    sync.dma_start(out=kstg[(k + 1) % 2][:], in_=keysH[k + 1]) \
                        .then_inc(ldk.h, 16)
            if k + 1 < NE:
                ldk.n += 16

            for c in range(NCH):
                kc = k * NCH + c

                @b.block.tensor
                def _(t, k=k, c=c, kc=kc, cvn=cvk.n, hevn=hev.n, ghn0=ghe.n):
                    if kc == 0:
                        t.wait_ge(ldx.h, ldx.n)
                        t.wait_ge(ghe.h, ghn0)
                    t.wait_ge(cvk.h, cvn)
                    if kc >= 2:
                        t.wait_ge(hev.h, kc - 1)
                    for a in range(8):
                        i = t.matmul(ps_h[kc % 2][:],
                                     kbf[k % 2][:, a * 128:(a + 1) * 128],
                                     xn_sb[:, a * T + c * NT: a * T + (c + 1) * NT],
                                     start=(a == 0), stop=(a == 7))
                    i.then_inc(hmm.h, 1)
                    if kc >= 1:
                        t.wait_ge(gcv.h, kc)
                    t.matmul(ps_g[:], sel64[:, k * 128:(k + 1) * 128],
                             gTbf[:, c * NT:(c + 1) * NT], start=True, stop=True) \
                        .then_inc(gbc.h, 1)
                hmm.n += 1
                gbc.n += 1

                @b.block.vector
                def _(v, k=k, c=c, kc=kc, gn=gbc.n, hn=hmm.n, hsn=hst.n, gb=gcv.n):
                    v.wait_ge(gbc.h, gn)
                    v.tensor_copy(g_sb[:], ps_g[:]).then_inc(gcv.h, 1)
                    v.wait_ge(hmm.h, hn)
                    if kc >= 1:
                        v.wait_ge(hst.h, hsn)
                    v.wait_ge(gcv.h, gb + 1)
                    i = v.scalar_tensor_tensor(out=hg_sb[:], in0=ps_h[kc % 2][:],
                                               scalar=0.0, in1=g_sb[:],
                                               op0=ALU.max, op1=ALU.mult)
                    i.then_inc(hev.h, 1)
                gcv.n += 1
                hev.n += 1

                @b.block.sync
                def _(sync, k=k, c=c, hn=hev.n):
                    sync.wait_ge(hev.h, hn)
                    sync.dma_start(out=hg_c[k, c], in_=hg_sb[:]).then_inc(hst.h, 16)
                hst.n += 16

        @b.block.sync
        def _(sync, hn=hst.n):
            sync.wait_ge(hst.h, hn)

    sel64_holder.close()
    b.barrier()

    with ExitStack() as ctx2:
        vals_bf = ctx2.enter_context(nc.sbuf_tensor("nVal", [128, NE * 1024], BF16))
        hgl = ctx2.enter_context(nc.sbuf_tensor("nHgl", [128, 32 * NT], BF16))
        x1l = ctx2.enter_context(nc.sbuf_tensor("nX1l", [128, 2 * NT], F32))
        s1l = ctx2.enter_context(nc.sbuf_tensor("nS1l", [128, 2 * NT], F32))
        oacc = ctx2.enter_context(nc.sbuf_tensor("nOac", [128, 8 * NT], F32))
        ps_o = [ctx2.enter_context(nc.psum_tensor(f"nPo{j}", [128, NT], F32))
                for j in range(8)]
        ldv = b.sem("nldv")
        cvv = b.sem("ncvv")
        ldh = b.sem("nldh")
        ldxs = b.sem("nldxs")
        omm = b.sem("nomm")
        oev = b.sem("noev")
        sil = b.sem("nsil")
        cmb = b.sem("ncmb")
        ost = b.sem("nost")

        @b.block.gpsimd
        def _(g):
            for k in range(NE):
                g.dma_start(out=vals_bf[:, k * 1024:(k + 1) * 1024],
                            in_=values[k]).then_inc(cvv.h, 16)
        cvv.n += 16 * NE

        for c in range(NCH):
            for half in range(2):
                @b.block.sync
                def _(sync, c=c, half=half, omn=omm.n):
                    if c + half > 0:
                        sync.wait_ge(omm.h, omn)
                    for kk in range(32):
                        k = half * 32 + kk
                        sync.dma_start(out=hgl[:, kk * NT:(kk + 1) * NT],
                                       in_=hg_c[k, c]).then_inc(ldh.h, 16)
                ldh.n += 16 * 32

                for m in range(8):
                    @b.block.tensor
                    def _(t, c=c, half=half, m=m, ldhn=ldh.n, oevn=oev.n):
                        t.wait_ge(ldh.h, ldhn)
                        if half == 0 and m == 0 and c == 0:
                            t.wait_ge(cvv.h, cvv.n)
                        if half == 0 and c > 0 and m == 0:
                            t.wait_ge(oev.h, oevn)
                        for kk in range(32):
                            k = half * 32 + kk
                            i = t.matmul(ps_o[m][:],
                                         vals_bf[:, k * 1024 + m * 128: k * 1024 + (m + 1) * 128],
                                         hgl[:, kk * NT:(kk + 1) * NT],
                                         start=(half == 0 and kk == 0),
                                         stop=(half == 1 and kk == 31))
                        if half == 1:
                            i.then_inc(omm.h, 1)
                    if half == 1:
                        omm.n += 1
                if half == 0:
                    @b.block.tensor
                    def _(t):
                        t.nop().then_inc(omm.h, 1)
                    omm.n += 1


            @b.block.vector
            def _(v, c=c, omn=omm.n, osn=ost.n):
                v.wait_ge(omm.h, omn)
                if c > 0:
                    v.wait_ge(ost.h, osn)
                for m in range(8):
                    i = v.tensor_copy(oacc[:, m * NT:(m + 1) * NT], ps_o[m][:])
                i.then_inc(oev.h, 1)
            oev.n += 1

            @b.block.scalar
            def _(s, c=c, on=oev.n):
                s.wait_ge(oev.h, on)
                s.activation(out=oacc[:], in_=oacc[:], func=AF.Silu).then_inc(sil.h, 1)
            sil.n += 1

            for a in range(8):
                @b.block.sync
                def _(sync, c=c, a=a, cmn=cmb.n):
                    if c > 0 or a >= 2:
                        sync.wait_ge(cmb.h, cmn - 1)
                    sync.dma_start(out=x1l[:, (a % 2) * NT:(a % 2 + 1) * NT],
                                   in_=x1_c[c, a]).then_inc(ldxs.h, 16)
                    sync.dma_start(out=s1l[:, (a % 2) * NT:(a % 2 + 1) * NT],
                                   in_=s1_c[c, a]).then_inc(ldxs.h, 16)
                ldxs.n += 32

                @b.block.vector
                def _(v, c=c, a=a, sn=sil.n, ldn=ldxs.n):
                    v.wait_ge(sil.h, sn)
                    v.wait_ge(ldxs.h, ldn)
                    asl = slice(a * NT, (a + 1) * NT)
                    bsl = slice((a % 2) * NT, (a % 2 + 1) * NT)
                    v.tensor_tensor(out=oacc[:, asl], in0=oacc[:, asl],
                                    in1=s1l[:, bsl], op=ALU.mult)
                    i = v.tensor_tensor(out=oacc[:, asl], in0=oacc[:, asl],
                                        in1=x1l[:, bsl], op=ALU.add)
                    i.then_inc(cmb.h, 1)
                cmb.n += 1

                @b.block.sync
                def _(sync, c=c, a=a, cn=cmb.n):
                    sync.wait_ge(cmb.h, cn)
                    sync.dma_start(out=out_c[c, a], in_=oacc[:, a * NT:(a + 1) * NT]) \
                        .then_inc(ost.h, 16)
                ost.n += 16

        @b.block.sync
        def _(sync, on=ost.n):
            sync.wait_ge(ost.h, on)





def build_launch(phase_names, weight_shapes):
    nc = bass.Bass("TRN2", target_bir_lowering=False)
    reads = set()
    writes = set()
    produced = {}
    consumed_ext = set()
    for pn in phase_names:
        r, w = PHASES[pn]
        for t in r:
            if t not in produced:
                consumed_ext.add(t)
            reads.add(t)
        for t in w:
            produced[t] = pn
            writes.add(t)
    out_ext = set(w for w in writes)
    with ExitStack() as ctx:
        block = None
        b = B(nc, ctx, None)
        for t in sorted(consumed_ext):
            shape, dt = BOUND[t]
            b.dram[t] = nc.dram_tensor(t, shape, dt, kind="ExternalInput").ap()
        for t in sorted(writes):
            shape, dt = BOUND[t]
            b.dram[t] = nc.dram_tensor(t, shape, dt, kind="ExternalOutput").ap()
        for pn in phase_names:
            for k, (shape, dt) in PHASE_SCRATCH.get(pn, {}).items():
                kind = "ExternalOutput" if k in DEBUG_TENSORS else "Internal"
                b.dram[k] = nc.dram_tensor(k, shape, dt, kind=kind).ap()
                if k in DEBUG_TENSORS:
                    writes.add(k)
        wkeys = []
        for pn in phase_names:
            wkeys += PHASE_WEIGHTS[pn]
        for k in wkeys:
            shape, dt = weight_shapes[k]
            b.dram[k] = nc.dram_tensor(k, list(shape), dt, kind="ExternalInput").ap()
        b.block = ctx.enter_context(nc.Block())
        for i, pn in enumerate(phase_names):
            if i > 0:
                b.barrier()
            emit_phase(b, pn)
    return nc, sorted(consumed_ext), wkeys, sorted(writes)




def host_prep(inputs):
    f = np.ascontiguousarray
    w = {}
    w["wrT"] = (f(inputs["L0_ffn_Wr"].T), F32)
    w["w1"] = (f(inputs["L0_ffn_W1"].reshape(8, 8, 128, 32, 128)
                 .transpose(0, 3, 2, 1, 4).reshape(8, 32, 128, 1024)), F32)
    w["b1r"] = (f(inputs["L0_ffn_b1"].reshape(8, 32, 128).transpose(0, 2, 1)), F32)
    w["wo"] = (f(inputs["L0_ffn_Wo"]), F32)
    w["bor"] = (f(inputs["L0_ffn_bo"].reshape(8, 128).T), F32)
    w["gamma0r"] = (f(inputs["L0_ffn_gamma"].reshape(8, 128).T), F32)
    w["beta0r"] = (f(inputs["L0_ffn_beta"].reshape(8, 128).T), F32)
    w["w_dkv"] = (f(inputs["L0_attn_W_dkv"]), F32)
    w["w_kr"] = (f(inputs["L0_attn_W_kr"]), F32)
    w["idx_wk"] = (f(inputs["L0_attn_idx_wk"]), F32)
    w["idx_wq"] = (f(inputs["L0_attn_idx_wq"]), F32)
    w["w_dq"] = (f(inputs["L0_attn_W_dq"]), F32)
    w["w_uq"] = (f(inputs["L0_attn_W_uq"]), F32)
    w["w_qr"] = (f(inputs["L0_attn_W_qr"].reshape(512, 8, 2, 16)
                   .transpose(0, 2, 1, 3).reshape(512, 256)), F32)
    w["w_ukTh"] = (f(inputs["L0_attn_W_uk"].reshape(D_CKV, N_HEAD, D_HEAD)
                     .transpose(1, 2, 0)), F32)
    w["w_uvTh"] = (f(inputs["L0_attn_W_uv"].reshape(D_CKV, N_HEAD, D_HEAD)
                     .transpose(1, 2, 0)), F32)
    w["w_outm"] = (f(inputs["L0_attn_W_out"]), F32)
    w["idx_w"] = (f(inputs["L0_attn_idx_w"]).reshape(1, 4), F32)
    s_pos = np.arange(T)
    w["c63"] = (np.cos((s_pos[None, :] + 63) * THETA[:, None]).astype(np.float32), F32)
    w["s63"] = (np.sin((s_pos[None, :] + 63) * THETA[:, None]).astype(np.float32), F32)
    w["c0"] = (np.cos(s_pos[None, :64] * THETA[:, None]).astype(np.float32), F32)
    w["s0"] = (np.sin(s_pos[None, :64] * THETA[:, None]).astype(np.float32), F32)
    w["ct_tm"] = (np.cos(s_pos[:, None] * THETA[None, :]).astype(np.float32), F32)
    w["st_tm"] = (np.sin(s_pos[:, None] * THETA[None, :]).astype(np.float32), F32)
    w["c2t_tm"] = (np.cos(2 * s_pos[:, None] * THETA[None, :]).astype(np.float32), F32)
    w["s2t_tm"] = (np.sin(2 * s_pos[:, None] * THETA[None, :]).astype(np.float32), F32)
    import ml_dtypes
    pj = np.arange(128) % 32 + LW
    w["crot"] = (np.cos(pj[:, None] * THETA[None, :]).astype(np.float32), F32)
    w["srot"] = (np.sin(pj[:, None] * THETA[None, :]).astype(np.float32), F32)
    w["identbf"] = (np.eye(128, dtype=ml_dtypes.bfloat16), BF16)
    mdyn = np.full((128, 512), NEG, np.float32)
    for h in range(8):
        for lg in range(16):
            a_, b_ = lg // 4, lg % 4
            mdyn[h * 16 + lg, a_ * 128 + b_ * 32: a_ * 128 + b_ * 32 + 32] = 0.0
    w["maskdyn"] = (mdyn, F32)
    mband = np.full((128, 80), NEG, np.float32)
    for h in range(8):
        for lg in range(16):
            mband[h * 16 + lg, lg:lg + 64] = 0.0
    w["maskband"] = (mband, F32)
    w["nfin0"] = (np.clip(np.arange(128) - 63, 0, 32).astype(np.float32)
                  .reshape(128, 1), F32)
    w["iotaj"] = (np.broadcast_to(np.arange(32, dtype=np.float32), (128, 32)).copy(), F32)
    w["fill64"] = (np.broadcast_to(np.arange(64, 96).astype(np.uint32),
                                   (128, 32)).copy(), U32)
    w["idxwb"] = (np.broadcast_to(np.asarray(inputs["L0_attn_idx_w"],
                                             np.float32).reshape(1, 4),
                                  (128, 4)).copy(), F32)
    w["wqkvT"] = (f(inputs["L1_attn_w_qkv"].T), F32)
    w["proj"] = (f(inputs["L1_attn_proj"]), F32)
    w["b_outr"] = (f(inputs["L1_attn_b_out"].reshape(8, 128).T), F32)
    w["gamma1r"] = (f(inputs["L1_attn_gamma"].reshape(8, 128).T), F32)
    w["beta1r"] = (f(inputs["L1_attn_beta"].reshape(8, 128).T), F32)
    w["wout1T"] = (f(inputs["L1_attn_w_out"].T), F32)
    w["w1aT"] = (f(inputs["L1_attn_w1"].T), F32)
    w["b1ar"] = (f(inputs["L1_attn_b1"].reshape(32, 128).T), F32)
    w["w2aT"] = (f(inputs["L1_attn_w2"].T), F32)
    w["b2ar"] = (f(inputs["L1_attn_b2"].reshape(8, 128).T), F32)
    w["eselT"] = (f(inputs["L1_ffn_expert_sel"].T), F32)
    w["keys"] = (f(inputs["L1_ffn_keys"].reshape(64, 8, 128, 128)
                   .transpose(0, 2, 1, 3).reshape(64, 128, 1024)), F32)
    w["values"] = (f(inputs["L1_ffn_values"]), F32)
    return w


DEBUG_TENSORS = set()

LAUNCH_PLAN = [["ln0"], ["l0ffn"], ["prep"], ["l0att"], ["ln1"], ["l1att"],
               ["l1ffn"]]


def run_plan(inputs, plan):
    w = host_prep(inputs)
    weight_shapes = {k: (v[0].shape, v[1]) for k, v in w.items()}
    x = np.ascontiguousarray(inputs["L0_attn_x"])
    percore = {c: {"xT": np.ascontiguousarray(x[c].T)} for c in range(8)}
    for launch in plan:
        nc, ext_in, wkeys, ext_out = build_launch(launch, weight_shapes)
        in_maps = []
        for c in range(8):
            m = {}
            for t in ext_in:
                m[t] = percore[c][t]
            for k in wkeys:
                m[k] = w[k][0]
            in_maps.append(m)
        res = run_bass_kernel_spmd(nc, in_maps, core_ids=list(range(8)))
        for c in range(8):
            for t in ext_out:
                percore[c][t] = res.results[c][t]
    return percore


def kernel(**inputs):
    percore = run_plan(inputs, LAUNCH_PLAN)
    out = np.stack([percore[c]["outT"].T for c in range(8)])
    return out.astype(np.float32)


# version 03
# speedup vs baseline: 1.0004x; 1.0004x over previous
import numpy as np
from contextlib import ExitStack

import concourse.bass as bass
import concourse.mybir as mybir
from concourse.bass_utils import run_bass_kernel_spmd

F32 = mybir.dt.float32
BF16 = mybir.dt.bfloat16
I32 = mybir.dt.int32
U32 = mybir.dt.uint32
U8 = mybir.dt.uint8
AF = mybir.ActivationFunctionType
ALU = mybir.AluOpType
AX = mybir.AxisListType

T = 4096
D = 1024
NEG = -1e30
NT = 512
NCH = T // NT

D_CKV, D_CQ = 256, 512
N_HEAD, D_HEAD, D_ROPE = 8, 64, 32
N_IDX_HEAD = 4
K_TS, LW = 96, 64
N_DYN = K_TS - LW
ATT_SCALE = (D_HEAD + D_ROPE) ** -0.5
RHALF = D_ROPE // 2
THETA = (1.0 / (10000.0 ** (2.0 * np.arange(RHALF) / D_ROPE)))

ENGINES = ("sync", "vector", "scalar", "tensor", "gpsimd")


class Sem:
    def __init__(self, h):
        self.h = h
        self.n = 0


class B:

    POOL = 95

    def __init__(self, nc, ctx, block):
        self.nc = nc
        self.ctx = ctx
        self.block = block
        self.dram = {}
        self._sems = {}
        self._pool = []
        self._cursor = 0
        self._barsem = None

    def new_phase(self, base=0):
        self._cursor = base

    def sem(self, name) -> Sem:
        if name not in self._sems:
            slot = self._cursor
            self._cursor += 1
            assert slot < self.POOL, "semaphore pool exhausted"
            while len(self._pool) <= slot:
                i = len(self._pool)
                self._pool.append(Sem(self.ctx.enter_context(
                    self.nc.semaphore(f"pool{i}"))))
            self._sems[name] = self._pool[slot]
        return self._sems[name]

    def sb(self, name, shape, dt=F32):
        return self.ctx.enter_context(self.nc.sbuf_tensor(name, shape, dt))

    def psum(self, name, shape, dt=F32):
        return self.ctx.enter_context(self.nc.psum_tensor(name, shape, dt))

    def barrier(self, clear=False):
        if self._barsem is None:
            self._barsem = Sem(self.ctx.enter_context(self.nc.semaphore("barsem")))
        s = self._barsem
        drains = [(p.h, p.n) for p in self._pool if p.n > 0]

        @self.block.sync
        def _(sync, drains=drains):
            for h, n in drains:
                sync.wait_ge(h, n)

        for e in ENGINES:
            @getattr(self.block, e)
            def _(eng, s=s):
                eng.nop().then_inc(s.h, 1)
        s.n += 5
        arrive = s.n
        if clear:
            pool = list(self._pool)

            @self.block.gpsimd
            def _(g, s=s, arrive=arrive, pool=pool):
                g.wait_ge(s.h, arrive)
                i = None
                for p in pool:
                    g.dma_reset(range(p.h.num, p.h.num + 1))
                    i = g.sem_clear(p.h)
                if i is None:
                    i = g.engine_nop()
                i.then_inc(s.h, 1)
            s.n += 1
            for p in self._pool:
                p.n = 0
        tgt = s.n
        for e in ENGINES:
            @getattr(self.block, e)
            def _(eng, s=s, tgt=tgt):
                eng.wait_ge(s.h, tgt)




class LNAffine:

    def __init__(self, b: B, tag, ctx, gvec, bvec, eps):
        nc = b.nc
        self.b, self.tag, self.eps = b, tag, eps
        self.gvec, self.bvec = gvec, bvec
        sb = lambda n, s, d=F32: ctx.enter_context(nc.sbuf_tensor(tag + n, s, d))
        self.y_bf = ctx.enter_context(nc.sbuf_tensor(tag + "ybf", [128, 8 * NT], BF16))
        self.tn = ctx.enter_context(nc.sbuf_tensor(tag + "tn", [128, 8 * NT], F32))
        self.ysq = ctx.enter_context(nc.sbuf_tensor(tag + "ysq", [128, 8 * NT], BF16))
        self.ones1 = ctx.enter_context(nc.sbuf_tensor(tag + "on1", [128, 1], BF16))
        self.onesc = sb("onc", [1, 128])
        self.mu_r = sb("mu", [1, NT])
        self.var_r = sb("var", [1, NT])
        self.musq = sb("musq", [1, NT])
        self.rstd_r = sb("rstd", [1, NT])
        self.ps_sx = ctx.enter_context(nc.psum_tensor(tag + "psx", [1, NT], F32))
        self.ps_sq = ctx.enter_context(nc.psum_tensor(tag + "psq", [1, NT], F32))
        self.ps_mu = ctx.enter_context(nc.psum_tensor(tag + "pmu", [128, NT], F32))
        self.ps_rs = ctx.enter_context(nc.psum_tensor(tag + "prs", [128, NT], F32))
        self.cvy = b.sem(tag + "cvy")
        self.smm = b.sem(tag + "smm")
        self.sve = b.sem(tag + "sve")
        self.sae = b.sem(tag + "sae")
        self.bcy = b.sem(tag + "bcy")
        self.nrm = b.sem(tag + "nrm")
        self.k = 0

        @b.block.vector
        def _(v):
            v.memset(self.ones1[:], 1.0)
            v.memset(self.onesc[:], 1.0)

    def emit_chunk(self, y_ready: Sem, y_ready_n: int, y_sb, out_sb,
                   out_free: Sem = None, out_free_n: int = 0):
        b, tag = self.b, self.tag
        k = self.k
        self.k += 1
        cvy, smm, sve, sae, bcy, nrm = (self.cvy, self.smm, self.sve, self.sae,
                                        self.bcy, self.nrm)
        y_bf, ysq, mu_r, var_r, musq, rstd_r = (self.y_bf, self.ysq, self.mu_r,
                                                self.var_r, self.musq, self.rstd_r)
        ps_sx, ps_sq, ps_mu, ps_rs = self.ps_sx, self.ps_sq, self.ps_mu, self.ps_rs
        gvec, bvec, eps = self.gvec, self.bvec, self.eps

        @b.block.vector
        def _(v, k=k, yn=y_ready_n):
            v.wait_ge(y_ready.h, yn)
            if k > 0:
                v.wait_ge(smm.h, k)
            v.tensor_copy(y_bf[:], y_sb[:]).then_inc(cvy.h, 1)
        cvy.n += 1

        @b.block.scalar
        def _(s, k=k, yn=y_ready_n):
            s.wait_ge(y_ready.h, yn)
            if k > 0:
                s.wait_ge(smm.h, k)
            s.activation(out=ysq[:], in_=y_sb[:], func=AF.Square).then_inc(cvy.h, 1)
        cvy.n += 1

        @b.block.tensor
        def _(t, k=k, cvn=cvy.n, nrn=nrm.n):
            t.wait_ge(cvy.h, cvn)
            if k > 0:
                t.wait_ge(nrm.h, nrn)
            for a in range(8):
                t.matmul(ps_sx[:], self.ones1[:], y_bf[:, a * NT:(a + 1) * NT],
                         start=(a == 0), stop=(a == 7))
            for a in range(8):
                i = t.matmul(ps_sq[:], self.ones1[:], ysq[:, a * NT:(a + 1) * NT],
                             start=(a == 0), stop=(a == 7))
            i.then_inc(smm.h, 1)
        smm.n += 1

        @b.block.vector
        def _(v, k=k, svb=sve.n):
            v.wait_ge(smm.h, k + 1)
            v.tensor_scalar(out=mu_r[:], in0=ps_sx[:], scalar1=1.0 / D,
                            scalar2=None, op0=ALU.mult)
            v.tensor_scalar(out=var_r[:], in0=ps_sq[:], scalar1=1.0 / D,
                            scalar2=float(eps), op0=ALU.mult, op1=ALU.add)
            v.tensor_tensor(out=musq[:], in0=mu_r[:], in1=mu_r[:],
                            op=ALU.mult).then_inc(sve.h, 1)
            v.wait_ge(sve.h, svb + 1)
            v.tensor_tensor(out=var_r[:], in0=var_r[:], in1=musq[:],
                            op=ALU.subtract).then_inc(sve.h, 1)
        sve.n += 2

        @b.block.scalar
        def _(s, k=k, svn=sve.n, sab=sae.n):
            s.wait_ge(sve.h, svn)
            s.activation(out=rstd_r[:], in_=var_r[:], func=AF.Ln).then_inc(sae.h, 1)
            s.wait_ge(sae.h, sab + 1)
            s.activation(out=rstd_r[:], in_=rstd_r[:], func=AF.Exp, scale=-0.5) \
                .then_inc(sae.h, 2)
        sae.n += 3

        @b.block.tensor
        def _(t, k=k, san=sae.n):
            t.wait_ge(sae.h, san)
            t.matmul(ps_mu[:], self.onesc[:], mu_r[:], start=True, stop=True)
            t.matmul(ps_rs[:], self.onesc[:], rstd_r[:], start=True, stop=True) \
                .then_inc(bcy.h, 1)
        bcy.n += 1

        @b.block.vector
        def _(v, k=k, bn=bcy.n, ofn=out_free_n):
            v.wait_ge(bcy.h, bn)
            if out_free is not None:
                v.wait_ge(out_free.h, ofn)
            for m in range(8):
                sl = slice(m * NT, (m + 1) * NT)
                v.tensor_tensor(out=self.tn[:, sl], in0=y_sb[:, sl], in1=ps_mu[:],
                                op=ALU.subtract)
            for m in range(8):
                sl = slice(m * NT, (m + 1) * NT)
                v.tensor_tensor(out=out_sb[:, sl], in0=self.tn[:, sl], in1=ps_rs[:],
                                op=ALU.mult)
            for m in range(8):
                sl = slice(m * NT, (m + 1) * NT)
                i = v.tensor_scalar(out=out_sb[:, sl], in0=out_sb[:, sl],
                                    scalar1=gvec[:, m:m + 1],
                                    scalar2=bvec[:, m:m + 1],
                                    op0=ALU.mult, op1=ALU.add)
            i.then_inc(nrm.h, 1)
        nrm.n += 1
        return nrm




def emit_ln(b: B, tag, src, dst, mu_out, rstd_out):
    nc = b.nc
    with ExitStack() as ctx:
        def sb(n, s, d=F32):
            return ctx.enter_context(nc.sbuf_tensor(tag + n, s, d))
        x = sb("x", [128, 8 * NT])
        x_bf = sb("xbf", [128, 8 * NT], BF16)
        xsq = sb("xsq", [128, 8 * NT], BF16)
        tnorm = sb("tn", [128, 8 * NT])
        out_t = sb("out", [128, 8 * NT], BF16)
        ones = sb("ones", [128, 1], BF16)
        onesc = sb("onesc", [1, 128])
        mu_r = sb("mu", [1, NT])
        var_r = sb("var", [1, NT])
        musq_r = sb("musq", [1, NT])
        rstd_r = sb("rstd", [1, NT])
        ps_sx = ctx.enter_context(nc.psum_tensor(tag + "psx", [1, NT], F32))
        ps_sq = ctx.enter_context(nc.psum_tensor(tag + "psq", [1, NT], F32))
        ps_mu = ctx.enter_context(nc.psum_tensor(tag + "pmu", [128, NT], F32))
        ps_rs = ctx.enter_context(nc.psum_tensor(tag + "prs", [128, NT], F32))
        ld = b.sem(tag + "ld")
        cv = b.sem(tag + "cv")
        mm = b.sem(tag + "mm")
        sv = b.sem(tag + "sv")
        sa = b.sem(tag + "sa")
        bc = b.sem(tag + "bc")
        nr = b.sem(tag + "nr")
        st = b.sem(tag + "st")

        src_c = src.rearrange("(a p) (c n) -> c a p n", p=128, n=NT)
        dst_c = dst.rearrange("(a p) (c n) -> c a p n", p=128, n=NT)
        mu_c = mu_out.rearrange("o (c n) -> c o n", n=NT)
        rstd_c = rstd_out.rearrange("o (c n) -> c o n", n=NT)

        for c in range(NCH):
            @b.block.sync
            def _(sync, c=c, nrn=nr.n):
                if c > 0:
                    sync.wait_ge(nr.h, nrn)
                for a in range(8):
                    sync.dma_start(out=x[:, a * NT:(a + 1) * NT], in_=src_c[c, a]) \
                        .then_inc(ld.h, 16)
            ld.n += 128

            @b.block.vector
            def _(v, c=c, ldn=ld.n):
                if c == 0:
                    v.memset(ones[:], 1.0)
                    v.memset(onesc[:], 1.0)
                v.wait_ge(ld.h, ldn)
                v.tensor_copy(x_bf[:], x[:]).then_inc(cv.h, 1)
            cv.n += 1

            @b.block.scalar
            def _(s, c=c, ldn=ld.n):
                s.wait_ge(ld.h, ldn)
                s.activation(out=xsq[:], in_=x[:], func=AF.Square).then_inc(cv.h, 1)
            cv.n += 1

            @b.block.tensor
            def _(t, c=c, cvn=cv.n, nrn=nr.n):
                t.wait_ge(cv.h, cvn)
                if c > 0:
                    t.wait_ge(nr.h, nrn)
                for a in range(8):
                    t.matmul(ps_sx[:], ones[:], x_bf[:, a * NT:(a + 1) * NT],
                             start=(a == 0), stop=(a == 7))
                for a in range(8):
                    i = t.matmul(ps_sq[:], ones[:], xsq[:, a * NT:(a + 1) * NT],
                                 start=(a == 0), stop=(a == 7))
                i.then_inc(mm.h, 1)
            mm.n += 1

            @b.block.vector
            def _(v, c=c, mmn=mm.n, svb=sv.n):
                v.wait_ge(mm.h, mmn)
                v.tensor_scalar(out=mu_r[:], in0=ps_sx[:], scalar1=1.0 / D,
                                scalar2=None, op0=ALU.mult)
                v.tensor_scalar(out=var_r[:], in0=ps_sq[:], scalar1=1.0 / D,
                                scalar2=1e-5, op0=ALU.mult, op1=ALU.add)
                v.tensor_tensor(out=musq_r[:], in0=mu_r[:], in1=mu_r[:], op=ALU.mult) \
                    .then_inc(sv.h, 1)
                v.wait_ge(sv.h, svb + 1)
                v.tensor_tensor(out=var_r[:], in0=var_r[:], in1=musq_r[:],
                                op=ALU.subtract).then_inc(sv.h, 1)
            sv.n += 2

            @b.block.scalar
            def _(s, c=c, svn=sv.n, san=sa.n):
                s.wait_ge(sv.h, svn)
                s.activation(out=rstd_r[:], in_=var_r[:], func=AF.Ln) \
                    .then_inc(sa.h, 1)
                s.wait_ge(sa.h, san + 1)
                s.activation(out=rstd_r[:], in_=rstd_r[:], func=AF.Exp, scale=-0.5) \
                    .then_inc(sa.h, 2)
            sa.n += 3

            @b.block.tensor
            def _(t, c=c, san=sa.n):
                t.wait_ge(sa.h, san)
                t.matmul(ps_mu[:], onesc[:], mu_r[:], start=True, stop=True)
                i = t.matmul(ps_rs[:], onesc[:], rstd_r[:], start=True, stop=True)
                i.then_inc(bc.h, 1)
            bc.n += 1

            @b.block.vector
            def _(v, c=c, bcn=bc.n, stn=st.n):
                v.wait_ge(bc.h, bcn)
                if c > 1:
                    v.wait_ge(st.h, stn - 160)
                for a in range(8):
                    sl = slice(a * NT, (a + 1) * NT)
                    v.tensor_tensor(out=tnorm[:, sl], in0=x[:, sl], in1=ps_mu[:],
                                    op=ALU.subtract)
                for a in range(8):
                    i = v.tensor_tensor(out=out_t[:, a * NT:(a + 1) * NT],
                                        in0=tnorm[:, a * NT:(a + 1) * NT],
                                        in1=ps_rs[:], op=ALU.mult)
                i.then_inc(nr.h, 1)
            nr.n += 1

            @b.block.sync
            def _(sync, c=c, nrn=nr.n):
                sync.wait_ge(nr.h, nrn)
                for a in range(8):
                    sync.dma_start(out=dst_c[c, a], in_=out_t[:, a * NT:(a + 1) * NT]) \
                        .then_inc(st.h, 16)
                sync.dma_start(out=mu_c[c], in_=mu_r[:]).then_inc(st.h, 16)
                sync.dma_start(out=rstd_c[c], in_=rstd_r[:]).then_inc(st.h, 16)
            st.n += 160

        @b.block.sync
        def _(sync, stn=st.n):
            sync.wait_ge(st.h, stn)



BOUND = {
    "xT": ([D, T], F32),
    "xn0T": ([D, T], BF16),
    "mu0": ([1, T], F32),
    "rstd0": ([1, T], F32),
    "g0T": ([D, T], F32),
    "kcat": ([T + 16, 320], BF16),
    "ckvT": ([D_CKV, T], BF16),
    "kiT": ([D_HEAD, T], BF16),
    "gkb1T": ([RHALF, T], BF16),
    "gkb2T": ([RHALF, T], BF16),
    "gka1T": ([RHALF, 64], BF16),
    "gka2T": ([RHALF, 64], BF16),
    "wcatT": ([N_HEAD * D_CKV, D], BF16),
    "sT": ([D, T], F32),
    "x1T": ([D, T], F32),
    "mu1": ([1, T], F32),
    "rstd1": ([1, T], F32),
    "xn1T": ([D, T], BF16),
    "s1T": ([D, T], F32),
    "outT": ([D, T], F32),
}

PHASE_SCRATCH = {
    "l0att": {"idx_d": ([128, 32], U32), "dbg_sd": ([8 * 128, 512], F32),
              "dbg_sb": ([8 * 128, 80], F32), "dbg_idx": ([128, 32], U32)},
    "l0ffn": {"interG": ([4 * D, T], BF16),
              "dbg_oh": ([T, 8], F32), "dbg_log": ([T, 8], F32)},
    "l1ffn": {"l1hg": ([64 * 128, T], BF16)},
    "l1att": {"l1h": ([4 * D, T], BF16), "l1xm": ([D, T], BF16),
              "dbg_attn": ([D, T], BF16), "dbg_kv": ([128, 16 * 2 * 65], BF16),
              "dbg_kf": ([128, 32 * 256], BF16), "dbg_vh": ([128, 32 * 64], BF16),
              "dbg_kt": ([64, T], BF16)},
}

PHASE_SLOT_BASE = {"l1ffn": 55}

PHASES = {
    "ln0": (["xT"], ["xn0T", "mu0", "rstd0"]),
    "l0ffn": (["xT", "xn0T", "mu0", "rstd0"], ["g0T"]),
    "prep": (["xn0T"], ["kcat", "ckvT", "kiT", "gkb1T", "gkb2T", "gka1T",
                         "gka2T", "wcatT"]),
    "l0att": (["xT", "xn0T", "g0T", "kcat", "ckvT", "kiT", "gkb1T", "gkb2T",
               "gka1T", "gka2T", "wcatT"], ["x1T"]),
    "ln1": (["x1T"], ["xn1T", "mu1", "rstd1"]),
    "l1att": (["xn1T"], ["s1T"]),
    "l1ffn": (["x1T", "xn1T", "mu1", "rstd1", "s1T"], ["outT"]),
}

PHASE_WEIGHTS = {
    "ln0": [],
    "l0ffn": ["wrT", "w1", "b1r", "wo", "bor", "gamma0r", "beta0r"],
    "prep": ["w_dkv", "w_kr", "idx_wk", "w_uvTh", "w_outm", "c63", "s63", "c0", "s0"],
    "l0att": ["idx_wq", "w_dq", "w_uq", "w_qr", "w_ukTh", "identbf", "maskdyn",
              "maskband", "crot", "srot", "nfin0", "iotaj", "fill64", "idxwb",
              "ct_tm", "st_tm", "c2t_tm", "s2t_tm"],
    "ln1": [],
    "l1att": ["wqkvT", "proj", "wout1T", "b_outr", "gamma1r", "beta1r", "w1aT",
              "b1ar", "w2aT", "b2ar"],
    "l1ffn": ["eselT", "keys", "values"],
}


def emit_phase(b: B, name):
    if name == "ln0":
        emit_ln(b, "ln0", b.dram["xT"], b.dram["xn0T"], b.dram["mu0"], b.dram["rstd0"])
    elif name == "ln1":
        emit_ln(b, "ln1", b.dram["x1T"], b.dram["xn1T"], b.dram["mu1"], b.dram["rstd1"])
    elif name == "l0ffn":
        emit_l0ffn(b)
    elif name == "prep":
        emit_prep(b)
    elif name == "l0att":
        emit_l0att(b)
    elif name == "l1att":
        emit_l1att(b)
    elif name == "l1ffn":
        emit_l1ffn(b)
    else:
        raise KeyError(name)


def emit_identity(b: B, g, tile, ones_tile):
    g.affine_select(out=tile[:], in_=ones_tile[:], pattern=[[1, tile.shape[1]]],
                    compare_op=ALU.is_equal, fill=0.0, base=0, channel_multiplier=-1)


def emit_l0ffn(b: B):
    nc = b.nc
    xT, xn0T = b.dram["xT"], b.dram["xn0T"]
    mu0, rstd0 = b.dram["mu0"], b.dram["rstd0"]
    g0T = b.dram["g0T"]
    wrT, w1, b1r = b.dram["wrT"], b.dram["w1"], b.dram["b1r"]
    wo, bor = b.dram["wo"], b.dram["bor"]
    gamma0r, beta0r = b.dram["gamma0r"], b.dram["beta0r"]
    interG_d = b.dram["interG"]

    xT_c = xT.rearrange("(a p) (c n) -> c a p n", p=128, n=NT)
    xn_c = xn0T.rearrange("(a p) (c n) -> c a p n", p=128, n=NT)
    g0_c = g0T.rearrange("(m p) (c n) -> c m p n", p=128, n=NT)
    ig_d = interG_d.rearrange("(i p) (c n) -> i c p n", p=128, n=NT)
    w1_t = w1
    wo_t = wo.rearrange("(i p) (m q) -> i m p q", p=128, q=128)

    with ExitStack() as ctx:
        def sb(n, s, d=F32):
            return ctx.enter_context(nc.sbuf_tensor("fA" + n, s, d))
        xf = sb("xf", [128, 8 * NT])
        xnf = sb("xnf", [128, 8 * NT])
        wr_sb = sb("wr", [128, 8 * 8])
        murow = sb("mur", [1, NT])
        rsrow = sb("rsr", [1, NT])
        onesc = sb("onesc", [1, 128])
        ones128 = sb("ones128", [128, 128])
        identf = sb("identf", [128, 128])
        logT = sb("logT", [8, T])
        log_tm = sb("logtm", [128, 8])
        m8s = sb("m8", [128, 8])
        mval8 = sb("mval8", [128, 8])
        knock = sb("knock", [128, 8])
        oh_tm = sb("ohtm", [128, 8])
        ohT_bf = ctx.enter_context(nc.sbuf_tensor("fAohTbf", [8, T], BF16))
        sel8 = sb("sel8", [8, 8 * 128], BF16)
        masks = ctx.enter_context(nc.sbuf_tensor("fAmasks", [128, 8 * T], U8))
        ps_log = ctx.enter_context(nc.psum_tensor("fAplog", [8, NT], F32))
        ps_b = ctx.enter_context(nc.psum_tensor("fApb", [128, NT], F32))
        ps_tr = ctx.enter_context(nc.psum_tensor("fAptr", [128, 8], F32))
        ps_tr2 = ctx.enter_context(nc.psum_tensor("fAptr2", [8, 128], F32))
        ps_m = ctx.enter_context(nc.psum_tensor("fApm", [128, NT], F32))
        ld = b.sem("fAld")
        pre = b.sem("fApre")
        bcn = b.sem("fAbc")
        xnd = b.sem("fAxnd")
        lgm = b.sem("fAlgm")
        lge = b.sem("fAlge")
        trd = b.sem("fAtrd")
        ohd = b.sem("fAohd")
        tr2 = b.sem("fAtr2")
        ohe = b.sem("fAohe")
        mbc = b.sem("fAmbc")
        mev = b.sem("fAmev")
        dbg = b.sem("fAdbg")

        @b.block.gpsimd
        def _(g):
            g.memset(ones128[:], 1.0)
            emit_identity(b, g, identf, ones128)
            g.tensor_copy(onesc[:], ones128[0:1, 0:128])
            g.memset(sel8[:], 1.0)
            g.affine_select(out=sel8[:], in_=sel8[:], pattern=[[1, 8], [0, 128]],
                            compare_op=ALU.is_equal, fill=0.0, base=0,
                            channel_multiplier=-1)
            g.engine_nop().then_inc(pre.h, 1)
        pre.n += 1

        @b.block.sync
        def _(sync):
            for a in range(8):
                sync.dma_start(out=wr_sb[:, a * 8:(a + 1) * 8],
                               in_=wrT.rearrange("(a p) e -> a p e", p=128)[a]) \
                    .then_inc(ld.h, 16)
        ld.n += 128

        for c in range(NCH):
            @b.block.sync
            def _(sync, c=c, xnn=xnd.n):
                if c > 0:
                    sync.wait_ge(xnd.h, xnn)
                for a in range(8):
                    sync.dma_start(out=xf[:, a * NT:(a + 1) * NT], in_=xT_c[c, a]) \
                        .then_inc(ld.h, 16)
                sync.dma_start(out=murow[:], in_=mu0[:, c * NT:(c + 1) * NT]) \
                    .then_inc(ld.h, 16)
                sync.dma_start(out=rsrow[:], in_=rstd0[:, c * NT:(c + 1) * NT]) \
                    .then_inc(ld.h, 16)
            ld.n += 160

            @b.block.tensor
            def _(t, c=c, ldn=ld.n, xnn=xnd.n):
                t.wait_ge(ld.h, ldn)
                t.wait_ge(pre.h, 1)
                if c > 0:
                    t.wait_ge(xnd.h, xnn)
                t.matmul(ps_b[:], onesc[:], murow[:], start=True, stop=True)
                t.matmul(ps_m[:], onesc[:], rsrow[:], start=True, stop=True) \
                    .then_inc(bcn.h, 1)
            bcn.n += 1

            @b.block.vector
            def _(v, c=c, bb=bcn.n, trn=trd.n):
                v.wait_ge(bcn.h, bb)
                if c > 0:
                    v.wait_ge(trd.h, trn)
                for a in range(8):
                    sl = slice(a * NT, (a + 1) * NT)
                    v.tensor_tensor(out=xnf[:, sl], in0=xf[:, sl], in1=ps_b[:],
                                    op=ALU.subtract)
                for a in range(8):
                    sl = slice(a * NT, (a + 1) * NT)
                    i = v.tensor_tensor(out=xnf[:, sl], in0=xnf[:, sl], in1=ps_m[:],
                                        op=ALU.mult)
                i.then_inc(xnd.h, 1)
            xnd.n += 1

            for q in range(4 * c, 4 * c + 4):
                qo = (q % 4) * 128

                @b.block.tensor
                def _(t, q=q, qo=qo, ohn=ohd.n, xnn=xnd.n):
                    t.wait_ge(xnd.h, xnn)
                    if q > 0:
                        t.wait_ge(ohd.h, ohn)
                    for a in range(8):
                        i = t.matmul(ps_tr[:, 0:8],
                                     xnf[:, a * NT + qo: a * NT + qo + 128],
                                     wr_sb[:, a * 8:(a + 1) * 8],
                                     start=(a == 0), stop=(a == 7))
                    i.then_inc(trd.h, 1)
                trd.n += 1

                @b.block.vector
                def _(v, q=q, tn=trd.n, ohb=ohd.n):
                    v.wait_ge(trd.h, tn)
                    v.tensor_copy(log_tm[:], ps_tr[:, 0:8])
                    v.memset(mval8[:], NEG).then_inc(ohd.h, 1)
                    v.wait_ge(ohd.h, ohb + 1)
                    v.max(out=m8s[:], in_=log_tm[:]).then_inc(ohd.h, 1)
                    v.wait_ge(ohd.h, ohb + 2)
                    v.tensor_copy(mval8[:, 0:1], m8s[:, 0:1]).then_inc(ohd.h, 1)
                    v.wait_ge(ohd.h, ohb + 3)
                    v.match_replace(out=knock[:], in_to_replace=mval8[:],
                                    in_values=log_tm[:], imm_value=NEG) \
                        .then_inc(ohd.h, 1)
                    v.wait_ge(ohd.h, ohb + 4)
                    i = v.tensor_tensor(out=oh_tm[:], in0=log_tm[:], in1=knock[:],
                                        op=ALU.not_equal)
                    i.then_inc(ohd.h, 1)
                ohd.n += 5

                @b.block.tensor
                def _(t, q=q, ohn=ohd.n, ohe_n=ohe.n):
                    t.wait_ge(ohd.h, ohn)
                    if q > 0:
                        t.wait_ge(ohe.h, ohe_n)
                    t.transpose(ps_tr2[:], oh_tm[:], identf[:]).then_inc(tr2.h, 1)
                tr2.n += 1

                @b.block.vector
                def _(v, q=q, tn=tr2.n):
                    v.wait_ge(tr2.h, tn)
                    v.tensor_copy(ohT_bf[:, q * 128:(q + 1) * 128], ps_tr2[0:8, :]) \
                        .then_inc(ohe.h, 1)
                ohe.n += 1

                if DEBUG_TENSORS:
                    @b.block.scalar
                    def _(s, q=q, ohn=ohd.n):
                        s.wait_ge(ohd.h, ohn)
                        s.dma_start(out=b.dram["dbg_oh"][q * 128:(q + 1) * 128, :],
                                    in_=oh_tm[:]).then_inc(dbg.h, 16)
                        s.dma_start(out=b.dram["dbg_log"][q * 128:(q + 1) * 128, :],
                                    in_=log_tm[:]).then_inc(dbg.h, 16)
                    dbg.n += 32

                    @b.block.vector
                    def _(v, dbn=dbg.n):
                        v.wait_ge(dbg.h, dbn)


        for e in range(8):
            for c in range(NCH):
                r = e * NCH + c

                @b.block.tensor
                def _(t, e=e, c=c, r=r):
                    if r == 0:
                        t.wait_ge(ohe.h, T // 128)
                    if r > 0:
                        t.wait_ge(mev.h, r)
                    t.matmul(ps_m[:], sel8[:, e * 128:(e + 1) * 128],
                             ohT_bf[:, c * NT:(c + 1) * NT],
                             start=True, stop=True).then_inc(mbc.h, 1)
                mbc.n += 1

                @b.block.vector
                def _(v, e=e, c=c, mn=mbc.n):
                    v.wait_ge(mbc.h, mn)
                    v.tensor_copy(masks[:, e * T + c * NT: e * T + (c + 1) * NT],
                                  ps_m[:]).then_inc(mev.h, 1)
                mev.n += 1

        with ExitStack() as ctxB:
            def sbB(n, s, d=BF16):
                return ctxB.enter_context(nc.sbuf_tensor("fB" + n, s, d))
            xn_sb = sbB("xn", [128, 8 * T])
            w1t = [sbB(f"w1_{j}", [128, 8 * 128]) for j in range(2)]
            b1_sb = sbB("b1", [128, 8 * 32], F32)
            scr = sbB("scr", [128, NT])
            accs = [sbB(f"acc{j}", [128, T]) for j in range(2)]
            ps_i = [ctxB.enter_context(nc.psum_tensor(f"fBpi{j}", [128, NT], F32))
                    for j in range(2)]
            ldx = b.sem("fBldx")
            ldw = b.sem("fBldw")
            ldb = b.sem("fBldb")
            mmi = b.sem("fBmmi")
            gel = b.sem("fBgel")
            prc = b.sem("fBprc")
            sti = b.sem("fBsti")

            @b.block.sync
            def _(sync):
                for a in range(8):
                    for c in range(NCH):
                        sync.dma_start(
                            out=xn_sb[:, a * T + c * NT: a * T + (c + 1) * NT],
                            in_=xn_c[c, a]).then_inc(ldx.h, 16)
            ldx.n += 16 * 64

            @b.block.scalar
            def _(s):
                for e in range(8):
                    s.dma_start(out=b1_sb[:, e * 32:(e + 1) * 32], in_=b1r[e]) \
                        .then_inc(ldb.h, 16)
            ldb.n += 16 * 8

            NEI = 32 * 8

            @b.block.gpsimd
            def _(g):
                g.dma_start(out=w1t[0][:], in_=w1_t[0, 0]).then_inc(ldw.h, 16)
            ldw.n += 16

            for j in range(NEI):
                i_, e_ = j // 8, j % 8

                @b.block.gpsimd
                def _(g, j=j):
                    if j + 1 < NEI:
                        if j >= 1:
                            g.wait_ge(prc.h, 8 * j)
                        i2, e2 = (j + 1) // 8, (j + 1) % 8
                        g.dma_start(out=w1t[(j + 1) % 2][:], in_=w1_t[e2, i2]) \
                            .then_inc(ldw.h, 16)
                if j + 1 < NEI:
                    ldw.n += 16

                for n in range(NCH):
                    k = 8 * j + n

                    @b.block.tensor
                    def _(t, j=j, n=n, k=k, ldxn=ldx.n):
                        if k == 0:
                            t.wait_ge(ldx.h, ldxn)
                        t.wait_ge(ldw.h, 16 * (j + 1))
                        if k >= 2:
                            t.wait_ge(gel.h, k - 1)
                        for a in range(8):
                            i = t.matmul(
                                ps_i[k % 2][:], w1t[j % 2][:, a * 128:(a + 1) * 128],
                                xn_sb[:, a * T + n * NT: a * T + (n + 1) * NT],
                                start=(a == 0), stop=(a == 7))
                        i.then_inc(mmi.h, 1)
                    mmi.n += 1

                    @b.block.scalar
                    def _(s, j=j, n=n, k=k, i_=i_, e_=e_, ldbn=ldb.n):
                        s.wait_ge(mmi.h, k + 1)
                        if k == 0:
                            s.wait_ge(ldb.h, ldbn)
                        if k >= 1:
                            s.wait_ge(prc.h, k)
                        s.activation(out=scr[:], in_=ps_i[k % 2][:], func=AF.Gelu,
                                     bias=b1_sb[:, e_ * 32 + i_: e_ * 32 + i_ + 1]) \
                            .then_inc(gel.h, 1)
                    gel.n += 1

                    @b.block.vector
                    def _(v, j=j, n=n, k=k, i_=i_, e_=e_, stn=sti.n):
                        v.wait_ge(gel.h, k + 1)
                        if e_ == 0 and n == 0 and i_ >= 2:
                            v.wait_ge(sti.h, stn - 128)
                        v.copy_predicated(
                            out=accs[i_ % 2][:, n * NT:(n + 1) * NT],
                            mask=masks[:, e_ * T + n * NT: e_ * T + (n + 1) * NT],
                            data=scr[:]).then_inc(prc.h, 1)
                    prc.n += 1

                if e_ == 7:
                    @b.block.sync
                    def _(sync, i_=i_, prn=prc.n):
                        sync.wait_ge(prc.h, prn)
                        for c in range(NCH):
                            sync.dma_start(out=ig_d[i_, c],
                                           in_=accs[i_ % 2][:, c * NT:(c + 1) * NT]) \
                                .then_inc(sti.h, 16)
                    sti.n += 16 * 8

            @b.block.sync
            def _(sync, stn=sti.n):
                sync.wait_ge(sti.h, stn)



    b.barrier()

    with ExitStack() as ctxC:
        def sbC(n, s, d=F32):
            return ctxC.enter_context(nc.sbuf_tensor("fC" + n, s, d))
        wo_sb = ctxC.enter_context(nc.sbuf_tensor("fCwo", [128, 256 * 128], BF16))
        ig_sb = ctxC.enter_context(nc.sbuf_tensor("fCig", [128, 32 * NT], BF16))
        xn_n = ctxC.enter_context(nc.sbuf_tensor("fCxn", [128, 8 * NT], BF16))
        y_sb = sbC("y", [128, 8 * NT])
        y_bf = ctxC.enter_context(nc.sbuf_tensor("fCybf", [128, 8 * NT], BF16))
        ysq = ctxC.enter_context(nc.sbuf_tensor("fCysq", [128, 8 * NT], BF16))
        gvec = sbC("gv", [128, 8])
        bvec = sbC("bv", [128, 8])
        bovec = sbC("bo", [128, 8])
        ones1 = ctxC.enter_context(nc.sbuf_tensor("fCones", [128, 1], BF16))
        onescC = sbC("onescC", [1, 128])
        mu_r = sbC("mu", [1, NT])
        var_r = sbC("var", [1, NT])
        musq = sbC("musq", [1, NT])
        rstd_r = sbC("rstd", [1, NT])
        outg = sbC("out", [128, 8 * NT])
        ps_y = [ctxC.enter_context(nc.psum_tensor(f"fCpy{j}", [128, NT], F32))
                for j in range(2)]
        ps_sx = ctxC.enter_context(nc.psum_tensor("fCpsx", [1, NT], F32))
        ps_sq = ctxC.enter_context(nc.psum_tensor("fCpsq", [1, NT], F32))
        ps_mu = ctxC.enter_context(nc.psum_tensor("fCpmu", [128, NT], F32))
        ps_rs = ctxC.enter_context(nc.psum_tensor("fCprs", [128, NT], F32))
        ldw2 = b.sem("fCldw")
        ldi = b.sem("fCldi")
        ldn2 = b.sem("fCldn")
        mmy = b.sem("fCmmy")
        evy = b.sem("fCevy")
        cvy = b.sem("fCcvy")
        smm = b.sem("fCsmm")
        sve = b.sem("fCsve")
        sae = b.sem("fCsae")
        bcy = b.sem("fCbcy")
        nrm = b.sem("fCnrm")
        sil = b.sem("fCsil")
        stg = b.sem("fCstg")

        @b.block.vector
        def _(v):
            v.memset(ones1[:], 1.0)
            v.memset(onescC[:], 1.0)

        @b.block.gpsimd
        def _(g):
            for i in range(32):
                for m in range(8):
                    g.dma_start(
                        out=wo_sb[:, (i * 8 + m) * 128:(i * 8 + m + 1) * 128],
                        in_=wo_t[i, m]).then_inc(ldw2.h, 16)
            g.dma_start(out=bovec[:], in_=bor).then_inc(ldw2.h, 16)
            g.dma_start(out=gvec[:], in_=gamma0r).then_inc(ldw2.h, 16)
            g.dma_start(out=bvec[:], in_=beta0r).then_inc(ldw2.h, 16)
        ldw2.n += 16 * (256 + 3)

        for c in range(NCH):
            @b.block.sync
            def _(sync, c=c, nrn=nrm.n):
                if c > 0:
                    sync.wait_ge(nrm.h, nrn)
                for i in range(32):
                    sync.dma_start(out=ig_sb[:, i * NT:(i + 1) * NT],
                                   in_=ig_d[i, c]).then_inc(ldi.h, 16)
                for a in range(8):
                    sync.dma_start(out=xn_n[:, a * NT:(a + 1) * NT],
                                   in_=xn_c[c, a]).then_inc(ldn2.h, 16)
            ldi.n += 16 * 32
            ldn2.n += 16 * 8

            for m in range(8):
                km = c * 8 + m

                @b.block.tensor
                def _(t, c=c, m=m, km=km, ldin=ldi.n, ldwn=ldw2.n):
                    t.wait_ge(ldi.h, ldin)
                    if km == 0:
                        t.wait_ge(ldw2.h, ldwn)
                    if km >= 2:
                        t.wait_ge(evy.h, km - 1)
                    for i in range(32):
                        ii = t.matmul(
                            ps_y[m % 2][:],
                            wo_sb[:, (i * 8 + m) * 128:(i * 8 + m + 1) * 128],
                            ig_sb[:, i * NT:(i + 1) * NT],
                            start=(i == 0), stop=(i == 31))
                    ii.then_inc(mmy.h, 1)
                mmy.n += 1

                @b.block.vector
                def _(v, c=c, m=m, km=km, ldnn=ldn2.n):
                    v.wait_ge(mmy.h, km + 1)
                    v.wait_ge(ldn2.h, ldnn)
                    i = v.scalar_tensor_tensor(
                        out=y_sb[:, m * NT:(m + 1) * NT], in0=ps_y[m % 2][:],
                        scalar=bovec[:, m:m + 1], in1=xn_n[:, m * NT:(m + 1) * NT],
                        op0=ALU.add, op1=ALU.add)
                    i.then_inc(evy.h, 1)
                evy.n += 1

            @b.block.vector
            def _(v, c=c, evn=evy.n):
                v.wait_ge(evy.h, evn)
                if c > 0:
                    v.wait_ge(smm.h, c)
                i = v.tensor_copy(y_bf[:], y_sb[:])
                i.then_inc(cvy.h, 1)
            cvy.n += 1

            @b.block.scalar
            def _(s, c=c, evn=evy.n):
                s.wait_ge(evy.h, evn)
                if c > 0:
                    s.wait_ge(smm.h, c)
                s.activation(out=ysq[:], in_=y_sb[:], func=AF.Square) \
                    .then_inc(cvy.h, 1)
            cvy.n += 1

            @b.block.tensor
            def _(t, c=c, cvn=cvy.n, nrn=nrm.n):
                t.wait_ge(cvy.h, cvn)
                if c > 0:
                    t.wait_ge(nrm.h, nrn)
                for a in range(8):
                    t.matmul(ps_sx[:], ones1[:], y_bf[:, a * NT:(a + 1) * NT],
                             start=(a == 0), stop=(a == 7))
                for a in range(8):
                    i = t.matmul(ps_sq[:], ones1[:], ysq[:, a * NT:(a + 1) * NT],
                                 start=(a == 0), stop=(a == 7))
                i.then_inc(smm.h, 1)
            smm.n += 1

            @b.block.vector
            def _(v, c=c, svb=sve.n):
                v.wait_ge(smm.h, c + 1)
                v.tensor_scalar(out=mu_r[:], in0=ps_sx[:], scalar1=1.0 / D,
                                scalar2=None, op0=ALU.mult)
                v.tensor_scalar(out=var_r[:], in0=ps_sq[:], scalar1=1.0 / D,
                                scalar2=1e-12, op0=ALU.mult, op1=ALU.add)
                v.tensor_tensor(out=musq[:], in0=mu_r[:], in1=mu_r[:],
                                op=ALU.mult).then_inc(sve.h, 1)
                v.wait_ge(sve.h, svb + 1)
                v.tensor_tensor(out=var_r[:], in0=var_r[:], in1=musq[:],
                                op=ALU.subtract).then_inc(sve.h, 1)
            sve.n += 2

            @b.block.scalar
            def _(s, c=c, svn=sve.n, sab=sae.n):
                s.wait_ge(sve.h, svn)
                s.activation(out=rstd_r[:], in_=var_r[:], func=AF.Ln) \
                    .then_inc(sae.h, 1)
                s.wait_ge(sae.h, sab + 1)
                s.activation(out=rstd_r[:], in_=rstd_r[:], func=AF.Exp,
                             scale=-0.5).then_inc(sae.h, 2)
            sae.n += 3

            @b.block.tensor
            def _(t, c=c, san=sae.n):
                t.wait_ge(sae.h, san)
                t.matmul(ps_mu[:], onescC[:], mu_r[:], start=True, stop=True)
                t.matmul(ps_rs[:], onescC[:], rstd_r[:], start=True, stop=True) \
                    .then_inc(bcy.h, 1)
            bcy.n += 1

            @b.block.vector
            def _(v, c=c, bn=bcy.n, stn=stg.n):
                v.wait_ge(bcy.h, bn)
                if c > 0:
                    v.wait_ge(stg.h, stn)
                for m in range(8):
                    sl = slice(m * NT, (m + 1) * NT)
                    v.tensor_tensor(out=outg[:, sl], in0=y_sb[:, sl],
                                    in1=ps_mu[:], op=ALU.subtract)
                for m in range(8):
                    sl = slice(m * NT, (m + 1) * NT)
                    v.tensor_tensor(out=outg[:, sl], in0=outg[:, sl],
                                    in1=ps_rs[:], op=ALU.mult)
                for m in range(8):
                    sl = slice(m * NT, (m + 1) * NT)
                    i = v.tensor_scalar(out=outg[:, sl], in0=outg[:, sl],
                                        scalar1=gvec[:, m:m + 1],
                                        scalar2=bvec[:, m:m + 1],
                                        op0=ALU.mult, op1=ALU.add)
                i.then_inc(nrm.h, 1)
            nrm.n += 1

            @b.block.scalar
            def _(s, c=c, nrn=nrm.n):
                s.wait_ge(nrm.h, nrn)
                i = s.activation(out=outg[:], in_=outg[:], func=AF.Silu)
                i.then_inc(sil.h, 1)
            sil.n += 1

            @b.block.sync
            def _(sync, c=c, sln=sil.n):
                sync.wait_ge(sil.h, sln)
                for m in range(8):
                    sync.dma_start(out=g0_c[c, m],
                                   in_=outg[:, m * NT:(m + 1) * NT]) \
                        .then_inc(stg.h, 16)
            stg.n += 128

        @b.block.sync
        def _(sync, stn=stg.n):
            sync.wait_ge(stg.h, stn)


def emit_prep(b: B):
    nc = b.nc
    xn0T = b.dram["xn0T"]
    xn_c = xn0T.rearrange("(a p) (c n) -> c a p n", p=128, n=NT)
    ckvT_c = b.dram["ckvT"].rearrange("(m p) (c n) -> c m p n", p=128, n=NT)
    kiT_c = b.dram["kiT"].rearrange("p (c n) -> c p n", n=NT)
    kcat_r = b.dram["kcat"][0:T, :].rearrange("(tm p) f -> tm p f", p=128)
    wcat_r = b.dram["wcatT"].rearrange("(h m p) (nn q) -> h m nn p q", h=8, p=128, q=NT)

    with ExitStack() as ctx:
        def sb(n, s, d=BF16):
            return ctx.enter_context(nc.sbuf_tensor("pp" + n, s, d))
        xn_sb = sb("xn", [128, 8 * T])
        wdkv = sb("wdkv", [128, 8 * 256])
        wkr = sb("wkr", [128, 8 * 32])
        wik = sb("wik", [128, 8 * 64])
        wuvh = sb("wuvh", [64, 8 * 256])
        wout_h = sb("wouth", [64, 1024])
        c63 = sb("c63", [16, T], F32)
        s63 = sb("s63", [16, T], F32)
        c0t = sb("c0t", [16, 64], F32)
        s0t = sb("s0t", [16, 64], F32)
        kmt = sb("kmt", [32, T], F32)
        km2t = sb("km2t", [16, T], F32)
        sps = sb("sps", [32, NT], F32)
        onesb = sb("onesb", [32, 1], F32)
        gko = sb("gko", [16, T])
        gkoa = sb("gkoa", [16, 64])
        t16 = sb("t16", [16, T], F32)
        t16b = sb("t16b", [16, T], F32)
        ident32 = sb("id32", [32, 32], F32)
        ones32 = sb("on32", [32, 32], F32)
        ckv_ev = sb("ckvev", [128, NT])
        ki_ev = sb("kiev", [64, NT])
        kcat_t = sb("kct", [128, 320])
        wc_ev = sb("wcev", [128, NT])
        ps_ck = ctx.enter_context(nc.psum_tensor("ppPc", [128, NT], F32))
        ps_ki = ctx.enter_context(nc.psum_tensor("ppPk", [64, NT], F32))
        ps_kr = ctx.enter_context(nc.psum_tensor("ppPr", [32, NT], F32))
        ps_tm = [ctx.enter_context(nc.psum_tensor(f"ppPt{j}", [128, 288], F32))
                 for j in range(2)]
        ps_wc = ctx.enter_context(nc.psum_tensor("ppPw", [128, NT], F32))
        ldx = b.sem("ppldx")
        ldw = b.sem("ppldw")
        ldt = b.sem("ppldt")
        mme = b.sem("ppmme")
        eve = b.sem("ppeve")
        spd = b.sem("ppspd")
        stc = b.sem("ppstc")
        kmd = b.sem("ppkmd")
        gkd = b.sem("ppgkd")

        @b.block.sync
        def _(sync):
            for a in range(8):
                for c in range(NCH):
                    sync.dma_start(out=xn_sb[:, a * T + c * NT: a * T + (c + 1) * NT],
                                   in_=xn_c[c, a]).then_inc(ldx.h, 16)
        ldx.n += 16 * 64

        @b.block.gpsimd
        def _(g):
            for a in range(8):
                g.dma_start(out=wdkv[:, a * 256:(a + 1) * 256],
                            in_=b.dram["w_dkv"][a * 128:(a + 1) * 128, :]) \
                    .then_inc(ldw.h, 16)
                g.dma_start(out=wkr[:, a * 32:(a + 1) * 32],
                            in_=b.dram["w_kr"][a * 128:(a + 1) * 128, :]) \
                    .then_inc(ldw.h, 16)
                g.dma_start(out=wik[:, a * 64:(a + 1) * 64],
                            in_=b.dram["idx_wk"][a * 128:(a + 1) * 128, :]) \
                    .then_inc(ldw.h, 16)
            for h in range(8):
                g.dma_start(out=wuvh[:, h * 256:(h + 1) * 256],
                            in_=b.dram["w_uvTh"][h]).then_inc(ldw.h, 16)
            g.dma_start(out=c63[:], in_=b.dram["c63"]).then_inc(ldt.h, 16)
            g.dma_start(out=s63[:], in_=b.dram["s63"]).then_inc(ldt.h, 16)
            g.dma_start(out=c0t[:], in_=b.dram["c0"]).then_inc(ldt.h, 16)
            g.dma_start(out=s0t[:], in_=b.dram["s0"]).then_inc(ldt.h, 16)
        ldw.n += 16 * 32
        ldt.n += 64

        @b.block.vector
        def _(v):
            v.memset(onesb[:], 1.0)
            v.memset(ones32[:], 1.0)
            v.memset(kcat_t[:], 0.0)

        @b.block.gpsimd
        def _(g):
            g.affine_select(out=ident32[:], in_=ones32[:], pattern=[[1, 32]],
                            compare_op=ALU.is_equal, fill=0.0, base=0,
                            channel_multiplier=-1).then_inc(ldt.h, 1)
        ldt.n += 1

        for c in range(NCH):
            for mc in range(2):
                @b.block.tensor
                def _(t, c=c, mc=mc, en=eve.n):
                    if c == 0 and mc == 0:
                        t.wait_ge(ldx.h, ldx.n)
                        t.wait_ge(ldw.h, 16 * 24)
                    if c + mc > 0:
                        t.wait_ge(eve.h, en)
                    for a in range(8):
                        i = t.matmul(ps_ck[:],
                                     wdkv[:, a * 256 + mc * 128: a * 256 + (mc + 1) * 128],
                                     xn_sb[:, a * T + c * NT: a * T + (c + 1) * NT],
                                     start=(a == 0), stop=(a == 7))
                    i.then_inc(mme.h, 1)
                mme.n += 1

                @b.block.vector
                def _(v, c=c, mc=mc, mn=mme.n, sn=stc.n):
                    v.wait_ge(mme.h, mn)
                    if c + mc > 0:
                        v.wait_ge(stc.h, sn)
                    v.tensor_copy(ckv_ev[:], ps_ck[:]).then_inc(eve.h, 1)
                eve.n += 1

                @b.block.sync
                def _(sync, c=c, mc=mc, en=eve.n):
                    sync.wait_ge(eve.h, en)
                    sync.dma_start(out=ckvT_c[c, mc], in_=ckv_ev[:]).then_inc(stc.h, 16)
                stc.n += 16

            @b.block.tensor
            def _(t, c=c, en=eve.n):
                if c > 0:
                    t.wait_ge(eve.h, en - 1)
                for a in range(8):
                    i = t.matmul(ps_ki[:], wik[:, a * 64:(a + 1) * 64],
                                 xn_sb[:, a * T + c * NT: a * T + (c + 1) * NT],
                                 start=(a == 0), stop=(a == 7))
                i.then_inc(mme.h, 1)
            mme.n += 1

            @b.block.vector
            def _(v, c=c, mn=mme.n, sn=stc.n):
                v.wait_ge(mme.h, mn)
                if c > 0:
                    v.wait_ge(stc.h, sn - 16)
                v.tensor_copy(ki_ev[:], ps_ki[:]).then_inc(eve.h, 1)
            eve.n += 1

            @b.block.sync
            def _(sync, c=c, en=eve.n):
                sync.wait_ge(eve.h, en)
                sync.dma_start(out=kiT_c[c], in_=ki_ev[:]).then_inc(stc.h, 16)
            stc.n += 16

            @b.block.tensor
            def _(t, c=c, spn=spd.n):
                if c > 0:
                    t.wait_ge(spd.h, spn)
                for a in range(8):
                    i = t.matmul(ps_kr[:], wkr[:, a * 32:(a + 1) * 32],
                                 xn_sb[:, a * T + c * NT: a * T + (c + 1) * NT],
                                 start=(a == 0), stop=(a == 7))
                i.then_inc(mme.h, 1)
            mme.n += 1

            @b.block.scalar
            def _(s, c=c, mn=mme.n, spb=spd.n):
                s.wait_ge(mme.h, mn)
                s.activation(out=sps[:], in_=ps_kr[:], func=AF.Abs).then_inc(spd.h, 1)
                s.wait_ge(spd.h, spb + 1)
                s.activation(out=sps[:], in_=sps[:], func=AF.Exp, scale=-1.0) \
                    .then_inc(spd.h, 1)
                s.wait_ge(spd.h, spb + 2)
                s.activation(out=sps[:], in_=sps[:], func=AF.Ln, bias=onesb[:]) \
                    .then_inc(spd.h, 1)
            spd.n += 3

            @b.block.vector
            def _(v, c=c, spn=spd.n):
                v.wait_ge(spd.h, spn)
                i = v.scalar_tensor_tensor(out=kmt[:, c * NT:(c + 1) * NT],
                                           in0=ps_kr[:], scalar=0.0, in1=sps[:],
                                           op0=ALU.max, op1=ALU.add)
                i.then_inc(spd.h, 1)
            spd.n += 1

        @b.block.gpsimd
        def _(g, spn=spd.n):
            g.wait_ge(spd.h, spn)
            g.dma_start(out=km2t[:], in_=kmt[16:32, :]).then_inc(kmd.h, 16)
        kmd.n += 16

        @b.block.vector
        def _(v, kn=kmd.n, gb=gkd.n):
            v.wait_ge(kmd.h, kn)
            v.wait_ge(ldt.h, 65)
            v.tensor_tensor(out=t16[:], in0=kmt[0:16, :], in1=c63[:], op=ALU.mult)
            v.tensor_tensor(out=t16b[:], in0=km2t[:], in1=s63[:], op=ALU.mult)
            v.tensor_tensor(out=gko[:], in0=t16[:], in1=t16b[:], op=ALU.subtract) \
                .then_inc(gkd.h, 1)
        gkd.n += 1

        @b.block.sync
        def _(sync, gn=gkd.n):
            sync.wait_ge(gkd.h, gn)
            sync.dma_start(out=b.dram["gkb1T"], in_=gko[:]).then_inc(stc.h, 16)
        stc.n += 16

        @b.block.vector
        def _(v, sn=stc.n):
            v.wait_ge(stc.h, sn)
            v.tensor_tensor(out=t16[:], in0=kmt[0:16, :], in1=s63[:], op=ALU.mult)
            v.tensor_tensor(out=t16b[:], in0=km2t[:], in1=c63[:], op=ALU.mult)
            v.tensor_tensor(out=gko[:], in0=t16[:], in1=t16b[:], op=ALU.add) \
                .then_inc(gkd.h, 1)
        gkd.n += 1

        @b.block.sync
        def _(sync, gn=gkd.n):
            sync.wait_ge(gkd.h, gn)
            sync.dma_start(out=b.dram["gkb2T"], in_=gko[:]).then_inc(stc.h, 16)
        stc.n += 16

        @b.block.vector
        def _(v, sn=stc.n):
            v.wait_ge(stc.h, sn)
            v.tensor_tensor(out=t16[:, 0:64], in0=kmt[0:16, 0:64], in1=c0t[:],
                            op=ALU.mult)
            v.tensor_tensor(out=t16b[:, 0:64], in0=km2t[:, 0:64], in1=s0t[:],
                            op=ALU.mult)
            v.tensor_tensor(out=gkoa[:], in0=t16[:, 0:64], in1=t16b[:, 0:64],
                            op=ALU.subtract).then_inc(gkd.h, 1)
        gkd.n += 1

        @b.block.sync
        def _(sync, gn=gkd.n):
            sync.wait_ge(gkd.h, gn)
            sync.dma_start(out=b.dram["gka1T"], in_=gkoa[:]).then_inc(stc.h, 16)
        stc.n += 16

        @b.block.vector
        def _(v, sn=stc.n):
            v.wait_ge(stc.h, sn)
            v.tensor_tensor(out=t16[:, 0:64], in0=kmt[0:16, 0:64], in1=s0t[:],
                            op=ALU.mult)
            v.tensor_tensor(out=t16b[:, 0:64], in0=km2t[:, 0:64], in1=c0t[:],
                            op=ALU.mult)
            v.tensor_tensor(out=gkoa[:], in0=t16[:, 0:64], in1=t16b[:, 0:64],
                            op=ALU.add).then_inc(gkd.h, 1)
        gkd.n += 1

        @b.block.sync
        def _(sync, gn=gkd.n):
            sync.wait_ge(gkd.h, gn)
            sync.dma_start(out=b.dram["gka2T"], in_=gkoa[:]).then_inc(stc.h, 16)
        stc.n += 16

        tmev = b.sem("pptmev")
        tmst = b.sem("pptmst")
        for tm in range(32):
            @b.block.tensor
            def _(t, tm=tm, en=tmev.n, spn=spd.n, ltn=ldt.n):
                if tm == 0:
                    t.wait_ge(spd.h, spn)
                    t.wait_ge(ldt.h, ltn)
                if tm >= 2:
                    t.wait_ge(tmev.h, en - 1)
                for a in range(8):
                    t.matmul(ps_tm[tm % 2][:, 0:256],
                             xn_sb[:, a * T + tm * 128: a * T + (tm + 1) * 128],
                             wdkv[:, a * 256:(a + 1) * 256],
                             start=(a == 0), stop=(a == 7))
                i = t.transpose(ps_tm[tm % 2][:, 256:288],
                                kmt[:, tm * 128:(tm + 1) * 128], ident32[:])
                i.then_inc(mme.h, 1)
            mme.n += 1

            @b.block.vector
            def _(v, tm=tm, mn=mme.n, sn=tmst.n):
                v.wait_ge(mme.h, mn)
                if tm >= 1:
                    v.wait_ge(tmst.h, sn)
                v.tensor_copy(kcat_t[:, 0:256], ps_tm[tm % 2][:, 0:256])
                v.tensor_copy(kcat_t[:, 256:272], ps_tm[tm % 2][:, 256:272])
                i = v.tensor_copy(kcat_t[:, 288:304], ps_tm[tm % 2][:, 272:288])
                i.then_inc(tmev.h, 1)
            tmev.n += 1

            @b.block.sync
            def _(sync, tm=tm, en=tmev.n):
                sync.wait_ge(tmev.h, en)
                sync.dma_start(out=kcat_r[tm], in_=kcat_t[:]).then_inc(tmst.h, 16)
            tmst.n += 16

        wch = b.sem("ppwch")
        for h in range(8):
            @b.block.gpsimd
            def _(g, h=h, wn=mme.n):
                if h > 0:
                    g.wait_ge(mme.h, wn)
                g.dma_start(out=wout_h[:],
                            in_=b.dram["w_outm"][h * 64:(h + 1) * 64, :]) \
                    .then_inc(ldw.h, 16)
            ldw.n += 16

            for mc2 in range(2):
                for nn in range(2):
                    @b.block.tensor
                    def _(t, h=h, mc2=mc2, nn=nn, ldn=ldw.n, en=eve.n):
                        t.wait_ge(ldw.h, ldn)
                        if h + mc2 + nn > 0:
                            t.wait_ge(eve.h, en)
                        i = t.matmul(ps_wc[:],
                                     wuvh[:, h * 256 + mc2 * 128: h * 256 + (mc2 + 1) * 128],
                                     wout_h[:, nn * NT:(nn + 1) * NT],
                                     start=True, stop=True)
                        i.then_inc(mme.h, 1)
                    mme.n += 1

                    @b.block.vector
                    def _(v, h=h, mc2=mc2, nn=nn, mn=mme.n, sn=stc.n):
                        v.wait_ge(mme.h, mn)
                        if h + mc2 + nn > 0:
                            v.wait_ge(stc.h, sn)
                        v.tensor_copy(wc_ev[:], ps_wc[:]).then_inc(eve.h, 1)
                    eve.n += 1

                    @b.block.sync
                    def _(sync, h=h, mc2=mc2, nn=nn, en=eve.n):
                        sync.wait_ge(eve.h, en)
                        sync.dma_start(out=wcat_r[h, mc2, nn], in_=wc_ev[:]) \
                            .then_inc(stc.h, 16)
                    stc.n += 16

        @b.block.sync
        def _(sync, tn=tmev.n):
            sync.wait_ge(tmev.h, tn)

        @b.block.vector
        def _(v, tn=tmst.n):
            v.wait_ge(tmst.h, tn)
            v.memset(kcat_t[:, :], 0.0).then_inc(tmev.h, 1)
        tmev.n += 1

        @b.block.sync
        def _(sync, tn=tmev.n):
            sync.wait_ge(tmev.h, tn)
            sync.dma_start(out=b.dram["kcat"][T:T + 16, :], in_=kcat_t[0:16, :]) \
                .then_inc(tmst.h, 16)
        tmst.n += 16

        @b.block.sync
        def _(sync, sn=stc.n, tn=tmst.n):
            sync.wait_ge(stc.h, sn)
            sync.wait_ge(tmst.h, tn)



L0ATT_TILES = 32
L0ATT_STAGE = 9


def emit_l0att(b: B):
    NTIL = L0ATT_TILES
    STG = L0ATT_STAGE
    nc = b.nc
    xn_r = b.dram["xn0T"].rearrange("(a p) t -> a p t", p=128)
    xT_r = b.dram["xT"].rearrange("(a p) t -> a p t", p=128)
    g0_r = b.dram["g0T"].rearrange("(a p) t -> a p t", p=128)
    x1_r = b.dram["x1T"].rearrange("(a p) t -> a p t", p=128)
    ckvT_r = b.dram["ckvT"].rearrange("(m p) t -> m p t", p=128)
    kcat = b.dram["kcat"]
    idx_d = b.dram["idx_d"]
    GL = 16
    TP = T + 16

    with ExitStack() as ctx:
        def sb(n, s, d=BF16):
            return ctx.enter_context(nc.sbuf_tensor("at" + n, s, d))
        ckvb = sb("ckvb", [128, 2 * 256])
        ki_sb = sb("ki", [64, T])
        gkwP = sb("gkwP", [128, 256])
        gkaP = sb("gkaP", [128, 64])
        wiq = sb("wiq", [128, 8 * 256])
        wdq = sb("wdq", [128, 8 * 512])
        wuq = sb("wuq", [128, 4 * 512])
        wqr = sb("wqr", [128, 4 * 256])
        wukh = sb("wukh", [64, 8 * 256])
        wcat_sb = sb("wcat", [128, 16 * 1024])
        identb = sb("idb", [128, 128])
        maskd = sb("mskd", [128, 512], F32)
        maskb = sb("mskb", [128, 80], F32)
        crot_rep = sb("crotr", [128, 512])
        srot_rep = sb("srotr", [128, 512])
        crot = sb("crot", [128, 16])
        srot = sb("srot", [128, 16])
        nfin0 = sb("nfin", [128, 1], F32)
        iotaj = sb("ioj", [128, 32], F32)
        fill64 = sb("f64", [128, 32], U32)
        idxw_b = sb("idxw", [128, 4], F32)
        ones1 = sb("on1", [128, 1], F32)
        xn_t = sb("xnt", [128, 8 * 128])
        ctr = sb("ctr", [128, 128], F32)
        str_ = sb("stru", [128, 128], F32)
        c2r = sb("c2r", [128, 128], F32)
        s2r = sb("s2r", [128, 128], F32)
        ct4 = sb("ct4", [128, 4 * 16], F32)
        qi64 = sb("qi", [64, 4 * 128])
        cq_sb = sb("cq", [128, 4 * 128])
        quc64 = sb("quc", [64, 8 * 128])
        qmu = sb("qmu", [128, 256], F32)
        fsc1 = sb("fsc1", [128, 128], F32)
        fsc2 = sb("fsc2", [128, 128], F32)
        ftile = sb("ft", [128, 4 * 128])
        qabs = sb("qabs", [128, 2 * 1024])
        qfPa = sb("qfPa", [128, 1024])
        qfPb = sb("qfPb", [128, 1024])
        I0 = ctx.enter_context(nc.sbuf_tensor("atI0", [128, T], F32))
        I1 = ctx.enter_context(nc.sbuf_tensor("atI1", [128, T], F32))
        itmp = sb("itmp", [128, 512], F32)
        m8x = sb("m8x", [128, 8], F32)
        idxall = sb("idxa", [128, 32], U32)
        selm2 = sb("selm", [128, 32], U8)
        idxg = sb("idxg", [128, 32], U32)
        kg = sb("kg", [128, 32 * 320])
        mtmp1 = sb("mt1", [128, 512])
        mtmp2 = sb("mt2", [128, 512])
        r1c = sb("r1c", [128, 512])
        kgt1 = sb("kgt1", [128, T])
        kgt2 = sb("kgt2", [128, T])
        krotP = sb("krP", [128, T])
        sdyn = sb("sdyn", [128, 512], F32)
        sband = sb("sbnd", [128, 80], F32)
        mrow = sb("mrow", [128, 1], F32)
        mrow2 = sb("mrw2", [128, 1], F32)
        negm = sb("negm", [128, 1], F32)
        z1 = sb("z1", [128, 1], F32)
        z2 = sb("z2", [128, 1], F32)
        zsum = sb("zsum", [128, 1], F32)
        zinv = sb("zinv", [128, 1], F32)
        pd_sb = sb("pd", [128, 512])
        pb_sb = sb("pb", [128, 80])
        pt_sb = sb("pt", [128, 4 * 128])
        ptb = sb("ptb", [80, 128])
        band_sb = sb("band", [80, 256])
        oc_sb = sb("oc", [128, 2 * 1024])
        xg_sb = sb("xg", [128, 2 * 128], F32)
        x1o = sb("x1o", [128, 128], F32)
        ps_q = ctx.enter_context(nc.psum_tensor("atPq", [128, 512], F32))
        ps_I = ctx.enter_context(nc.psum_tensor("atPI", [128, 512], F32))
        ps_sc = ctx.enter_context(nc.psum_tensor("atPs", [128, 512], F32))
        ps_bd = ctx.enter_context(nc.psum_tensor("atPbd", [128, 80], F32))
        ps_t1 = ctx.enter_context(nc.psum_tensor("atPt1", [128, 128], BF16))
        ps_t2 = ctx.enter_context(nc.psum_tensor("atPt2", [128, 128], BF16))
        ps_av = ctx.enter_context(nc.psum_tensor("atPav", [128, 256], F32))
        ps_ep = ctx.enter_context(nc.psum_tensor("atPep", [128, 128], F32))

        ldp = b.sem("atldp")
        ldn = b.sem("atldn")
        qmm = b.sem("atqmm")
        qev = b.sem("atqev")
        sfp = b.sem("atsfp")
        iev = b.sem("atiev")
        ext = b.sem("atext")
        gpd = b.sem("atgpd")
        idd = b.sem("atidd")
        mrt = b.sem("atmrt")
        tev = b.sem("attev")
        smx = b.sem("atsmx")
        sxe = b.sem("atsxe")
        ave = b.sem("atave")
        bnd = b.sem("atbnd")
        cmb = b.sem("atcmb")
        xst = b.sem("atxst")
        dbgs = b.sem("atdbgs")

        @b.block.sync
        def _(sync):
            sync.dma_start(out=ki_sb[:], in_=b.dram["kiT"]).then_inc(ldp.h, 16)
            sync.dma_start(out=identb[:], in_=b.dram["identbf"]).then_inc(ldp.h, 16)
            for hc in range(16):
                sync.dma_start(out=wcat_sb[:, hc * 1024:(hc + 1) * 1024],
                               in_=b.dram["wcatT"][hc * 128:(hc + 1) * 128, :]) \
                    .then_inc(ldp.h, 16)
        ldp.n += 16 * 18

        @b.block.gpsimd
        def _(g):
            for a in range(8):
                g.dma_start(out=wiq[:, a * 256:(a + 1) * 256],
                            in_=b.dram["idx_wq"][a * 128:(a + 1) * 128, :]) \
                    .then_inc(ldp.h, 16)
                g.dma_start(out=wdq[:, a * 512:(a + 1) * 512],
                            in_=b.dram["w_dq"][a * 128:(a + 1) * 128, :]) \
                    .then_inc(ldp.h, 16)
            for a in range(4):
                g.dma_start(out=wuq[:, a * 512:(a + 1) * 512],
                            in_=b.dram["w_uq"][a * 128:(a + 1) * 128, :]) \
                    .then_inc(ldp.h, 16)
                g.dma_start(out=wqr[:, a * 256:(a + 1) * 256],
                            in_=b.dram["w_qr"][a * 128:(a + 1) * 128, :]) \
                    .then_inc(ldp.h, 16)
            for h in range(8):
                g.dma_start(out=wukh[:, h * 256:(h + 1) * 256],
                            in_=b.dram["w_ukTh"][h]).then_inc(ldp.h, 16)
            g.dma_start(out=maskd[:], in_=b.dram["maskdyn"]).then_inc(ldp.h, 16)
            g.dma_start(out=maskb[:], in_=b.dram["maskband"]).then_inc(ldp.h, 16)
            g.dma_start(out=crot[:], in_=b.dram["crot"]).then_inc(ldp.h, 16)
            g.dma_start(out=srot[:], in_=b.dram["srot"]).then_inc(ldp.h, 16)
            g.dma_start(out=nfin0[:], in_=b.dram["nfin0"]).then_inc(ldp.h, 16)
            g.dma_start(out=iotaj[:], in_=b.dram["iotaj"]).then_inc(ldp.h, 16)
            g.dma_start(out=fill64[:], in_=b.dram["fill64"]).then_inc(ldp.h, 16)
            g.dma_start(out=idxw_b[:], in_=b.dram["idxwb"]).then_inc(ldp.h, 16)
        ldp.n += 16 * 36

        @b.block.vector
        def _(v, lp=ldp.n):
            v.memset(ones1[:], 1.0)
            v.memset(qfPa[:], 0.0)
            v.memset(qfPb[:], 0.0)
            v.memset(krotP[:], 0.0)
            v.memset(gkaP[:], 0.0)
            v.wait_ge(ldp.h, lp)
            for cc in range(32):
                v.tensor_copy(crot_rep[:, cc * 16:(cc + 1) * 16], crot[:])
                i = v.tensor_copy(srot_rep[:, cc * 16:(cc + 1) * 16], srot[:])
            i.then_inc(ldp.h, 1)
        ldp.n += 1

        @b.block.sync
        def _(sync, lp=ldp.n):
            sync.wait_ge(ldp.h, lp)
            sync.dma_start(out=gkaP[0:16, :], in_=b.dram["gka1T"]).then_inc(ldp.h, 16)
            sync.dma_start(out=gkaP[32:48, :], in_=b.dram["gka2T"]).then_inc(ldp.h, 16)
        ldp.n += 32

        for ti in range(NTIL):
            t0 = ti * 128
            WS = min((t0 + 64 + NT - 1) // NT, NCH)
            tile0 = (ti == 0)

            qmm_prev = qmm.n
            gpd_prev = gpd.n
            idd_prev = idd.n

            @b.block.sync
            def _(sync, ti=ti, t0=t0, qp=qmm_prev):
                if ti > 0:
                    sync.wait_ge(qmm.h, qp)
                for a in range(8):
                    sync.dma_start(out=xn_t[:, a * 128:(a + 1) * 128],
                                   in_=xn_r[a][:, t0:t0 + 128]).then_inc(ldn.h, 16)
                for (j, nmk) in enumerate(("ct_tm", "st_tm", "c2t_tm", "s2t_tm")):
                    sync.dma_start(out=ct4[:, j * 16:(j + 1) * 16],
                                   in_=b.dram[nmk][t0:t0 + 128, :]).then_inc(ldn.h, 16)
            ldn.n += 16 * 12

            wbase = max(t0 - 63, 0)
            wid = min(192, T - wbase)

            @b.block.vector
            def _(v, ti=ti, qp=qmm_prev, bz=bnd.n):
                if ti > 0:
                    v.wait_ge(qmm.h, qp)
                v.memset(ckvb[:], 0.0)
                v.memset(gkwP[:], 0.0).then_inc(bnd.h, 1)
            bnd.n += 1

            @b.block.sync
            def _(sync, ti=ti, wbase=wbase, wid=wid, bz=bnd.n):
                sync.wait_ge(bnd.h, bz)
                for mc in range(2):
                    sync.dma_start(out=ckvb[:, mc * 256: mc * 256 + wid],
                                   in_=ckvT_r[mc][:, wbase:wbase + wid]) \
                        .then_inc(bnd.h, 16)
                sync.dma_start(out=gkwP[0:16, 0:wid],
                               in_=b.dram["gkb1T"][:, wbase:wbase + wid]) \
                    .then_inc(bnd.h, 16)
                sync.dma_start(out=gkwP[32:48, 0:wid],
                               in_=b.dram["gkb2T"][:, wbase:wbase + wid]) \
                    .then_inc(bnd.h, 16)
            bnd.n += 64

            for (dst, wt, nmc, kk, wstep) in (
                    ("qi", wiq, 2, 8, 256), ("cq", wdq, 4, 8, 512)):
                for mc in range(nmc):
                    @b.block.tensor
                    def _(t, ti=ti, dst=dst, wt=wt, mc=mc, kk=kk, wstep=wstep,
                          qn=qev.n, ldnn=ldn.n, lp=ldp.n):
                        t.wait_ge(ldn.h, ldnn)
                        if ti == 0 and dst == "qi" and mc == 0:
                            t.wait_ge(ldp.h, lp)
                        if qn > 0:
                            t.wait_ge(qev.h, qn)
                        for a in range(kk):
                            i = t.matmul(
                                ps_q[:, 0:128],
                                wt[:, a * wstep + mc * 128: a * wstep + (mc + 1) * 128],
                                xn_t[:, a * 128:(a + 1) * 128],
                                start=(a == 0), stop=(a == kk - 1))
                        i.then_inc(qmm.h, 1)
                    qmm.n += 1

                    @b.block.vector
                    def _(v, dst=dst, mc=mc, qn=qmm.n):
                        v.wait_ge(qmm.h, qn)
                        if dst == "qi":
                            v.tensor_copy(qi64[:, (2 * mc) * 128:(2 * mc + 1) * 128],
                                          ps_q[0:64, 0:128])
                            v.tensor_copy(qi64[:, (2 * mc + 1) * 128:(2 * mc + 2) * 128],
                                          ps_q[64:128, 0:128]).then_inc(qev.h, 1)
                        else:
                            v.tensor_copy(cq_sb[:, mc * 128:(mc + 1) * 128],
                                          ps_q[:, 0:128]).then_inc(qev.h, 1)
                    qev.n += 1

            for mc in range(4):
                @b.block.tensor
                def _(t, mc=mc, qn=qev.n):
                    t.wait_ge(qev.h, qn)
                    for a in range(4):
                        i = t.matmul(
                            ps_q[:, 0:128],
                            wuq[:, a * 512 + mc * 128: a * 512 + (mc + 1) * 128],
                            cq_sb[:, a * 128:(a + 1) * 128],
                            start=(a == 0), stop=(a == 3))
                    i.then_inc(qmm.h, 1)
                qmm.n += 1

                @b.block.vector
                def _(v, mc=mc, qn=qmm.n):
                    v.wait_ge(qmm.h, qn)
                    v.tensor_copy(quc64[:, (2 * mc) * 128:(2 * mc + 1) * 128],
                                  ps_q[0:64, 0:128])
                    v.tensor_copy(quc64[:, (2 * mc + 1) * 128:(2 * mc + 2) * 128],
                                  ps_q[64:128, 0:128]).then_inc(qev.h, 1)
                qev.n += 1

            @b.block.tensor
            def _(t, ti=ti, qn=qev.n, sn=sfp.n):
                t.wait_ge(qev.h, qn)
                if ti > 0:
                    t.wait_ge(sfp.h, sn)
                for a in range(4):
                    i = t.matmul(ps_q[:, 0:256], cq_sb[:, a * 128:(a + 1) * 128],
                                 wqr[:, a * 256:(a + 1) * 256],
                                 start=(a == 0), stop=(a == 3))
                i.then_inc(qmm.h, 1)
            qmm.n += 1

            @b.block.scalar
            def _(s, qn=qmm.n, sb_=sfp.n):
                s.wait_ge(qmm.h, qn)
                s.activation(out=qmu[:], in_=ps_q[:, 0:256], func=AF.Abs) \
                    .then_inc(sfp.h, 1)
                s.wait_ge(sfp.h, sb_ + 1)
                s.activation(out=qmu[:], in_=qmu[:], func=AF.Exp, scale=-1.0) \
                    .then_inc(sfp.h, 1)
                s.wait_ge(sfp.h, sb_ + 2)
                s.activation(out=qmu[:], in_=qmu[:], func=AF.Ln, bias=ones1[:]) \
                    .then_inc(sfp.h, 1)
            sfp.n += 3

            @b.block.vector
            def _(v, sn=sfp.n, ldnn=ldn.n):
                v.wait_ge(sfp.h, sn)
                i = v.scalar_tensor_tensor(out=qmu[:], in0=ps_q[:, 0:256],
                                           scalar=0.0, in1=qmu[:],
                                           op0=ALU.max, op1=ALU.add)
                i.then_inc(sfp.h, 1)
                v.wait_ge(ldn.h, ldnn)
                for h in range(8):
                    v.tensor_copy(ctr[:, h * 16:(h + 1) * 16], ct4[:, 0:16])
                    v.tensor_copy(str_[:, h * 16:(h + 1) * 16], ct4[:, 16:32])
                    v.tensor_copy(c2r[:, h * 16:(h + 1) * 16], ct4[:, 32:48])
                    i = v.tensor_copy(s2r[:, h * 16:(h + 1) * 16], ct4[:, 48:64])
                i.then_inc(sfp.h, 1)
                for (fi, c_, s_) in ((0, ctr, str_), (2, c2r, s2r)):
                    v.tensor_tensor(out=fsc1[:], in0=qmu[:, 0:128], in1=c_[:],
                                    op=ALU.mult)
                    v.tensor_tensor(out=fsc2[:], in0=qmu[:, 128:256], in1=s_[:],
                                    op=ALU.mult)
                    v.tensor_tensor(out=ftile[:, fi * 128:(fi + 1) * 128],
                                    in0=fsc1[:], in1=fsc2[:], op=ALU.subtract)
                    v.tensor_tensor(out=fsc1[:], in0=qmu[:, 0:128], in1=s_[:],
                                    op=ALU.mult)
                    v.tensor_tensor(out=fsc2[:], in0=qmu[:, 128:256], in1=c_[:],
                                    op=ALU.mult)
                    i = v.tensor_tensor(out=ftile[:, (fi + 1) * 128:(fi + 2) * 128],
                                        in0=fsc1[:], in1=fsc2[:], op=ALU.add)
                i.then_inc(sfp.h, 1)
            sfp.n += 3

            for h in range(8):
                for mc in range(2):
                    @b.block.tensor
                    def _(t, h=h, mc=mc, qn=qev.n, sn=sfp.n):
                        t.wait_ge(qev.h, qn)
                        if h == 0 and mc == 0:
                            t.wait_ge(sfp.h, sn)
                        i = t.matmul(
                            ps_q[:, 0:128],
                            wukh[:, h * 256 + mc * 128: h * 256 + (mc + 1) * 128],
                            quc64[:, h * 128:(h + 1) * 128],
                            start=True, stop=True)
                        i.then_inc(qmm.h, 1)
                    qmm.n += 1

                    @b.block.vector
                    def _(v, h=h, mc=mc, qn=qmm.n):
                        v.wait_ge(qmm.h, qn)
                        dst = qabs[:, mc * 1024:(mc + 1) * 1024] \
                            .rearrange("p (m x) -> p m x", x=128)[:, :, h * GL:(h + 1) * GL]
                        v.tensor_scalar(out=dst, in0=ps_q[:, 0:128],
                                        scalar1=ATT_SCALE, scalar2=None,
                                        op0=ALU.mult).then_inc(qev.h, 1)
                    qev.n += 1

            qf_order = ("1a", "2a", "1b", "2b")
            for fi in range(4):
                for h in range(8):
                    kk = fi * 8 + h

                    @b.block.tensor
                    def _(t, fi=fi, h=h, kk=kk, sn=sfp.n, tn=tev.n):
                        t.wait_ge(sfp.h, sn)
                        if tn >= 2:
                            t.wait_ge(tev.h, tn - 1)
                        pst = ps_t1 if kk % 2 == 0 else ps_t2
                        i = t.transpose(pst[0:16, :],
                                        ftile[:, fi * 128 + h * 16: fi * 128 + (h + 1) * 16],
                                        identb[:])
                        i.then_inc(qmm.h, 1)
                    qmm.n += 1

                    @b.block.vector
                    def _(v, fi=fi, h=h, kk=kk, qn=qmm.n):
                        v.wait_ge(qmm.h, qn)
                        pst = ps_t1 if kk % 2 == 0 else ps_t2
                        tile_ = qfPa if fi < 2 else qfPb
                        rbase = 0 if fi % 2 == 0 else 32
                        dst = tile_[rbase:rbase + 16, :] \
                            .rearrange("p (m x) -> p m x", x=128)[:, :, h * GL:(h + 1) * GL]
                        v.tensor_scalar(out=dst, in0=pst[0:16, :],
                                        scalar1=ATT_SCALE, scalar2=None,
                                        op0=ALU.mult).then_inc(tev.h, 1)
                    tev.n += 1

            if STG < 2:
                continue
            for wc in range(WS):
                for h4 in range(4):
                    @b.block.tensor
                    def _(t, wc=wc, h4=h4, ien=iev.n, gp=gpd_prev):
                        if wc == 0 and h4 == 0:
                            t.wait_ge(gpd.h, gp)
                        if ien > 0:
                            t.wait_ge(iev.h, ien)
                        i = t.matmul(ps_I[:],
                                     qi64[:, h4 * 128:(h4 + 1) * 128],
                                     ki_sb[:, wc * NT:(wc + 1) * NT],
                                     start=True, stop=True)
                        i.then_inc(qmm.h, 1)
                    qmm.n += 1

                    @b.block.vector
                    def _(v, wc=wc, h4=h4, qn=qmm.n, ib=iev.n):
                        v.wait_ge(qmm.h, qn)
                        sl = slice(wc * NT, (wc + 1) * NT)
                        if h4 == 0:
                            v.tensor_scalar(out=I0[:, sl], in0=ps_I[:],
                                            scalar1=0.0, scalar2=idxw_b[:, 0:1],
                                            op0=ALU.max, op1=ALU.mult) \
                                .then_inc(iev.h, 1)
                        else:
                            v.tensor_scalar(out=itmp[:], in0=ps_I[:],
                                            scalar1=0.0,
                                            scalar2=idxw_b[:, h4:h4 + 1],
                                            op0=ALU.max, op1=ALU.mult) \
                                .then_inc(iev.h, 1)
                            v.wait_ge(iev.h, ib + 1)
                            v.tensor_tensor(out=I0[:, sl], in0=I0[:, sl],
                                            in1=itmp[:], op=ALU.add) \
                                .then_inc(iev.h, 1)
                    iev.n += 1 if h4 == 0 else 2

            @b.block.gpsimd
            def _(g, ti=ti, t0=t0, WS=WS, ien=iev.n, en=ext.n):
                g.wait_ge(iev.h, ien)
                if ti > 0:
                    g.wait_ge(ext.h, en)
                g.affine_select(out=I0[:, 0:WS * NT], in_=I0[:, 0:WS * NT],
                                pattern=[[-1, WS * NT]], compare_op=ALU.is_ge,
                                fill=NEG, base=t0 - 64, channel_multiplier=1) \
                    .then_inc(gpd.h, 1)
            gpd.n += 1

            if STG < 3:
                continue
            W_ = WS * NT
            for r in range(4):
                src = I0 if r % 2 == 0 else I1
                dst = I1 if r % 2 == 0 else I0

                @b.block.vector
                def _(v, r=r, src=src, dst=dst, W_=W_, gn=gpd.n, eb=ext.n,
                      idp=idd_prev):
                    if r == 0:
                        v.wait_ge(gpd.h, gn)
                        v.wait_ge(idd.h, idp)
                    v.max(out=m8x[:], in_=src[:, 0:W_]).then_inc(ext.h, 1)
                    v.wait_ge(ext.h, eb + 1)
                    v.max_index(out=idxall[:, r * 8:(r + 1) * 8], in_max=m8x[:],
                                in_values=src[:, 0:W_]).then_inc(ext.h, 1)
                    v.wait_ge(ext.h, eb + 2)
                    if r < 3:
                        v.match_replace(out=dst[:, 0:W_], in_to_replace=m8x[:],
                                        in_values=src[:, 0:W_], imm_value=NEG) \
                            .then_inc(ext.h, 1)
                        v.wait_ge(ext.h, eb + 3)
                ext.n += 3 if r < 3 else 2

            if tile0:
                @b.block.vector
                def _(v, eb=ext.n):
                    v.tensor_scalar(out=selm2[:], in0=iotaj[:], scalar1=nfin0[:],
                                    scalar2=None, op0=ALU.is_ge).then_inc(ext.h, 1)
                    v.wait_ge(ext.h, eb + 1)
                    v.copy_predicated(out=idxall[:], mask=selm2[:], data=fill64[:]) \
                        .then_inc(ext.h, 1)
                ext.n += 2

            if STG < 4:
                continue
            if DEBUG_TENSORS and ti == 0:
                @b.block.scalar
                def _(s, en=ext.n):
                    s.wait_ge(ext.h, en)
                    s.dma_start(out=b.dram["dbg_idx"], in_=idxall[:]) \
                        .then_inc(dbgs.h, 16)
                dbgs.n += 16
            @b.block.sync
            def _(sync, ti=ti, en=ext.n, ib=idd.n):
                sync.wait_ge(ext.h, en)
                sync.dma_start(out=idx_d, in_=idxall[:]).then_inc(idd.h, 16)
                sync.wait_ge(idd.h, ib + 16)
                for cc in range(32):
                    sync.dma_start(
                        out=idxg[:, cc:cc + 1],
                        in_=idx_d.rearrange("l (j o) -> (l j) o", o=1)[cc * 128:(cc + 1) * 128]) \
                        .then_inc(idd.h, 16)
            idd.n += 16 * 33

            @b.block.gpsimd
            def _(g, ti=ti, idn=idd.n, qp=qmm_prev):
                g.wait_ge(idd.h, idn)
                if ti > 0:
                    g.wait_ge(qmm.h, qp)
                for cc in range(32):
                    g.indirect_dma_start(
                        out=kg[:, cc * 320:(cc + 1) * 320], out_offset=None,
                        in_=kcat,
                        in_offset=bass.IndirectOffsetOnAxis(ap=idxg[:, cc:cc + 1],
                                                            axis=0)) \
                        .then_inc(gpd.h, 16)
            gpd.n += 16 * 32

            if STG < 5:
                continue
            kg3 = kg[:].rearrange("p (cc f) -> p cc f", f=320)

            @b.block.vector
            def _(v, gn=gpd.n, mb=mrt.n):
                v.wait_ge(gpd.h, gn)
                m1v = mtmp1[:].rearrange("p (c i) -> p c i", i=16)
                m2v = mtmp2[:].rearrange("p (c i) -> p c i", i=16)
                r1v = r1c[:].rearrange("p (c i) -> p c i", i=16)
                crv = crot_rep[:].rearrange("p (c i) -> p c i", i=16)
                srv = srot_rep[:].rearrange("p (c i) -> p c i", i=16)
                v.tensor_tensor(out=m1v, in0=kg3[:, :, 256:272], in1=crv, op=ALU.mult)
                v.tensor_tensor(out=m2v, in0=kg3[:, :, 288:304], in1=srv, op=ALU.mult)
                v.tensor_tensor(out=r1v, in0=m1v, in1=m2v, op=ALU.subtract)
                v.tensor_tensor(out=m1v, in0=kg3[:, :, 256:272], in1=srv, op=ALU.mult)
                v.tensor_tensor(out=m2v, in0=kg3[:, :, 288:304], in1=crv,
                                op=ALU.mult).then_inc(mrt.h, 1)
                v.wait_ge(mrt.h, mb + 1)
                v.tensor_tensor(out=kg3[:, :, 288:304], in0=m1v, in1=m2v, op=ALU.add)
                v.tensor_copy(kg3[:, :, 256:272], r1v).then_inc(mrt.h, 1)
            mrt.n += 2

            if STG < 6:
                continue
            for cc in range(32):
                @b.block.tensor
                def _(t, cc=cc, mn=mrt.n, tn=tev.n):
                    t.wait_ge(mrt.h, mn)
                    if tn >= 4:
                        t.wait_ge(tev.h, tn)
                    t.transpose(ps_t1[:], kg[:, cc * 320: cc * 320 + 128],
                                identb[:]).then_inc(qmm.h, 1)
                    t.transpose(ps_t2[:], kg[:, cc * 320 + 128: cc * 320 + 256],
                                identb[:]).then_inc(qmm.h, 1)
                qmm.n += 2

                @b.block.vector
                def _(v, cc=cc, qn=qmm.n):
                    v.wait_ge(qmm.h, qn)
                    v.tensor_copy(kgt1[:, cc * 128:(cc + 1) * 128], ps_t1[:]) \
                        .then_inc(tev.h, 1)
                    v.tensor_copy(kgt2[:, cc * 128:(cc + 1) * 128], ps_t2[:]) \
                        .then_inc(tev.h, 1)
                tev.n += 2

                @b.block.tensor
                def _(t, cc=cc, tn=tev.n):
                    t.wait_ge(tev.h, tn)
                    t.transpose(ps_t1[0:64, :], kg[:, cc * 320 + 256:(cc + 1) * 320],
                                identb[:]).then_inc(qmm.h, 1)
                qmm.n += 1

                @b.block.vector
                def _(v, cc=cc, qn=qmm.n):
                    v.wait_ge(qmm.h, qn)
                    v.tensor_copy(krotP[0:16, cc * 128:(cc + 1) * 128],
                                  ps_t1[0:16, :]).then_inc(tev.h, 1)
                    v.tensor_copy(krotP[32:48, cc * 128:(cc + 1) * 128],
                                  ps_t1[32:48, :]).then_inc(tev.h, 1)
                tev.n += 2

            if STG < 6.5:
                continue
            for m in range(8):
                formA = tile0 and m < 4
                wlo = 0 if formA else t0 + GL * m - 63
                wwid = 64 if formA else 80

                @b.block.sync
                def _(sync, m=m, wlo=wlo, wwid=wwid, qp=qmm.n, bn=bnd.n):
                    if bn > 0:
                        sync.wait_ge(qmm.h, qp)
                    sync.dma_start(out=band_sb[0:wwid, :],
                                   in_=kcat[wlo:wlo + wwid, 0:256]) \
                        .then_inc(bnd.h, 16)
                bnd.n += 16

                wrel = wlo - (max(t0 - 63, 0))
                if STG < 6.9:
                    continue

                @b.block.tensor
                def _(t, m=m, ti=ti, formA=formA, wrel=wrel, wwid=wwid,
                      tn=tev.n, sn=smx.n, bz=bnd.n):
                    t.wait_ge(tev.h, tn)
                    t.wait_ge(bnd.h, bz)
                    if sn > 0:
                        t.wait_ge(smx.h, sn)
                    for mc in range(2):
                        t.matmul(ps_sc[:],
                                 qabs[:, mc * 1024 + m * 128: mc * 1024 + (m + 1) * 128],
                                 (kgt1 if mc == 0 else kgt2)[:, m * 512:(m + 1) * 512],
                                 start=(mc == 0), stop=False)
                    i = t.matmul(ps_sc[:], qfPa[:, m * 128:(m + 1) * 128],
                                 krotP[:, m * 512:(m + 1) * 512], start=False,
                                 stop=True)
                    i.then_inc(qmm.h, 1)
                    for mc in range(2):
                        t.matmul(ps_bd[:, 0:wwid],
                                 qabs[:, mc * 1024 + m * 128: mc * 1024 + (m + 1) * 128],
                                 ckvb[:, mc * 256 + wrel: mc * 256 + wrel + wwid],
                                 start=(mc == 0), stop=False)
                    if formA:
                        i = t.matmul(ps_bd[:, 0:wwid], qfPa[:, m * 128:(m + 1) * 128],
                                     gkaP[:, :], start=False, stop=True)
                    else:
                        i = t.matmul(ps_bd[:, 0:wwid], qfPb[:, m * 128:(m + 1) * 128],
                                     gkwP[:, wrel:wrel + wwid], start=False, stop=True)
                    i.then_inc(qmm.h, 1)
                qmm.n += 2

                if STG < 7.3:
                    continue

                @b.block.vector
                def _(v, m=m, formA=formA, wwid=wwid, qn=qmm.n, sb_=smx.n):
                    v.wait_ge(qmm.h, qn)
                    v.tensor_tensor(out=sdyn[:], in0=ps_sc[:], in1=maskd[:],
                                    op=ALU.add)
                    if formA:
                        v.tensor_copy(sband[:, 0:wwid], ps_bd[:, 0:wwid])
                    else:
                        v.tensor_tensor(out=sband[:, 0:wwid], in0=ps_bd[:, 0:wwid],
                                        in1=maskb[:], op=ALU.add)
                    v.tensor_reduce(out=mrow[:], in_=sdyn[:], axis=AX.X, op=ALU.max)
                    v.tensor_reduce(out=mrow2[:], in_=sband[:, 0:wwid], axis=AX.X,
                                    op=ALU.max).then_inc(smx.h, 1)
                    v.wait_ge(smx.h, sb_ + 1)
                    v.tensor_tensor(out=mrow[:], in0=mrow[:], in1=mrow2[:],
                                    op=ALU.max)
                    v.tensor_scalar(out=negm[:], in0=mrow[:], scalar1=-1.0,
                                    scalar2=None, op0=ALU.mult).then_inc(smx.h, 1)
                smx.n += 2

                if STG < 7.6:
                    continue

                @b.block.scalar
                def _(s, m=m, wwid=wwid, sn=smx.n, xb=sxe.n, tn=tev.n):
                    s.wait_ge(smx.h, sn)
                    if xb > 0:
                        s.wait_ge(tev.h, tn)
                    s.activation(out=pd_sb[:], in_=sdyn[:], func=AF.Exp,
                                 bias=negm[:], accum_out=z1[:]).then_inc(sxe.h, 1)
                    s.wait_ge(sxe.h, xb + 1)
                    s.activation(out=pb_sb[:, 0:wwid], in_=sband[:, 0:wwid],
                                 func=AF.Exp, bias=negm[:], accum_out=z2[:]) \
                        .then_inc(sxe.h, 1)
                sxe.n += 2

                if STG < 7.8:
                    continue

                @b.block.vector
                def _(v, m=m, wwid=wwid, xn_=sxe.n, sb_=smx.n):
                    v.wait_ge(sxe.h, xn_)
                    v.tensor_tensor(out=zsum[:], in0=z1[:], in1=z2[:], op=ALU.add) \
                        .then_inc(smx.h, 1)
                    v.wait_ge(smx.h, sb_ + 1)
                    v.reciprocal(out=zinv[:], in_=zsum[:]).then_inc(smx.h, 1)
                    v.wait_ge(smx.h, sb_ + 2)
                    v.tensor_scalar(out=pd_sb[:], in0=pd_sb[:], scalar1=zinv[:],
                                    scalar2=None, op0=ALU.mult)
                    v.tensor_scalar(out=pb_sb[:, 0:wwid], in0=pb_sb[:, 0:wwid],
                                    scalar1=zinv[:], scalar2=None, op0=ALU.mult) \
                        .then_inc(smx.h, 1)
                smx.n += 3

                if DEBUG_TENSORS and ti == 0:
                    @b.block.scalar
                    def _(s, m=m, sn=smx.n):
                        s.wait_ge(smx.h, sn)
                        s.dma_start(out=b.dram["dbg_sd"][m * 128:(m + 1) * 128, :],
                                    in_=sdyn[:]).then_inc(dbgs.h, 16)
                        s.dma_start(out=b.dram["dbg_sb"][m * 128:(m + 1) * 128, :],
                                    in_=sband[:]).then_inc(dbgs.h, 16)
                    dbgs.n += 32

                    @b.block.vector
                    def _(v, dn=dbgs.n):
                        v.wait_ge(dbgs.h, dn)

                if STG < 8:
                    continue
                for pc in range(4):
                    @b.block.tensor
                    def _(t, m=m, pc=pc, sn=smx.n, tn=tev.n):
                        t.wait_ge(smx.h, sn)
                        if tn >= 2:
                            t.wait_ge(tev.h, tn - 1)
                        pst = ps_t1 if pc % 2 == 0 else ps_t2
                        t.transpose(pst[:], pd_sb[:, pc * 128:(pc + 1) * 128],
                                    identb[:]).then_inc(qmm.h, 1)
                    qmm.n += 1

                    @b.block.vector
                    def _(v, m=m, pc=pc, qn=qmm.n):
                        v.wait_ge(qmm.h, qn)
                        pst = ps_t1 if pc % 2 == 0 else ps_t2
                        v.tensor_copy(pt_sb[:, pc * 128:(pc + 1) * 128], pst[:]) \
                            .then_inc(tev.h, 1)
                    tev.n += 1

                @b.block.tensor
                def _(t, m=m, wwid=wwid, sn=smx.n, tn=tev.n):
                    t.wait_ge(smx.h, sn)
                    t.wait_ge(tev.h, tn - 1)
                    t.transpose(ps_t1[0:wwid, :], pb_sb[:, 0:wwid], identb[:]) \
                        .then_inc(qmm.h, 1)
                qmm.n += 1

                @b.block.vector
                def _(v, m=m, wwid=wwid, qn=qmm.n):
                    v.wait_ge(qmm.h, qn)
                    v.tensor_copy(ptb[0:wwid, :], ps_t1[0:wwid, :]).then_inc(tev.h, 1)
                tev.n += 1

                @b.block.tensor
                def _(t, m=m, wwid=wwid, tn=tev.n, an=ave.n, bn=bnd.n):
                    t.wait_ge(tev.h, tn)
                    t.wait_ge(bnd.h, bn)
                    if an > 0:
                        t.wait_ge(ave.h, an)
                    first = True
                    for mc in range(2):
                        for cc4 in range(4):
                            t.matmul(ps_av[:, mc * 128:(mc + 1) * 128],
                                     kg[:, (4 * m + cc4) * 320 + mc * 128:
                                        (4 * m + cc4) * 320 + (mc + 1) * 128],
                                     pt_sb[:, cc4 * 128:(cc4 + 1) * 128],
                                     start=(cc4 == 0), stop=False)
                        i = t.matmul(ps_av[:, mc * 128:(mc + 1) * 128],
                                     band_sb[0:wwid, mc * 128:(mc + 1) * 128],
                                     ptb[0:wwid, :], start=False, stop=True)
                    i.then_inc(qmm.h, 1)
                qmm.n += 1

                @b.block.vector
                def _(v, m=m, qn=qmm.n):
                    v.wait_ge(qmm.h, qn)
                    v.tensor_copy(oc_sb[:, 0 * 1024 + m * 128: 0 * 1024 + (m + 1) * 128],
                                  ps_av[:, 0:128])
                    v.tensor_copy(oc_sb[:, 1 * 1024 + m * 128: 1 * 1024 + (m + 1) * 128],
                                  ps_av[:, 128:256]).then_inc(ave.h, 1)
                ave.n += 1

            if STG < 9:
                continue
            for md in range(8):
                @b.block.sync
                def _(sync, md=md, ti=ti, t0=t0, cn=cmb.n):
                    if cmb.n > 0:
                        sync.wait_ge(cmb.h, cn)
                    sync.dma_start(out=xg_sb[:, 0:128],
                                   in_=xT_r[md][:, t0:t0 + 128]).then_inc(xst.h, 16)
                    sync.dma_start(out=xg_sb[:, 128:256],
                                   in_=g0_r[md][:, t0:t0 + 128]).then_inc(xst.h, 16)
                xst.n += 32

                @b.block.tensor
                def _(t, md=md, an=ave.n, cn=cmb.n):
                    t.wait_ge(ave.h, an)
                    if cn > 0:
                        t.wait_ge(cmb.h, cn)
                    first = True
                    for h in range(8):
                        for mc in range(2):
                            rhs = oc_sb[:, mc * 1024:(mc + 1) * 1024] \
                                .rearrange("p (m x) -> p m x", x=128)[:, :, h * GL:(h + 1) * GL]
                            i = t.matmul(
                                ps_ep[:],
                                wcat_sb[:, (h * 2 + mc) * 1024 + md * 128:
                                        (h * 2 + mc) * 1024 + (md + 1) * 128],
                                rhs, start=first, stop=(h == 7 and mc == 1))
                            first = False
                    i.then_inc(qmm.h, 1)
                qmm.n += 1

                @b.block.vector
                def _(v, md=md, qn=qmm.n, xn_=xst.n, cb=cmb.n):
                    v.wait_ge(qmm.h, qn)
                    v.wait_ge(xst.h, xn_)
                    v.tensor_tensor(out=x1o[:], in0=ps_ep[:], in1=xg_sb[:, 128:256],
                                    op=ALU.mult).then_inc(cmb.h, 1)
                    v.wait_ge(cmb.h, cb + 1)
                    v.tensor_tensor(out=x1o[:], in0=x1o[:], in1=xg_sb[:, 0:128],
                                    op=ALU.add).then_inc(cmb.h, 1)
                cmb.n += 2

                @b.block.sync
                def _(sync, md=md, t0=t0, cn=cmb.n):
                    sync.wait_ge(cmb.h, cn)
                    sync.dma_start(out=x1_r[md][:, t0:t0 + 128], in_=x1o[:]) \
                        .then_inc(xst.h, 16)
                xst.n += 16

        @b.block.sync
        def _(sync, a=idd.n, b2=bnd.n, c2=ldn.n, d2=xst.n, e2=ldp.n):
            for s_, v_ in ((idd, a), (bnd, b2), (ldn, c2), (xst, d2), (ldp, e2)):
                if v_ > 0:
                    sync.wait_ge(s_.h, v_)

        @b.block.gpsimd
        def _(g, a=gpd.n):
            if a > 0:
                g.wait_ge(gpd.h, a)

        @b.block.tensor
        def _(t, a=qmm.n):
            if a > 0:
                t.wait_ge(qmm.h, a)

        @b.block.scalar
        def _(s, a=sxe.n, b2=sfp.n):
            for s_, v_ in ((sxe, a), (sfp, b2)):
                if v_ > 0:
                    s.wait_ge(s_.h, v_)

        @b.block.vector
        def _(v, vals=(ext.n, tev.n, smx.n, ave.n, iev.n, mrt.n, cmb.n, qev.n)):
            for s_, v_ in zip((ext, tev, smx, ave, iev, mrt, cmb, qev), vals):
                if v_ > 0:
                    v.wait_ge(s_.h, v_)



def emit_l1att(b: B):
    nc = b.nc
    xn1T = b.dram["xn1T"]
    s1T = b.dram["s1T"]
    wqkvT = b.dram["wqkvT"]
    proj = b.dram["proj"]
    wout1T = b.dram["wout1T"]
    l1h, l1xm = b.dram["l1h"], b.dram["l1xm"]
    H, F, HD = 16, 256, 64

    xn_c = xn1T.rearrange("(a p) (c n) -> c a p n", p=128, n=NT)
    s1_c = s1T.rearrange("(m p) (c n) -> c m p n", p=128, n=NT)
    xm_c = l1xm.rearrange("(a p) (c n) -> c a p n", p=128, n=NT)
    h_d = l1h.rearrange("(i p) (c n) -> i c p n", p=128, n=NT)

    proj_sb = b.sb("aProj", [64, H * F], BF16)
    kvall = b.sb("aKval", [128, H * 2 * 65], BF16)
    onesC = b.sb("aOnes", [128, 1], BF16)
    onesc64 = b.sb("aOn64", [1, 64])
    gvec = b.sb("aGv", [128, 8])
    bvec = b.sb("aBv", [128, 8])
    bovec = b.sb("aBo", [128, 8])
    ldw = b.sem("aldw")
    pre = b.sem("apre")

    @b.block.vector
    def _(v):
        v.memset(onesC[:], 1.0)
        v.memset(onesc64[:], 1.0)

    @b.block.scalar
    def _(s):
        s.dma_start(out=gvec[:], in_=b.dram["gamma1r"]).then_inc(ldw.h, 16)
        s.dma_start(out=bvec[:], in_=b.dram["beta1r"]).then_inc(ldw.h, 16)
        s.dma_start(out=bovec[:], in_=b.dram["b_outr"]).then_inc(ldw.h, 16)
    ldw.n += 48

    @b.block.gpsimd
    def _(g):
        for h in range(H):
            g.dma_start(out=proj_sb[:, h * F:(h + 1) * F], in_=proj[h]) \
                .then_inc(pre.h, 16)
    pre.n += 16 * H

    with ExitStack() as ctx1:
        wkv_bf = ctx1.enter_context(nc.sbuf_tensor("aWkv", [128, 8 * 2048], BF16))
        cvs = b.sem("acvs")
        with ExitStack() as ctxs:
            stg = ctxs.enter_context(nc.sbuf_tensor("aStg", [128, 4 * 2048], F32))
            lds = b.sem("alds")
            for half in range(2):
                @b.block.sync
                def _(sync, half=half, cn=cvs.n):
                    if half > 0:
                        sync.wait_ge(cvs.h, cn)
                    for a in range(4):
                        aa = half * 4 + a
                        sync.dma_start(out=stg[:, a * 2048:(a + 1) * 2048],
                                       in_=wqkvT[aa * 128:(aa + 1) * 128, 1024:3072]) \
                            .then_inc(lds.h, 16)
                lds.n += 64

                @b.block.vector
                def _(v, half=half, ln=lds.n):
                    v.wait_ge(lds.h, ln)
                    v.tensor_copy(wkv_bf[:, half * 4 * 2048:(half + 1) * 4 * 2048],
                                  stg[:]).then_inc(cvs.h, 1)
                cvs.n += 1
        xn_sb = ctx1.enter_context(nc.sbuf_tensor("aXn", [128, 8 * T], BF16))
        kT_h = ctx1.enter_context(nc.sbuf_tensor("aKTh", [64, T], BF16))
        v_h = ctx1.enter_context(nc.sbuf_tensor("aVh", [128, 32 * 64], BF16))
        kf_h = ctx1.enter_context(nc.sbuf_tensor("aKfh", [128, 32 * F], BF16))
        ps_k = ctx1.enter_context(nc.psum_tensor("aPk", [64, NT], F32))
        ps_v = [ctx1.enter_context(nc.psum_tensor(f"aPv{j}", [128, 64], F32))
                for j in range(2)]
        ps_kf = [ctx1.enter_context(nc.psum_tensor(f"aPkf{j}", [128, F], F32))
                 for j in range(2)]
        ps_kv = ctx1.enter_context(nc.psum_tensor("aPkv", [128, 2 * 65], F32))
        ldx = b.sem("aldx")
        kmm = b.sem("akmm")
        kev = b.sem("akev")
        vmm = b.sem("avmm")
        vev = b.sem("avev")
        fmm = b.sem("afmm")
        fel = b.sem("afel")
        kvm = b.sem("akvm")
        kve = b.sem("akve")

        @b.block.sync
        def _(sync):
            sync.wait_ge(cvs.h, 2)
            for a in range(8):
                for c in range(NCH):
                    sync.dma_start(out=xn_sb[:, a * T + c * NT: a * T + (c + 1) * NT],
                                   in_=xn_c[c, a]).then_inc(ldx.h, 16)
        ldx.n += 16 * 64

        for h in range(H):
            kcol = h * 64
            vcol = 1024 + h * 64
            for n in range(NCH):
                @b.block.tensor
                def _(t, h=h, n=n, kevn=kev.n):
                    if h == 0 and n == 0:
                        t.wait_ge(ldx.h, ldx.n)
                        t.wait_ge(cvs.h, 2)
                        t.wait_ge(pre.h, 16 * H)
                    if h + n > 0:
                        t.wait_ge(kev.h, kevn)
                    for a in range(8):
                        i = t.matmul(ps_k[:],
                                     wkv_bf[:, a * 2048 + kcol: a * 2048 + kcol + 64],
                                     xn_sb[:, a * T + n * NT: a * T + (n + 1) * NT],
                                     start=(a == 0), stop=(a == 7))
                    i.then_inc(kmm.h, 1)
                kmm.n += 1

                @b.block.vector
                def _(v, h=h, n=n, kn=kmm.n):
                    v.wait_ge(kmm.h, kn)
                    v.tensor_copy(kT_h[:, n * NT:(n + 1) * NT], ps_k[:]) \
                        .then_inc(kev.h, 1)
                kev.n += 1

            for tm in range(32):
                @b.block.tensor
                def _(t, h=h, tm=tm, vevn=vev.n):
                    if vevn >= 2:
                        t.wait_ge(vev.h, vevn - 1)
                    for a in range(8):
                        i = t.matmul(ps_v[tm % 2][:],
                                     xn_sb[:, a * T + tm * 128: a * T + (tm + 1) * 128],
                                     wkv_bf[:, a * 2048 + vcol: a * 2048 + vcol + 64],
                                     start=(a == 0), stop=(a == 7))
                    i.then_inc(vmm.h, 1)
                vmm.n += 1

                @b.block.vector
                def _(v, h=h, tm=tm, vn=vmm.n):
                    v.wait_ge(vmm.h, vn)
                    v.tensor_copy(v_h[:, tm * 64:(tm + 1) * 64], ps_v[tm % 2][:]) \
                        .then_inc(vev.h, 1)
                vev.n += 1

            for tm in range(32):
                @b.block.tensor
                def _(t, h=h, tm=tm, kevn=kev.n, feln=fel.n):
                    t.wait_ge(kev.h, kevn)
                    if feln >= 6:
                        t.wait_ge(fel.h, feln - 3)
                    i = t.matmul(ps_kf[tm % 2][:],
                                 kT_h[:, tm * 128:(tm + 1) * 128],
                                 proj_sb[:, h * F:(h + 1) * F],
                                 start=True, stop=True)
                    i.then_inc(fmm.h, 1)
                fmm.n += 1

                @b.block.vector
                def _(v, h=h, tm=tm, fn=fmm.n, fb=fel.n):
                    v.wait_ge(fmm.h, fn)
                    v.tensor_scalar(out=kf_h[:, tm * F:(tm + 1) * F],
                                    in0=ps_kf[tm % 2][:], scalar1=0.0, scalar2=None,
                                    op0=ALU.min).then_inc(fel.h, 1)
                fel.n += 1

                @b.block.scalar
                def _(s, h=h, tm=tm, fb=fel.n):
                    s.wait_ge(fel.h, fb)
                    s.activation(out=kf_h[:, tm * F:(tm + 1) * F],
                                 in_=kf_h[:, tm * F:(tm + 1) * F], func=AF.Exp) \
                        .then_inc(fel.h, 1)
                fel.n += 1

                @b.block.vector
                def _(v, h=h, tm=tm, fb=fel.n, fn=fmm.n):
                    v.wait_ge(fel.h, fb)
                    i = v.scalar_tensor_tensor(
                        out=kf_h[:, tm * F:(tm + 1) * F], in0=ps_kf[tm % 2][:],
                        scalar=0.0, in1=kf_h[:, tm * F:(tm + 1) * F],
                        op0=ALU.max, op1=ALU.add)
                    i.then_inc(fel.h, 1)
                fel.n += 1

            for fm in range(2):
                @b.block.tensor
                def _(t, h=h, fm=fm, feln=fel.n, kven=kve.n, vevn=vev.n):
                    t.wait_ge(fel.h, feln)
                    t.wait_ge(vev.h, vevn)
                    if h + fm > 0:
                        t.wait_ge(kve.h, kven)
                    for tm in range(32):
                        t.matmul(ps_kv[:, fm * 65: fm * 65 + 64],
                                 kf_h[:, tm * F + fm * 128: tm * F + fm * 128 + 128],
                                 v_h[:, tm * 64:(tm + 1) * 64],
                                 start=(tm == 0), stop=(tm == 31))
                    for tm in range(32):
                        i = t.matmul(ps_kv[:, fm * 65 + 64: fm * 65 + 65],
                                     kf_h[:, tm * F + fm * 128: tm * F + fm * 128 + 128],
                                     onesC[:],
                                     start=(tm == 0), stop=(tm == 31))
                    i.then_inc(kvm.h, 1)
                kvm.n += 1

                @b.block.vector
                def _(v, h=h, fm=fm, kn=kvm.n):
                    v.wait_ge(kvm.h, kn)
                    v.tensor_copy(kvall[:, (h * 2 + fm) * 65:(h * 2 + fm + 1) * 65],
                                  ps_kv[:, fm * 65:(fm + 1) * 65]).then_inc(kve.h, 1)
                kve.n += 1


        if "dbg_kf" in DEBUG_TENSORS:
            dbk = b.sem("adbk")

            @b.block.scalar
            def _(s, kn=kve.n):
                s.wait_ge(kve.h, 32)
                s.dma_start(out=b.dram["dbg_kf"], in_=kf_h[:]).then_inc(dbk.h, 16)
                s.dma_start(out=b.dram["dbg_vh"], in_=v_h[:]).then_inc(dbk.h, 16)
                s.dma_start(out=b.dram["dbg_kt"], in_=kT_h[:]).then_inc(dbk.h, 16)
                s.wait_ge(dbk.h, 48)

    b.barrier()

    ctx23 = ExitStack()
    attn_all = ctx23.enter_context(nc.sbuf_tensor("aAttn", [128, 8 * T], BF16))
    wq_bf = ctx23.enter_context(nc.sbuf_tensor("aWq", [128, 8 * 1024], BF16))
    cvq = b.sem("acvq")
    with ExitStack() as ctxs2:
        stgq = ctxs2.enter_context(nc.sbuf_tensor("aStgq", [128, 4 * 1024], F32))
        ldsq = b.sem("aldsq")
        for half in range(2):
            @b.block.sync
            def _(sync, half=half, cn=cvq.n):
                if half > 0:
                    sync.wait_ge(cvq.h, cn)
                for a in range(4):
                    aa = half * 4 + a
                    sync.dma_start(out=stgq[:, a * 1024:(a + 1) * 1024],
                                   in_=wqkvT[aa * 128:(aa + 1) * 128, 0:1024]) \
                        .then_inc(ldsq.h, 16)
            ldsq.n += 64

            @b.block.vector
            def _(v, half=half, ln=ldsq.n):
                v.wait_ge(ldsq.h, ln)
                v.tensor_copy(wq_bf[:, half * 4 * 1024:(half + 1) * 4 * 1024],
                              stgq[:]).then_inc(cvq.h, 1)
            cvq.n += 1
    with ExitStack() as ctx2:
        xn_sb2 = ctx2.enter_context(nc.sbuf_tensor("aXn2", [128, 8 * T], BF16))
        qT_h = ctx2.enter_context(nc.sbuf_tensor("aQTh", [64, T], BF16))
        qf_h = ctx2.enter_context(nc.sbuf_tensor("aQfh", [128, 2 * T], BF16))
        z_sb = ctx2.enter_context(nc.sbuf_tensor("aZsb", [64, NT], F32))
        zrow = ctx2.enter_context(nc.sbuf_tensor("aZr", [1, NT], F32))
        ps_q = ctx2.enter_context(nc.psum_tensor("aPq", [64, NT], F32))
        ps_qf = [ctx2.enter_context(nc.psum_tensor(f"aPqf{j}", [128, NT], F32))
                 for j in range(2)]
        ps_at = ctx2.enter_context(nc.psum_tensor("aPat", [65, NT], F32))
        ps_z = ctx2.enter_context(nc.psum_tensor("aPz", [64, NT], F32))
        ldx2 = b.sem("aldx2")
        qmm = b.sem("aqmm")
        qev = b.sem("aqev")
        qfm = b.sem("aqfm")
        qfe = b.sem("aqfe")
        amm = b.sem("aamm")
        zst = b.sem("azst")
        zbc = b.sem("azbc")
        aev = b.sem("aaev")

        @b.block.sync
        def _(sync):
            sync.wait_ge(cvq.h, 2)
            for a in range(8):
                for c in range(NCH):
                    sync.dma_start(out=xn_sb2[:, a * T + c * NT: a * T + (c + 1) * NT],
                                   in_=xn_c[c, a]).then_inc(ldx2.h, 16)
        ldx2.n += 16 * 64

        for h in range(H):
            qcol = h * 64
            for n in range(NCH):
                @b.block.tensor
                def _(t, h=h, n=n, qevn=qev.n):
                    if h == 0 and n == 0:
                        t.wait_ge(ldx2.h, ldx2.n)
                        t.wait_ge(cvq.h, 2)
                    if h + n > 0:
                        t.wait_ge(qev.h, qevn)
                    for a in range(8):
                        i = t.matmul(ps_q[:],
                                     wq_bf[:, a * 1024 + qcol: a * 1024 + qcol + 64],
                                     xn_sb2[:, a * T + n * NT: a * T + (n + 1) * NT],
                                     start=(a == 0), stop=(a == 7))
                    i.then_inc(qmm.h, 1)
                qmm.n += 1

                @b.block.vector
                def _(v, h=h, n=n, qn=qmm.n):
                    v.wait_ge(qmm.h, qn)
                    v.tensor_copy(qT_h[:, n * NT:(n + 1) * NT], ps_q[:]) \
                        .then_inc(qev.h, 1)
                qev.n += 1

            for n in range(NCH):
                for fm in range(2):
                    j = n * 2 + fm

                    @b.block.tensor
                    def _(t, h=h, n=n, fm=fm, j=j, qevn=qev.n, qfen=qfe.n):
                        t.wait_ge(qev.h, qevn)
                        if qfen >= 6:
                            t.wait_ge(qfe.h, qfen - 3)
                        i = t.matmul(ps_qf[j % 2][:],
                                     proj_sb[:, h * F + fm * 128: h * F + fm * 128 + 128],
                                     qT_h[:, n * NT:(n + 1) * NT],
                                     start=True, stop=True)
                        i.then_inc(qfm.h, 1)
                    qfm.n += 1

                    @b.block.vector
                    def _(v, h=h, n=n, fm=fm, j=j, qn=qfm.n, qb=qfe.n):
                        v.wait_ge(qfm.h, qn)
                        sl = slice(fm * T + n * NT, fm * T + (n + 1) * NT)
                        v.tensor_scalar(out=qf_h[:, sl], in0=ps_qf[j % 2][:],
                                        scalar1=0.0, scalar2=None, op0=ALU.min) \
                            .then_inc(qfe.h, 1)
                    qfe.n += 1

                    @b.block.scalar
                    def _(s, h=h, n=n, fm=fm, qb=qfe.n):
                        s.wait_ge(qfe.h, qb)
                        sl = slice(fm * T + n * NT, fm * T + (n + 1) * NT)
                        s.activation(out=qf_h[:, sl], in_=qf_h[:, sl], func=AF.Exp) \
                            .then_inc(qfe.h, 1)
                    qfe.n += 1

                    @b.block.vector
                    def _(v, h=h, n=n, fm=fm, j=j, qb=qfe.n):
                        v.wait_ge(qfe.h, qb)
                        sl = slice(fm * T + n * NT, fm * T + (n + 1) * NT)
                        i = v.scalar_tensor_tensor(out=qf_h[:, sl], in0=ps_qf[j % 2][:],
                                                   scalar=0.0, in1=qf_h[:, sl],
                                                   op0=ALU.max, op1=ALU.add)
                        i.then_inc(qfe.h, 1)
                    qfe.n += 1

            for n in range(NCH):
                @b.block.tensor
                def _(t, h=h, n=n, qfen=qfe.n, aevn=aev.n):
                    t.wait_ge(qfe.h, qfen)
                    if h + n > 0:
                        t.wait_ge(aev.h, aevn)
                    for fm in range(2):
                        i = t.matmul(ps_at[:],
                                     kvall[:, (h * 2 + fm) * 65:(h * 2 + fm + 1) * 65],
                                     qf_h[:, fm * T + n * NT: fm * T + (n + 1) * NT],
                                     start=(fm == 0), stop=(fm == 1))
                    i.then_inc(amm.h, 1)
                amm.n += 1

                @b.block.vector
                def _(v, h=h, n=n, an=amm.n, zb=zst.n):
                    v.wait_ge(amm.h, an)
                    v.tensor_scalar(out=zrow[:], in0=ps_at[64:65, :], scalar1=1e-8,
                                    scalar2=None, op0=ALU.add).then_inc(zst.h, 1)
                    v.wait_ge(zst.h, zb + 1)
                    v.reciprocal(out=zrow[:], in_=zrow[:]).then_inc(zst.h, 1)
                zst.n += 2

                @b.block.tensor
                def _(t, h=h, n=n, zn=zst.n, aevn=aev.n):
                    t.wait_ge(zst.h, zn)
                    t.matmul(ps_z[:], onesc64[:], zrow[:], start=True, stop=True) \
                        .then_inc(zbc.h, 1)
                zbc.n += 1

                @b.block.vector
                def _(v, h=h, n=n, zn=zbc.n, ab=aev.n):
                    v.wait_ge(zbc.h, zn)
                    v.tensor_copy(z_sb[:], ps_z[:]).then_inc(aev.h, 1)
                    v.wait_ge(aev.h, ab + 1)
                    base = (h % 2) * 64
                    i = v.tensor_tensor(
                        out=attn_all[base:base + 64,
                                     (h // 2) * T + n * NT:(h // 2) * T + (n + 1) * NT],
                        in0=ps_at[0:64, :], in1=z_sb[:], op=ALU.mult)
                    i.then_inc(aev.h, 1)
                aev.n += 2

    b.barrier()

    with ExitStack() as ctx3:
        wo_bf = ctx3.enter_context(nc.sbuf_tensor("aWo", [128, 8 * 1024], BF16))
        stg2 = ctx3.enter_context(nc.sbuf_tensor("aStg2", [128, 2 * 1024], F32))
        xn_n3 = ctx3.enter_context(nc.sbuf_tensor("aXn3", [128, 8 * NT], BF16))
        y_sb = ctx3.enter_context(nc.sbuf_tensor("aY", [128, 8 * NT], F32))
        xm_sb = ctx3.enter_context(nc.sbuf_tensor("aXm", [128, 8 * NT], BF16))
        ps_y = [ctx3.enter_context(nc.psum_tensor(f"aPy{j}", [128, NT], F32))
                for j in range(2)]
        lno = LNAffine(b, "aL1", ctx3, gvec, bvec, 1e-5)
        ldo = b.sem("aldo")
        cvo = b.sem("acvo")
        ldn3 = b.sem("aldn3")
        ymm = b.sem("aymm")
        yev = b.sem("ayev")
        xst = b.sem("axst")

        for part in range(4):
            @b.block.sync
            def _(sync, part=part, cn=cvo.n):
                if part > 0:
                    sync.wait_ge(cvo.h, cn)
                for a in range(2):
                    aa = part * 2 + a
                    sync.dma_start(out=stg2[:, a * 1024:(a + 1) * 1024],
                                   in_=wout1T[aa * 128:(aa + 1) * 128, :]) \
                        .then_inc(ldo.h, 16)
            ldo.n += 32

            @b.block.vector
            def _(v, part=part, ln=ldo.n):
                v.wait_ge(ldo.h, ln)
                v.tensor_copy(wo_bf[:, part * 2048:(part + 1) * 2048], stg2[:]) \
                    .then_inc(cvo.h, 1)
            cvo.n += 1

        for c in range(NCH):
            @b.block.sync
            def _(sync, c=c, yevn=yev.n):
                if c > 0:
                    sync.wait_ge(yev.h, yevn)
                for a in range(8):
                    sync.dma_start(out=xn_n3[:, a * NT:(a + 1) * NT],
                                   in_=xn_c[c, a]).then_inc(ldn3.h, 16)
            ldn3.n += 128

            for m in range(8):
                km = c * 8 + m

                @b.block.tensor
                def _(t, c=c, m=m, km=km, yevn=yev.n):
                    if km == 0:
                        t.wait_ge(cvo.h, 4)
                    if km >= 2:
                        t.wait_ge(yev.h, km - 1)
                    for a in range(8):
                        i = t.matmul(ps_y[m % 2][:],
                                     wo_bf[:, a * 1024 + m * 128: a * 1024 + (m + 1) * 128],
                                     attn_all[:, a * T + c * NT: a * T + (c + 1) * NT],
                                     start=(a == 0), stop=(a == 7))
                    i.then_inc(ymm.h, 1)
                ymm.n += 1

                @b.block.vector
                def _(v, c=c, m=m, km=km, ldnn=ldn3.n):
                    v.wait_ge(ymm.h, km + 1)
                    v.wait_ge(ldn3.h, ldnn)
                    i = v.scalar_tensor_tensor(
                        out=y_sb[:, m * NT:(m + 1) * NT], in0=ps_y[m % 2][:],
                        scalar=bovec[:, m:m + 1], in1=xn_n3[:, m * NT:(m + 1) * NT],
                        op0=ALU.add, op1=ALU.add)
                    i.then_inc(yev.h, 1)
                yev.n += 1

            nrm = lno.emit_chunk(yev, yev.n, y_sb, xm_sb, xst, xst.n)

            @b.block.sync
            def _(sync, c=c, nn=nrm.n):
                sync.wait_ge(nrm.h, nn)
                for a in range(8):
                    sync.dma_start(out=xm_c[c, a], in_=xm_sb[:, a * NT:(a + 1) * NT]) \
                        .then_inc(xst.h, 16)
            xst.n += 128

        @b.block.sync
        def _(sync, xn=xst.n):
            sync.wait_ge(xst.h, xn)

    if "dbg_attn" in DEBUG_TENSORS:
        dba = b.sem("adba")

        @b.block.scalar
        def _(s):
            for a in range(8):
                for c2 in range(NCH):
                    s.dma_start(
                        out=b.dram["dbg_attn"].rearrange(
                            "(a p) t -> a p t", p=128)[a][:, c2 * NT:(c2 + 1) * NT],
                        in_=attn_all[:, a * T + c2 * NT: a * T + (c2 + 1) * NT]) \
                        .then_inc(dba.h, 16)
            s.dma_start(out=b.dram["dbg_kv"], in_=kvall[:]).then_inc(dba.h, 16)
            s.wait_ge(dba.h, 16 * 65)
        dba.n += 16 * 65

    ctx23.close()
    b.barrier()

    with ExitStack() as ctx4:
        w1_bf = ctx4.enter_context(nc.sbuf_tensor("aW1", [128, 32 * 1024], BF16))
        stg4 = ctx4.enter_context(nc.sbuf_tensor("aStg4", [128, 4096], F32))
        xm_n = [ctx4.enter_context(nc.sbuf_tensor(f"aXmn{j}", [128, 8 * NT], BF16))
                for j in range(2)]
        h_sb = ctx4.enter_context(nc.sbuf_tensor("aHsb", [128, NT], BF16))
        b1vec = ctx4.enter_context(nc.sbuf_tensor("aB1v", [128, 32], F32))
        ps_h = [ctx4.enter_context(nc.psum_tensor(f"aPh{j}", [128, NT], F32))
                for j in range(2)]
        ldw4 = b.sem("aldw4")
        cvw4 = b.sem("acvw4")
        ldm = b.sem("aldm")
        hmm_ = b.sem("ahmm")
        hge = b.sem("ahge")
        hst = b.sem("ahst")

        @b.block.scalar
        def _(s):
            s.dma_start(out=b1vec[:], in_=b.dram["b1ar"]).then_inc(ldw4.h, 16)
        ldw4.n += 16

        for part in range(8):
            @b.block.sync
            def _(sync, part=part, cn=cvw4.n):
                if part > 0:
                    sync.wait_ge(cvw4.h, cn)
                sync.dma_start(out=stg4[:],
                               in_=b.dram["w1aT"][part * 128:(part + 1) * 128, :]) \
                    .then_inc(ldw4.h, 16)
            ldw4.n += 16

            @b.block.vector
            def _(v, part=part, ln=ldw4.n):
                v.wait_ge(ldw4.h, ln)
                for i in range(32):
                    ii = v.tensor_copy(
                        w1_bf[:, i * 1024 + part * 128: i * 1024 + (part + 1) * 128],
                        stg4[:, i * 128:(i + 1) * 128])
                ii.then_inc(cvw4.h, 1)
            cvw4.n += 1

        for n in range(NCH):
            @b.block.sync
            def _(sync, n=n, hn=hmm_.n):
                if n >= 2:
                    sync.wait_ge(hmm_.h, hn - 32)
                for a in range(8):
                    sync.dma_start(out=xm_n[n % 2][:, a * NT:(a + 1) * NT],
                                   in_=xm_c[n, a]).then_inc(ldm.h, 16)
            ldm.n += 128

            for i in range(32):
                ki = n * 32 + i

                @b.block.tensor
                def _(t, n=n, i=i, ki=ki, ldmn=ldm.n, hgen=hge.n):
                    if i == 0:
                        t.wait_ge(ldm.h, ldmn)
                    if ki == 0:
                        t.wait_ge(cvw4.h, 8)
                    if ki >= 2:
                        t.wait_ge(hge.h, ki - 1)
                    for a in range(8):
                        ii = t.matmul(ps_h[ki % 2][:],
                                      w1_bf[:, i * 1024 + a * 128: i * 1024 + (a + 1) * 128],
                                      xm_n[n % 2][:, a * NT:(a + 1) * NT],
                                      start=(a == 0), stop=(a == 7))
                    ii.then_inc(hmm_.h, 1)
                hmm_.n += 1

                @b.block.scalar
                def _(s, n=n, i=i, ki=ki, hsn=hst.n):
                    s.wait_ge(hmm_.h, ki + 1)
                    if ki >= 1:
                        s.wait_ge(hst.h, hsn)
                    s.activation(out=h_sb[:], in_=ps_h[ki % 2][:], func=AF.Gelu,
                                 bias=b1vec[:, i:i + 1]).then_inc(hge.h, 1)
                hge.n += 1

                @b.block.sync
                def _(sync, n=n, i=i, ki=ki, hgn=hge.n):
                    sync.wait_ge(hge.h, hgn)
                    sync.dma_start(out=h_d[i, n], in_=h_sb[:]).then_inc(hst.h, 16)
                hst.n += 16

        @b.block.sync
        def _(sync, hn=hst.n):
            sync.wait_ge(hst.h, hn)

    b.barrier()

    with ExitStack() as ctx5:
        w2_bf = ctx5.enter_context(nc.sbuf_tensor("aW2", [128, 256 * 128], BF16))
        stg5 = ctx5.enter_context(nc.sbuf_tensor("aStg5", [128, 4 * 1024], F32))
        hload = ctx5.enter_context(nc.sbuf_tensor("aHld", [128, 32 * NT], BF16))
        xm_n5 = ctx5.enter_context(nc.sbuf_tensor("aXm5", [128, 8 * NT], BF16))
        y5 = ctx5.enter_context(nc.sbuf_tensor("aY5", [128, 8 * NT], F32))
        s1_sb = ctx5.enter_context(nc.sbuf_tensor("aS1", [128, 8 * NT], F32))
        b2vec = ctx5.enter_context(nc.sbuf_tensor("aB2v", [128, 8], F32))
        ps_y5 = [ctx5.enter_context(nc.psum_tensor(f"aPy5{j}", [128, NT], F32))
                 for j in range(2)]
        lno5 = LNAffine(b, "aL15", ctx5, gvec, bvec, 1e-5)
        ldw5 = b.sem("aldw5")
        cvw5 = b.sem("acvw5")
        ldh5 = b.sem("aldh5")
        ldm5 = b.sem("aldm5")
        ymm5 = b.sem("aymm5")
        yev5 = b.sem("ayev5")
        sst = b.sem("asst")

        @b.block.scalar
        def _(s):
            s.dma_start(out=b2vec[:], in_=b.dram["b2ar"]).then_inc(ldw5.h, 16)
        ldw5.n += 16

        for part in range(8):
            @b.block.sync
            def _(sync, part=part, cn=cvw5.n):
                if part > 0:
                    sync.wait_ge(cvw5.h, cn)
                for a in range(4):
                    ii = part * 4 + a
                    sync.dma_start(out=stg5[:, a * 1024:(a + 1) * 1024],
                                   in_=b.dram["w2aT"][ii * 128:(ii + 1) * 128, :]) \
                        .then_inc(ldw5.h, 16)
            ldw5.n += 64

            @b.block.vector
            def _(v, part=part, ln=ldw5.n):
                v.wait_ge(ldw5.h, ln)
                for a in range(4):
                    ii = part * 4 + a
                    iv = v.tensor_copy(w2_bf[:, ii * 1024:(ii + 1) * 1024],
                                       stg5[:, a * 1024:(a + 1) * 1024])
                iv.then_inc(cvw5.h, 1)
            cvw5.n += 1

        for c in range(NCH):
            @b.block.sync
            def _(sync, c=c, yevn=yev5.n):
                if c > 0:
                    sync.wait_ge(yev5.h, yevn)
                for i in range(32):
                    sync.dma_start(out=hload[:, i * NT:(i + 1) * NT],
                                   in_=h_d[i, c]).then_inc(ldh5.h, 16)
                for a in range(8):
                    sync.dma_start(out=xm_n5[:, a * NT:(a + 1) * NT],
                                   in_=xm_c[c, a]).then_inc(ldm5.h, 16)
            ldh5.n += 16 * 32
            ldm5.n += 128

            for m in range(8):
                km = c * 8 + m

                @b.block.tensor
                def _(t, c=c, m=m, km=km, ldhn=ldh5.n, yevn=yev5.n):
                    t.wait_ge(ldh5.h, ldhn)
                    if km == 0:
                        t.wait_ge(cvw5.h, 8)
                    if km >= 2:
                        t.wait_ge(yev5.h, km - 1)
                    for i in range(32):
                        ii = t.matmul(ps_y5[m % 2][:],
                                      w2_bf[:, (i * 8 + m) * 128:(i * 8 + m + 1) * 128],
                                      hload[:, i * NT:(i + 1) * NT],
                                      start=(i == 0), stop=(i == 31))
                    ii.then_inc(ymm5.h, 1)
                ymm5.n += 1

                @b.block.vector
                def _(v, c=c, m=m, km=km, ldmn=ldm5.n):
                    v.wait_ge(ymm5.h, km + 1)
                    v.wait_ge(ldm5.h, ldmn)
                    i = v.scalar_tensor_tensor(
                        out=y5[:, m * NT:(m + 1) * NT], in0=ps_y5[m % 2][:],
                        scalar=b2vec[:, m:m + 1], in1=xm_n5[:, m * NT:(m + 1) * NT],
                        op0=ALU.add, op1=ALU.add)
                    i.then_inc(yev5.h, 1)
                yev5.n += 1

            nrm5 = lno5.emit_chunk(yev5, yev5.n, y5, s1_sb, sst, sst.n)

            @b.block.sync
            def _(sync, c=c, nn=nrm5.n):
                sync.wait_ge(nrm5.h, nn)
                for m in range(8):
                    sync.dma_start(out=s1_c[c, m], in_=s1_sb[:, m * NT:(m + 1) * NT]) \
                        .then_inc(sst.h, 16)
            sst.n += 128

        @b.block.sync
        def _(sync, sn=sst.n):
            sync.wait_ge(sst.h, sn)



def emit_l1ffn(b: B):
    nc = b.nc
    x1T, xn1T = b.dram["x1T"], b.dram["xn1T"]
    mu1, rstd1 = b.dram["mu1"], b.dram["rstd1"]
    s1T, outT = b.dram["s1T"], b.dram["outT"]
    eselT, keysH, values = b.dram["eselT"], b.dram["keys"], b.dram["values"]
    hgD = b.dram["l1hg"]

    x1_c = x1T.rearrange("(a p) (c n) -> c a p n", p=128, n=NT)
    xn_c = xn1T.rearrange("(a p) (c n) -> c a p n", p=128, n=NT)
    s1_c = s1T.rearrange("(a p) (c n) -> c a p n", p=128, n=NT)
    out_c = outT.rearrange("(a p) (c n) -> c a p n", p=128, n=NT)
    hg_c = hgD.rearrange("(k p) (c n) -> k c p n", p=128, n=NT)
    NE = 64

    gTbf = b.sb("nGt", [64, T], BF16)
    sel64_holder = ExitStack()
    sel64 = sel64_holder.enter_context(nc.sbuf_tensor("nSel", [64, NE * 128], BF16))
    with ExitStack() as ctxR:
        identf = ctxR.enter_context(nc.sbuf_tensor("nIdf", [128, 128], F32))
        def sbr(n, s, d=F32):
            return ctxR.enter_context(nc.sbuf_tensor("nR" + n, s, d))
        xf = sbr("xf", [128, 8 * NT])
        xnf = sbr("xnf", [128, 8 * NT])
        esel_sb = sbr("esel", [128, 8 * 64])
        murow = sbr("mur", [1, NT])
        rsrow = sbr("rsr", [1, NT])
        onesc = sbr("onc", [1, 128])
        log_sb = sbr("log", [128, 64])
        sel_sb = sbr("sel", [128, 64])
        m8s = sbr("m8", [128, 8])
        mval8 = sbr("mv8", [128, 8])
        knock = sbr("kn", [128, 64])
        gate_tm = sbr("gtm", [128, 64])
        ps_lg = ctxR.enter_context(nc.psum_tensor("nRplg", [128, 64], F32))
        ps_b = ctxR.enter_context(nc.psum_tensor("nRpb", [128, NT], F32))
        ps_m = ctxR.enter_context(nc.psum_tensor("nRpm", [128, NT], F32))
        ps_tr = ctxR.enter_context(nc.psum_tensor("nRptr", [64, 128], F32))
        ld = b.sem("nRld")
        pre = b.sem("nRpre")
        bcn = b.sem("nRbc")
        xnd = b.sem("nRxnd")
        lgm = b.sem("nRlgm")
        ohd = b.sem("nRohd")
        tr2 = b.sem("nRtr2")
        ghe = b.sem("nRghe")
        sgd = b.sem("nRsgd")

        @b.block.gpsimd
        def _(g):
            g.memset(sel64[:], 1.0)
            g.affine_select(out=sel64[:], in_=sel64[:], pattern=[[1, NE], [0, 128]],
                            compare_op=ALU.is_equal, fill=0.0, base=0,
                            channel_multiplier=-1)
            g.memset(identf[:], 1.0)
            emit_identity(b, g, identf, identf)
            g.engine_nop().then_inc(pre.h, 1)
        pre.n += 1

        @b.block.sync
        def _(sync):
            for a in range(8):
                sync.dma_start(out=esel_sb[:, a * 64:(a + 1) * 64],
                               in_=eselT[a * 128:(a + 1) * 128, :]).then_inc(ld.h, 16)
        ld.n += 128

        for c in range(NCH):
            @b.block.sync
            def _(sync, c=c, xnn=xnd.n):
                if c > 0:
                    sync.wait_ge(xnd.h, xnn)
                for a in range(8):
                    sync.dma_start(out=xf[:, a * NT:(a + 1) * NT], in_=x1_c[c, a]) \
                        .then_inc(ld.h, 16)
                sync.dma_start(out=murow[:], in_=mu1[:, c * NT:(c + 1) * NT]) \
                    .then_inc(ld.h, 16)
                sync.dma_start(out=rsrow[:], in_=rstd1[:, c * NT:(c + 1) * NT]) \
                    .then_inc(ld.h, 16)
            ld.n += 160

            @b.block.vector
            def _(v, c=c):
                if c == 0:
                    v.memset(onesc[:], 1.0)

            @b.block.tensor
            def _(t, c=c, ldn=ld.n, xnn=xnd.n):
                t.wait_ge(ld.h, ldn)
                t.wait_ge(pre.h, 1)
                if c > 0:
                    t.wait_ge(xnd.h, xnn)
                t.matmul(ps_b[:], onesc[:], murow[:], start=True, stop=True)
                t.matmul(ps_m[:], onesc[:], rsrow[:], start=True, stop=True) \
                    .then_inc(bcn.h, 1)
            bcn.n += 1

            @b.block.vector
            def _(v, c=c, bb=bcn.n, lgn=lgm.n):
                v.wait_ge(bcn.h, bb)
                if c > 0:
                    v.wait_ge(lgm.h, lgn)
                for a in range(8):
                    sl = slice(a * NT, (a + 1) * NT)
                    v.tensor_tensor(out=xnf[:, sl], in0=xf[:, sl], in1=ps_b[:],
                                    op=ALU.subtract)
                for a in range(8):
                    sl = slice(a * NT, (a + 1) * NT)
                    i = v.tensor_tensor(out=xnf[:, sl], in0=xnf[:, sl], in1=ps_m[:],
                                        op=ALU.mult)
                i.then_inc(xnd.h, 1)
            xnd.n += 1

            for q in range(4 * c, 4 * c + 4):
                qo = (q % 4) * 128

                @b.block.tensor
                def _(t, q=q, qo=qo, ohn=ohd.n, xnn=xnd.n, sgn=sgd.n):
                    t.wait_ge(xnd.h, xnn)
                    if q > 0:
                        t.wait_ge(ohd.h, ohn)
                        t.wait_ge(sgd.h, sgn)
                    for a in range(8):
                        i = t.matmul(ps_lg[:],
                                     xnf[:, a * NT + qo: a * NT + qo + 128],
                                     esel_sb[:, a * 64:(a + 1) * 64],
                                     start=(a == 0), stop=(a == 7))
                    i.then_inc(lgm.h, 1)
                lgm.n += 1

                @b.block.scalar
                def _(s, q=q, ln=lgm.n):
                    s.wait_ge(lgm.h, ln)
                    s.activation(out=sel_sb[:], in_=ps_lg[:], func=AF.Sigmoid) \
                        .then_inc(sgd.h, 1)
                sgd.n += 1

                @b.block.vector
                def _(v, q=q, ln=lgm.n, ohb=ohd.n):
                    v.wait_ge(lgm.h, ln)
                    v.tensor_copy(log_sb[:], ps_lg[:])
                    v.memset(mval8[:], NEG).then_inc(ohd.h, 1)
                    v.wait_ge(ohd.h, ohb + 1)
                    v.max(out=m8s[:], in_=log_sb[:]).then_inc(ohd.h, 1)
                    v.wait_ge(ohd.h, ohb + 2)
                    v.tensor_copy(mval8[:, 0:4], m8s[:, 0:4]).then_inc(ohd.h, 1)
                    v.wait_ge(ohd.h, ohb + 3)
                    v.match_replace(out=knock[:], in_to_replace=mval8[:],
                                    in_values=log_sb[:], imm_value=NEG) \
                        .then_inc(ohd.h, 1)
                    v.wait_ge(ohd.h, ohb + 4)
                    v.tensor_tensor(out=gate_tm[:], in0=log_sb[:], in1=knock[:],
                                    op=ALU.not_equal).then_inc(ohd.h, 1)
                    v.wait_ge(ohd.h, ohb + 5)
                    v.wait_ge(sgd.h, q + 1)
                    i = v.tensor_tensor(out=gate_tm[:], in0=gate_tm[:], in1=sel_sb[:],
                                        op=ALU.mult)
                    i.then_inc(ohd.h, 1)
                ohd.n += 6

                @b.block.tensor
                def _(t, q=q, ohn=ohd.n, ghn=ghe.n):
                    t.wait_ge(ohd.h, ohn)
                    if q > 0:
                        t.wait_ge(ghe.h, ghn)
                    t.transpose(ps_tr[:], gate_tm[:], identf[:]).then_inc(tr2.h, 1)
                tr2.n += 1

                @b.block.vector
                def _(v, q=q, tn=tr2.n):
                    v.wait_ge(tr2.h, tn)
                    v.tensor_copy(gTbf[:, q * 128:(q + 1) * 128], ps_tr[0:64, :]) \
                        .then_inc(ghe.h, 1)
                ghe.n += 1

    b.barrier()

    with ExitStack() as ctx1:
        xn_sb = ctx1.enter_context(nc.sbuf_tensor("nXn", [128, 8 * T], BF16))
        kstg = [ctx1.enter_context(nc.sbuf_tensor(f"nKs{j}", [128, 1024], F32))
                for j in range(2)]
        kbf = [ctx1.enter_context(nc.sbuf_tensor(f"nKb{j}", [128, 1024], BF16))
               for j in range(2)]
        g_sb = ctx1.enter_context(nc.sbuf_tensor("nGsb", [128, NT], BF16))
        hg_sb = ctx1.enter_context(nc.sbuf_tensor("nHg", [128, NT], BF16))
        ps_h = [ctx1.enter_context(nc.psum_tensor(f"nPh{j}", [128, NT], F32))
                for j in range(2)]
        ps_g = ctx1.enter_context(nc.psum_tensor("nPg", [128, NT], F32))
        ldx = b.sem("nldx")
        ldk = b.sem("nldk")
        cvk = b.sem("ncvk")
        hmm = b.sem("nhmm")
        gbc = b.sem("ngbc")
        gcv = b.sem("ngcv")
        hev = b.sem("nhev")
        hst = b.sem("nhst")

        @b.block.sync
        def _(sync):
            for a in range(8):
                for c in range(NCH):
                    sync.dma_start(out=xn_sb[:, a * T + c * NT: a * T + (c + 1) * NT],
                                   in_=xn_c[c, a]).then_inc(ldx.h, 16)
            sync.dma_start(out=kstg[0][:], in_=keysH[0]).then_inc(ldk.h, 16)
        ldx.n += 16 * 64
        ldk.n += 16

        for k in range(NE):
            @b.block.vector
            def _(v, k=k, lkn=ldk.n, hmn=hmm.n):
                v.wait_ge(ldk.h, lkn)
                if k >= 2:
                    v.wait_ge(hmm.h, hmn - 8)
                v.tensor_copy(kbf[k % 2][:], kstg[k % 2][:]).then_inc(cvk.h, 1)
            cvk.n += 1

            @b.block.sync
            def _(sync, k=k, cvn=cvk.n):
                if k + 1 < NE:
                    if k >= 1:
                        sync.wait_ge(cvk.h, cvn - 1)
                    sync.dma_start(out=kstg[(k + 1) % 2][:], in_=keysH[k + 1]) \
                        .then_inc(ldk.h, 16)
            if k + 1 < NE:
                ldk.n += 16

            for c in range(NCH):
                kc = k * NCH + c

                @b.block.tensor
                def _(t, k=k, c=c, kc=kc, cvn=cvk.n, hevn=hev.n, ghn0=ghe.n):
                    if kc == 0:
                        t.wait_ge(ldx.h, ldx.n)
                        t.wait_ge(ghe.h, ghn0)
                    t.wait_ge(cvk.h, cvn)
                    if kc >= 2:
                        t.wait_ge(hev.h, kc - 1)
                    for a in range(8):
                        i = t.matmul(ps_h[kc % 2][:],
                                     kbf[k % 2][:, a * 128:(a + 1) * 128],
                                     xn_sb[:, a * T + c * NT: a * T + (c + 1) * NT],
                                     start=(a == 0), stop=(a == 7))
                    i.then_inc(hmm.h, 1)
                    if kc >= 1:
                        t.wait_ge(gcv.h, kc)
                    t.matmul(ps_g[:], sel64[:, k * 128:(k + 1) * 128],
                             gTbf[:, c * NT:(c + 1) * NT], start=True, stop=True) \
                        .then_inc(gbc.h, 1)
                hmm.n += 1
                gbc.n += 1

                @b.block.vector
                def _(v, k=k, c=c, kc=kc, gn=gbc.n, hn=hmm.n, hsn=hst.n, gb=gcv.n):
                    v.wait_ge(gbc.h, gn)
                    v.tensor_copy(g_sb[:], ps_g[:]).then_inc(gcv.h, 1)
                    v.wait_ge(hmm.h, hn)
                    if kc >= 1:
                        v.wait_ge(hst.h, hsn)
                    v.wait_ge(gcv.h, gb + 1)
                    i = v.scalar_tensor_tensor(out=hg_sb[:], in0=ps_h[kc % 2][:],
                                               scalar=0.0, in1=g_sb[:],
                                               op0=ALU.max, op1=ALU.mult)
                    i.then_inc(hev.h, 1)
                gcv.n += 1
                hev.n += 1

                @b.block.sync
                def _(sync, k=k, c=c, hn=hev.n):
                    sync.wait_ge(hev.h, hn)
                    sync.dma_start(out=hg_c[k, c], in_=hg_sb[:]).then_inc(hst.h, 16)
                hst.n += 16

        @b.block.sync
        def _(sync, hn=hst.n):
            sync.wait_ge(hst.h, hn)

    sel64_holder.close()
    b.barrier()

    with ExitStack() as ctx2:
        vals_bf = ctx2.enter_context(nc.sbuf_tensor("nVal", [128, NE * 1024], BF16))
        hgl = ctx2.enter_context(nc.sbuf_tensor("nHgl", [128, 32 * NT], BF16))
        x1l = ctx2.enter_context(nc.sbuf_tensor("nX1l", [128, 2 * NT], F32))
        s1l = ctx2.enter_context(nc.sbuf_tensor("nS1l", [128, 2 * NT], F32))
        oacc = ctx2.enter_context(nc.sbuf_tensor("nOac", [128, 8 * NT], F32))
        ps_o = [ctx2.enter_context(nc.psum_tensor(f"nPo{j}", [128, NT], F32))
                for j in range(8)]
        ldv = b.sem("nldv")
        cvv = b.sem("ncvv")
        ldh = b.sem("nldh")
        ldxs = b.sem("nldxs")
        omm = b.sem("nomm")
        oev = b.sem("noev")
        sil = b.sem("nsil")
        cmb = b.sem("ncmb")
        ost = b.sem("nost")

        @b.block.gpsimd
        def _(g):
            for k in range(NE):
                g.dma_start(out=vals_bf[:, k * 1024:(k + 1) * 1024],
                            in_=values[k]).then_inc(cvv.h, 16)
        cvv.n += 16 * NE

        for c in range(NCH):
            for half in range(2):
                @b.block.sync
                def _(sync, c=c, half=half, omn=omm.n):
                    if c + half > 0:
                        sync.wait_ge(omm.h, omn)
                    for kk in range(32):
                        k = half * 32 + kk
                        sync.dma_start(out=hgl[:, kk * NT:(kk + 1) * NT],
                                       in_=hg_c[k, c]).then_inc(ldh.h, 16)
                ldh.n += 16 * 32

                for m in range(8):
                    @b.block.tensor
                    def _(t, c=c, half=half, m=m, ldhn=ldh.n, oevn=oev.n):
                        t.wait_ge(ldh.h, ldhn)
                        if half == 0 and m == 0 and c == 0:
                            t.wait_ge(cvv.h, cvv.n)
                        if half == 0 and c > 0 and m == 0:
                            t.wait_ge(oev.h, oevn)
                        for kk in range(32):
                            k = half * 32 + kk
                            i = t.matmul(ps_o[m][:],
                                         vals_bf[:, k * 1024 + m * 128: k * 1024 + (m + 1) * 128],
                                         hgl[:, kk * NT:(kk + 1) * NT],
                                         start=(half == 0 and kk == 0),
                                         stop=(half == 1 and kk == 31))
                        if half == 1:
                            i.then_inc(omm.h, 1)
                    if half == 1:
                        omm.n += 1
                if half == 0:
                    @b.block.tensor
                    def _(t):
                        t.nop().then_inc(omm.h, 1)
                    omm.n += 1


            @b.block.vector
            def _(v, c=c, omn=omm.n, osn=ost.n):
                v.wait_ge(omm.h, omn)
                if c > 0:
                    v.wait_ge(ost.h, osn)
                for m in range(8):
                    i = v.tensor_copy(oacc[:, m * NT:(m + 1) * NT], ps_o[m][:])
                i.then_inc(oev.h, 1)
            oev.n += 1

            @b.block.scalar
            def _(s, c=c, on=oev.n):
                s.wait_ge(oev.h, on)
                s.activation(out=oacc[:], in_=oacc[:], func=AF.Silu).then_inc(sil.h, 1)
            sil.n += 1

            for a in range(8):
                @b.block.sync
                def _(sync, c=c, a=a, cmn=cmb.n):
                    if c > 0 or a >= 2:
                        sync.wait_ge(cmb.h, cmn - 1)
                    sync.dma_start(out=x1l[:, (a % 2) * NT:(a % 2 + 1) * NT],
                                   in_=x1_c[c, a]).then_inc(ldxs.h, 16)
                    sync.dma_start(out=s1l[:, (a % 2) * NT:(a % 2 + 1) * NT],
                                   in_=s1_c[c, a]).then_inc(ldxs.h, 16)
                ldxs.n += 32

                @b.block.vector
                def _(v, c=c, a=a, sn=sil.n, ldn=ldxs.n):
                    v.wait_ge(sil.h, sn)
                    v.wait_ge(ldxs.h, ldn)
                    asl = slice(a * NT, (a + 1) * NT)
                    bsl = slice((a % 2) * NT, (a % 2 + 1) * NT)
                    v.tensor_tensor(out=oacc[:, asl], in0=oacc[:, asl],
                                    in1=s1l[:, bsl], op=ALU.mult)
                    i = v.tensor_tensor(out=oacc[:, asl], in0=oacc[:, asl],
                                        in1=x1l[:, bsl], op=ALU.add)
                    i.then_inc(cmb.h, 1)
                cmb.n += 1

                @b.block.sync
                def _(sync, c=c, a=a, cn=cmb.n):
                    sync.wait_ge(cmb.h, cn)
                    sync.dma_start(out=out_c[c, a], in_=oacc[:, a * NT:(a + 1) * NT]) \
                        .then_inc(ost.h, 16)
                ost.n += 16

        @b.block.sync
        def _(sync, on=ost.n):
            sync.wait_ge(ost.h, on)





def build_launch(phase_names, weight_shapes, keep=None):
    nc = bass.Bass("TRN2", target_bir_lowering=False)
    reads = set()
    writes = set()
    produced = {}
    consumed_ext = set()
    for pn in phase_names:
        r, w = PHASES[pn]
        for t in r:
            if t not in produced:
                consumed_ext.add(t)
            reads.add(t)
        for t in w:
            produced[t] = pn
            writes.add(t)
    out_ext = set(w for w in writes if keep is None or w in keep)
    with ExitStack() as ctx:
        block = None
        b = B(nc, ctx, None)
        for t in sorted(consumed_ext):
            shape, dt = BOUND[t]
            b.dram[t] = nc.dram_tensor(t, shape, dt, kind="ExternalInput").ap()
        for t in sorted(writes):
            shape, dt = BOUND[t]
            kind = "ExternalOutput" if t in out_ext else "Internal"
            b.dram[t] = nc.dram_tensor(t, shape, dt, kind=kind).ap()
        for pn in phase_names:
            for k, (shape, dt) in PHASE_SCRATCH.get(pn, {}).items():
                kind = "ExternalOutput" if k in DEBUG_TENSORS else "Internal"
                b.dram[k] = nc.dram_tensor(k, shape, dt, kind=kind).ap()
                if k in DEBUG_TENSORS:
                    out_ext.add(k)
        wkeys = []
        for pn in phase_names:
            wkeys += PHASE_WEIGHTS[pn]
        for k in wkeys:
            shape, dt = weight_shapes[k]
            b.dram[k] = nc.dram_tensor(k, list(shape), dt, kind="ExternalInput").ap()
        b.block = ctx.enter_context(nc.Block())
        for i, pn in enumerate(phase_names):
            if i > 0:
                b.barrier(clear=True)
            b.new_phase(PHASE_SLOT_BASE.get(pn, 0))
            emit_phase(b, pn)
    return nc, sorted(consumed_ext), wkeys, sorted(out_ext)




def host_prep(inputs):
    f = np.ascontiguousarray
    w = {}
    w["wrT"] = (f(inputs["L0_ffn_Wr"].T), F32)
    w["w1"] = (f(inputs["L0_ffn_W1"].reshape(8, 8, 128, 32, 128)
                 .transpose(0, 3, 2, 1, 4).reshape(8, 32, 128, 1024)), F32)
    w["b1r"] = (f(inputs["L0_ffn_b1"].reshape(8, 32, 128).transpose(0, 2, 1)), F32)
    w["wo"] = (f(inputs["L0_ffn_Wo"]), F32)
    w["bor"] = (f(inputs["L0_ffn_bo"].reshape(8, 128).T), F32)
    w["gamma0r"] = (f(inputs["L0_ffn_gamma"].reshape(8, 128).T), F32)
    w["beta0r"] = (f(inputs["L0_ffn_beta"].reshape(8, 128).T), F32)
    w["w_dkv"] = (f(inputs["L0_attn_W_dkv"]), F32)
    w["w_kr"] = (f(inputs["L0_attn_W_kr"]), F32)
    w["idx_wk"] = (f(inputs["L0_attn_idx_wk"]), F32)
    w["idx_wq"] = (f(inputs["L0_attn_idx_wq"]), F32)
    w["w_dq"] = (f(inputs["L0_attn_W_dq"]), F32)
    w["w_uq"] = (f(inputs["L0_attn_W_uq"]), F32)
    w["w_qr"] = (f(inputs["L0_attn_W_qr"].reshape(512, 8, 2, 16)
                   .transpose(0, 2, 1, 3).reshape(512, 256)), F32)
    w["w_ukTh"] = (f(inputs["L0_attn_W_uk"].reshape(D_CKV, N_HEAD, D_HEAD)
                     .transpose(1, 2, 0)), F32)
    w["w_uvTh"] = (f(inputs["L0_attn_W_uv"].reshape(D_CKV, N_HEAD, D_HEAD)
                     .transpose(1, 2, 0)), F32)
    w["w_outm"] = (f(inputs["L0_attn_W_out"]), F32)
    w["idx_w"] = (f(inputs["L0_attn_idx_w"]).reshape(1, 4), F32)
    s_pos = np.arange(T)
    w["c63"] = (np.cos((s_pos[None, :] + 63) * THETA[:, None]).astype(np.float32), F32)
    w["s63"] = (np.sin((s_pos[None, :] + 63) * THETA[:, None]).astype(np.float32), F32)
    w["c0"] = (np.cos(s_pos[None, :64] * THETA[:, None]).astype(np.float32), F32)
    w["s0"] = (np.sin(s_pos[None, :64] * THETA[:, None]).astype(np.float32), F32)
    w["ct_tm"] = (np.cos(s_pos[:, None] * THETA[None, :]).astype(np.float32), F32)
    w["st_tm"] = (np.sin(s_pos[:, None] * THETA[None, :]).astype(np.float32), F32)
    w["c2t_tm"] = (np.cos(2 * s_pos[:, None] * THETA[None, :]).astype(np.float32), F32)
    w["s2t_tm"] = (np.sin(2 * s_pos[:, None] * THETA[None, :]).astype(np.float32), F32)
    import ml_dtypes
    pj = np.arange(128) % 32 + LW
    w["crot"] = (np.cos(pj[:, None] * THETA[None, :]).astype(np.float32), F32)
    w["srot"] = (np.sin(pj[:, None] * THETA[None, :]).astype(np.float32), F32)
    w["identbf"] = (np.eye(128, dtype=ml_dtypes.bfloat16), BF16)
    mdyn = np.full((128, 512), NEG, np.float32)
    for h in range(8):
        for lg in range(16):
            a_, b_ = lg // 4, lg % 4
            mdyn[h * 16 + lg, a_ * 128 + b_ * 32: a_ * 128 + b_ * 32 + 32] = 0.0
    w["maskdyn"] = (mdyn, F32)
    mband = np.full((128, 80), NEG, np.float32)
    for h in range(8):
        for lg in range(16):
            mband[h * 16 + lg, lg:lg + 64] = 0.0
    w["maskband"] = (mband, F32)
    w["nfin0"] = (np.clip(np.arange(128) - 63, 0, 32).astype(np.float32)
                  .reshape(128, 1), F32)
    w["iotaj"] = (np.broadcast_to(np.arange(32, dtype=np.float32), (128, 32)).copy(), F32)
    w["fill64"] = (np.broadcast_to(np.arange(64, 96).astype(np.uint32),
                                   (128, 32)).copy(), U32)
    w["idxwb"] = (np.broadcast_to(np.asarray(inputs["L0_attn_idx_w"],
                                             np.float32).reshape(1, 4),
                                  (128, 4)).copy(), F32)
    w["wqkvT"] = (f(inputs["L1_attn_w_qkv"].T), F32)
    w["proj"] = (f(inputs["L1_attn_proj"]), F32)
    w["b_outr"] = (f(inputs["L1_attn_b_out"].reshape(8, 128).T), F32)
    w["gamma1r"] = (f(inputs["L1_attn_gamma"].reshape(8, 128).T), F32)
    w["beta1r"] = (f(inputs["L1_attn_beta"].reshape(8, 128).T), F32)
    w["wout1T"] = (f(inputs["L1_attn_w_out"].T), F32)
    w["w1aT"] = (f(inputs["L1_attn_w1"].T), F32)
    w["b1ar"] = (f(inputs["L1_attn_b1"].reshape(32, 128).T), F32)
    w["w2aT"] = (f(inputs["L1_attn_w2"].T), F32)
    w["b2ar"] = (f(inputs["L1_attn_b2"].reshape(8, 128).T), F32)
    w["eselT"] = (f(inputs["L1_ffn_expert_sel"].T), F32)
    w["keys"] = (f(inputs["L1_ffn_keys"].reshape(64, 8, 128, 128)
                   .transpose(0, 2, 1, 3).reshape(64, 128, 1024)), F32)
    w["values"] = (f(inputs["L1_ffn_values"]), F32)
    return w


DEBUG_TENSORS = set()

LAUNCH_PLAN = [["ln0", "l0ffn", "prep", "l0att", "ln1", "l1att", "l1ffn"]]


def run_plan(inputs, plan):
    w = host_prep(inputs)
    weight_shapes = {k: (v[0].shape, v[1]) for k, v in w.items()}
    x = np.ascontiguousarray(inputs["L0_attn_x"])
    percore = {c: {"xT": np.ascontiguousarray(x[c].T)} for c in range(8)}
    for li, launch in enumerate(plan):
        later_reads = set()
        for l2 in plan[li + 1:]:
            for pn in l2:
                later_reads |= set(PHASES[pn][0])
        keep = later_reads | {"outT"}
        nc, ext_in, wkeys, ext_out = build_launch(launch, weight_shapes, keep=keep)
        in_maps = []
        for c in range(8):
            m = {}
            for t in ext_in:
                m[t] = percore[c][t]
            for k in wkeys:
                m[k] = w[k][0]
            in_maps.append(m)
        res = run_bass_kernel_spmd(nc, in_maps, core_ids=list(range(8)))
        for c in range(8):
            for t in ext_out:
                percore[c][t] = res.results[c][t]
    return percore


def kernel(**inputs):
    percore = run_plan(inputs, LAUNCH_PLAN)
    out = np.stack([percore[c]["outT"].T for c in range(8)])
    return out.astype(np.float32)
